# version 01
# baseline (speedup 1.0000x reference)
import math
import contextlib
import numpy as np
import concourse.bass as bass
import concourse.mybir as mybir
from concourse.bass_utils import run_bass_kernel_spmd

F32 = mybir.dt.float32
I32 = mybir.dt.int32
U32 = mybir.dt.uint32
BF16 = mybir.dt.bfloat16
AF = mybir.ActivationFunctionType
ALU = mybir.AluOpType
AX = mybir.AxisListType

D = 1024
SEQ = 4096
B = 4
HALF = 2048
WIN = 128
NH = 16
HD = 64
DFF = 4096
RANK = 256
NE = 8
NI = 2048
KSEL = 1024
NB = 63
LNEPS = float(1e-5)
NEG = float(-1e30)


class PB:
    ENGINES = ("sync", "scalar", "vector", "tensor", "gpsimd")
    SELF_SERIAL = ("scalar", "vector", "gpsimd")

    def __init__(self, nc, est):
        self.nc = nc
        self.ops = {e: [] for e in self.ENGINES}
        self.H = {}
        self.V = {}
        self.est = est
        self.scopes = [est]

    @contextlib.contextmanager
    def scope(self):
        st = contextlib.ExitStack()
        self.scopes.append(st)
        try:
            yield st
        finally:
            self.scopes.pop()
            st.close()

    def sem(self, name):
        if name not in self.H:
            self.H[name] = self.est.enter_context(self.nc.semaphore(name))
            self.V[name] = 0
        return name

    def bump(self, name, amt):
        self.sem(name)
        self.V[name] += amt
        return self.V[name]

    def cur(self, name):
        self.sem(name)
        return self.V[name]

    def emit(self, eng, fn):
        self.ops[eng].append(fn)

    def wait(self, eng, name, val=None):
        v = self.cur(name) if val is None else val
        h = self.H[name]
        if v > 0:
            self.emit(eng, lambda E, h=h, v=v: E.wait_ge(h, v))

    def op(self, eng, fn, chain=True):
        q = "q_" + eng
        if eng in self.SELF_SERIAL and chain and self.cur(q) > 0:
            self.wait(eng, q, self.cur(q))
        v = self.bump(q, 1)
        h = self.H[q]
        self.emit(eng, lambda E, fn=fn, h=h: fn(E).then_inc(h, 1))
        return v

    def tick(self, eng):
        return self.cur("q_" + eng)

    def phase(self, engines=None, dma_sems=True):
        engines = engines or self.ENGINES
        ticks = {e: self.cur("q_" + e) for e in self.ENGINES}
        dmas = [(n, self.V[n]) for n in list(self.H)
                if not n.startswith("q_") and self.V[n] > 0] if dma_sems else []
        for e in engines:
            for o in self.ENGINES:
                if o != e and ticks[o] > 0:
                    self.wait(e, "q_" + o, ticks[o])
            for n, v in dmas:
                self.wait(e, n, v)

    def dma(self, eng, out, in_, sem):
        v = self.bump(sem, 16)
        h = self.H[sem]
        self.emit(eng, lambda E, o=out, i=in_, h=h: E.dma_start(out=o, in_=i).then_inc(h, 16))
        return v

    def build(self):
        with self.nc.Block() as blk:
            for eng in self.ENGINES:
                ops = self.ops[eng]
                if not ops:
                    continue
                dec = getattr(blk, eng)

                def runner(E, ops=ops):
                    for f in ops:
                        f(E)
                dec(runner)


def sbuf(pb, name, shape, dt=F32):
    return pb.scopes[-1].enter_context(pb.nc.sbuf_tensor(name, list(shape), dt))


def psum8(pb):
    return [pb.est.enter_context(pb.nc.psum_tensor("psum%d" % i, [128, 512], F32))
            for i in range(8)]


def make_identity(pb):
    di = sbuf(pb, "idn_i", (128, 128), I32)
    idn = sbuf(pb, "idn_f", (128, 128), F32)
    gv = pb.op("gpsimd", lambda E, di=di: E.iota(di[:], [[-1, 128]], base=0, channel_multiplier=1))
    pb.wait("vector", "q_gpsimd", gv)
    vv = pb.op("vector", lambda E, di=di, idn=idn: E.tensor_scalar(idn[:], di[:], 0, None, ALU.is_equal))
    pb.idn_ready = vv
    return idn


def load_bias_cols(pb, name, dram_vec, nch, idn, bank):
    tmp = sbuf(pb, name + "_r", (nch, 128))
    out = sbuf(pb, name, (128, nch))
    lv = pb.dma("sync", tmp[:], dram_vec.rearrange("(c p) -> c p", p=128), name + "_ld")
    pb.wait("tensor", name + "_ld", lv)
    pb.wait("tensor", "q_vector", pb.tick("vector"))
    tv = pb.op("tensor", lambda E: E.transpose(bank[0:128, 0:nch], tmp[:], idn[0:nch, 0:nch]))
    pb.wait("vector", "q_tensor", tv)
    vv = pb.op("vector", lambda E: E.tensor_copy(out[:], bank[:, 0:nch]))
    return out, vv


def emit_ln_chunk(pb, x_chunk, h_chunk, stats, sq_scratch, dim=D):
    mu = stats[:, 0:1]
    sq = stats[:, 1:2]
    rstd = stats[:, 2:3]
    nbias = stats[:, 3:4]
    var = stats[:, 4:5]
    pb.wait("scalar", "q_vector", pb.tick("vector"))
    pb.wait("vector", "q_scalar", pb.tick("scalar"))
    pb.op("vector", lambda E: E.reduce_sum(mu, x_chunk, axis=AX.X))
    pb.op("vector", lambda E: E.tensor_scalar(mu, mu, 1.0 / dim, None, ALU.mult))
    sv = pb.op("scalar", lambda E: E.activation(sq_scratch[:], x_chunk, AF.Square, accum_out=sq))
    pb.wait("vector", "q_scalar", sv)
    pb.op("vector", lambda E: E.tensor_scalar(var, sq, 1.0 / dim, None, ALU.mult))
    pb.op("vector", lambda E: E.tensor_tensor(sq, mu, mu, ALU.mult))
    pb.op("vector", lambda E: E.tensor_tensor(var, var, sq, ALU.subtract))
    vv = pb.op("vector", lambda E: E.tensor_scalar(var, var, LNEPS, None, ALU.add))
    pb.wait("scalar", "q_vector", vv)
    sv = pb.op("scalar", lambda E: E.activation(rstd, var, AF.Sqrt))
    pb.wait("vector", "q_scalar", sv)
    pb.op("vector", lambda E: E.reciprocal(rstd, rstd))
    pb.op("vector", lambda E: E.tensor_tensor(nbias, mu, rstd, ALU.mult))
    pb.op("vector", lambda E: E.tensor_scalar(nbias, nbias, -1.0, None, ALU.mult))
    return pb.op("vector", lambda E: E.tensor_scalar(h_chunk, x_chunk, rstd, nbias, ALU.mult, ALU.add))


def build_l0ffn(T):
    nc = bass.Bass("TRN2", target_bir_lowering=False, debug=False)
    x_in = nc.dram_tensor("x", [T, D], F32, kind="ExternalInput").ap()
    U1 = nc.dram_tensor("U1", [D, RANK], F32, kind="ExternalInput").ap()
    V1 = nc.dram_tensor("V1", [RANK, DFF], F32, kind="ExternalInput").ap()
    b1 = nc.dram_tensor("b1", [DFF], F32, kind="ExternalInput").ap()
    U2 = nc.dram_tensor("U2", [DFF, RANK], F32, kind="ExternalInput").ap()
    V2 = nc.dram_tensor("V2", [RANK, D], F32, kind="ExternalInput").ap()
    b2 = nc.dram_tensor("b2", [D], F32, kind="ExternalInput").ap()
    y = nc.dram_tensor("y", [T, D], F32, kind="ExternalOutput").ap()

    est = contextlib.ExitStack()
    with est:
        pb = PB(nc, est)
        idn = make_identity(pb)
        u1 = sbuf(pb, "u1", (128, 8, RANK))
        v1 = sbuf(pb, "v1", (128, 2, DFF))
        u2 = sbuf(pb, "u2", (128, 32, RANK))
        v2 = sbuf(pb, "v2", (128, 2, D))
        p = psum8(pb)
        pb.dma("sync", u1[:], U1.rearrange("(c p) r -> p c r", p=128), "wld")
        pb.dma("sync", v1[:], V1.rearrange("(c p) f -> p c f", p=128), "wld")
        pb.dma("sync", u2[:], U2.rearrange("(c p) r -> p c r", p=128), "wld")
        pb.dma("sync", v2[:], V2.rearrange("(c p) o -> p c o", p=128), "wld")
        b1t, b1v = load_bias_cols(pb, "b1t", b1, 32, idn, p[7])
        b2t, b2v = load_bias_cols(pb, "b2t", b2, 8, idn, p[6])
        pb.wait("tensor", "q_vector", b2v)
        pb.wait("scalar", "q_vector", b2v)
        wld = pb.cur("wld")

        xc = sbuf(pb, "xc", (128, 4, D))
        hc = sbuf(pb, "hc", (128, 4, D))
        stats = sbuf(pb, "stats", (128, 8))
        sqs = sbuf(pb, "sqs", (128, D))
        hT = sbuf(pb, "hT", (128, 8, 512))
        Pfm = sbuf(pb, "Pfm", (128, 2, 512))
        Hc = sbuf(pb, "Hc", (128, 2, 512))
        Qfm = sbuf(pb, "Qfm", (128, 2, 512))
        dfm = sbuf(pb, "dfm", (128, 8, 512))
        oc = sbuf(pb, "oc", (128, 4, D))

        n_blk = (T + 511) // 512
        oev_last = 0
        hv = 0
        for blk_i in range(n_blk):
            t0 = blk_i * 512
            bt = min(512, T - t0)
            nch = bt // 128
            pb.wait("sync", "q_vector", oev_last)
            pb.wait("tensor", "q_vector", oev_last)
            for c in range(nch):
                r0 = t0 + c * 128
                lv = pb.dma("sync", xc[:, c, :], x_in[r0:r0 + 128, :], "xld_s%d" % c)
                pb.wait("vector", "xld_s%d" % c, lv)
                pb.wait("scalar", "xld_s%d" % c, lv)
                hv = emit_ln_chunk(pb, xc[:, c, :], hc[:, c, :], stats, sqs)
                pb.wait("tensor", "q_vector", hv)
                for dch in range(8):
                    bank = p[dch % 2]
                    tv = pb.op("tensor", lambda E, bank=bank, c=c, dch=dch:
                               E.transpose(bank[:, 0:128], hc[:, c, dch * 128:(dch + 1) * 128], idn[:]))
                    pb.wait("vector", "q_tensor", tv)
                    ev = pb.op("vector", lambda E, bank=bank, c=c, dch=dch:
                               E.tensor_copy(hT[:, dch, c * 128:(c + 1) * 128], bank[:, 0:128]))
                    pb.wait("tensor", "q_vector", ev)
            pb.wait("tensor", "wld", wld)
            mmP = 0
            for rch in range(2):
                bank = p[2 + rch]
                for k in range(8):
                    mmP = pb.op("tensor", lambda E, r=rch, k=k, bank=bank, bt=bt:
                                E.matmul(bank[:, 0:bt], u1[:, k, r * 128:(r + 1) * 128],
                                         hT[:, k, 0:bt], start=(k == 0), stop=(k == 7)))
            pb.wait("vector", "q_tensor", mmP)
            pev = 0
            for rch in range(2):
                pev = pb.op("vector", lambda E, r=rch, bank=p[2 + rch], bt=bt:
                            E.tensor_copy(Pfm[:, r, 0:bt], bank[:, 0:bt]))
            pb.wait("tensor", "q_vector", pev)
            gel = {}
            mmQ = {}
            for f in range(32):
                bank = p[f % 2]
                mmH = 0
                for rch in range(2):
                    mmH = pb.op("tensor", lambda E, f=f, r=rch, bank=bank, bt=bt:
                                E.matmul(bank[:, 0:bt], v1[:, r, f * 128:(f + 1) * 128],
                                         Pfm[:, r, 0:bt], start=(r == 0), stop=(r == 1)))
                pb.wait("scalar", "q_tensor", mmH)
                if f >= 2:
                    pb.wait("scalar", "q_tensor", mmQ[f - 2])
                hbuf = Hc[:, f % 2, 0:bt]
                gel[f] = pb.op("scalar", lambda E, f=f, bank=bank, hb=hbuf, bt=bt:
                               E.activation(hb, bank[:, 0:bt], AF.Gelu_apprx_tanh,
                                            bias=b1t[:, f:f + 1]))
                pb.wait("tensor", "q_scalar", gel[f])
                for rch in range(2):
                    bank2 = p[4 + rch]
                    mmQ[f] = pb.op("tensor", lambda E, f=f, r=rch, hb=hbuf, bank2=bank2, bt=bt:
                                   E.matmul(bank2[:, 0:bt], u2[:, f, r * 128:(r + 1) * 128], hb,
                                            start=(f == 0), stop=(f == 31)))
            pb.wait("vector", "q_tensor", mmQ[31])
            qev = 0
            for rch in range(2):
                qev = pb.op("vector", lambda E, r=rch, bank=p[4 + rch], bt=bt:
                            E.tensor_copy(Qfm[:, r, 0:bt], bank[:, 0:bt]))
            pb.wait("tensor", "q_vector", qev)
            dev = {}
            for och in range(8):
                bank = p[6 + och % 2]
                if och >= 2:
                    pb.wait("tensor", "q_vector", dev[och - 2])
                mmD = 0
                for rch in range(2):
                    mmD = pb.op("tensor", lambda E, o=och, r=rch, bank=bank, bt=bt:
                                E.matmul(bank[:, 0:bt], v2[:, r, o * 128:(o + 1) * 128],
                                         Qfm[:, r, 0:bt], start=(r == 0), stop=(r == 1)))
                pb.wait("vector", "q_tensor", mmD)
                dev[och] = pb.op("vector", lambda E, o=och, bank=bank, bt=bt:
                                 E.tensor_scalar(dfm[:, o, 0:bt], bank[:, 0:bt],
                                                 b2t[:, o:o + 1], None, ALU.add))
            pb.wait("gpsimd", "q_vector", hv)
            pb.wait("tensor", "q_vector", dev[7])
            for c in range(nch):
                r0 = t0 + c * 128
                lv2 = pb.dma("gpsimd", xc[:, c, :], x_in[r0:r0 + 128, :], "xld2_s%d" % c)
                pb.wait("tensor", "q_vector", oev_last)
                oev = {}
                for och in range(8):
                    bank = p[och % 2]
                    if och >= 2:
                        pb.wait("tensor", "q_vector", oev[och - 2])
                    tv = pb.op("tensor", lambda E, bank=bank, c=c, o=och:
                               E.transpose(bank[:, 0:128], dfm[:, o, c * 128:(c + 1) * 128], idn[:]))
                    pb.wait("vector", "q_tensor", tv)
                    if och == 0:
                        pb.wait("vector", "xld2_s%d" % c, lv2)
                    oev[och] = pb.op("vector", lambda E, bank=bank, c=c, o=och:
                                     E.scalar_tensor_tensor(oc[:, c, o * 128:(o + 1) * 128],
                                                            bank[:, 0:128], 0.5,
                                                            xc[:, c, o * 128:(o + 1) * 128],
                                                            ALU.mult, ALU.add))
                oev_last = oev[7]
                pb.wait("gpsimd", "q_vector", oev_last)
                pb.dma("gpsimd", y[r0:r0 + 128, :], oc[:, c, :], "yst")
        pb.wait("gpsimd", "yst", pb.cur("yst"))
        pb.build()
    return nc


def bcast_row(pb, name, row_ap, width, idn_ready_tick, bank, ones1):
    out = sbuf(pb, name, (128, width))
    vv = 0
    for c0 in range(0, width, 512):
        w = min(512, width - c0)
        tv = pb.op("tensor", lambda E, c0=c0, w=w:
                   E.matmul(bank[:, 0:w], ones1[:, 0:128], row_ap[:, c0:c0 + w],
                            start=True, stop=True))
        pb.wait("vector", "q_tensor", tv)
        vv = pb.op("vector", lambda E, c0=c0, w=w:
                   E.tensor_copy(out[:, c0:c0 + w], bank[:, 0:w]))
        pb.wait("tensor", "q_vector", vv)
    return out, vv


def build_l0attn(T, halo):
    TK = T + halo
    nc = bass.Bass("TRN2", target_bir_lowering=False, debug=False)
    x_in = nc.dram_tensor("x", [TK, D], F32, kind="ExternalInput").ap()
    Wq = nc.dram_tensor("Wq", [D, D], F32, kind="ExternalInput").ap()
    Wk = nc.dram_tensor("Wk", [D, D], F32, kind="ExternalInput").ap()
    Wv = nc.dram_tensor("Wv", [D, D], F32, kind="ExternalInput").ap()
    Wo = nc.dram_tensor("Wo", [D, D], F32, kind="ExternalInput").ap()
    W1 = nc.dram_tensor("W1", [D, DFF], F32, kind="ExternalInput").ap()
    W2 = nc.dram_tensor("W2", [DFF, D], F32, kind="ExternalInput").ap()
    bq = nc.dram_tensor("bq", [D], F32, kind="ExternalInput").ap()
    bk = nc.dram_tensor("bk", [D], F32, kind="ExternalInput").ap()
    bv = nc.dram_tensor("bv", [D], F32, kind="ExternalInput").ap()
    bo = nc.dram_tensor("bo", [D], F32, kind="ExternalInput").ap()
    b1 = nc.dram_tensor("b1", [DFF], F32, kind="ExternalInput").ap()
    b2 = nc.dram_tensor("b2", [D], F32, kind="ExternalInput").ap()
    lnp = nc.dram_tensor("lnp", [4, D], F32, kind="ExternalInput").ap()
    y = nc.dram_tensor("y", [T, D], F32, kind="ExternalOutput").ap()
    xi_d = nc.dram_tensor("xi_d", [TK, D], F32, kind="Internal").ap()
    xmid_d = nc.dram_tensor("xmid_d", [T, D], F32, kind="Internal").ap()
    ctx_d = nc.dram_tensor("ctx_d", [D, T], F32, kind="Internal").ap()
    hg_d = nc.dram_tensor("hg_d", [DFF, T], F32, kind="Internal").ap()

    est = contextlib.ExitStack()
    with est:
        pb = PB(nc, est)
        idn = make_identity(pb)
        p = psum8(pb)
        ones1 = sbuf(pb, "ones1", (1, 128))
        ov = pb.op("vector", lambda E: E.memset(ones1[:], 1.0))
        pb.wait("tensor", "q_vector", ov)
        lnrows = []
        for i in range(4):
            r = sbuf(pb, "lnrow%d" % i, (1, D))
            lv = pb.dma("sync", r[:], lnp[i:i + 1, :], "lnld")
            lnrows.append(r)
        pb.wait("tensor", "lnld", pb.cur("lnld"))
        pb.wait("tensor", "q_vector", pb.idn_ready)
        g1bc, _ = bcast_row(pb, "g1bc", lnrows[0][:, :], D, pb.idn_ready, p[0], ones1)
        b1bc, _ = bcast_row(pb, "b1bc", lnrows[1][:, :], D, pb.idn_ready, p[0], ones1)
        g2bc, _ = bcast_row(pb, "g2bc", lnrows[2][:, :], D, pb.idn_ready, p[0], ones1)
        b2bc, bcv = bcast_row(pb, "b2bc", lnrows[3][:, :], D, pb.idn_ready, p[0], ones1)
        b2row = sbuf(pb, "b2row", (1, D))
        lvb = pb.dma("sync", b2row[:], b2.rearrange("(a d) -> a d", a=1), "b2r_ld")
        pb.wait("tensor", "b2r_ld", lvb)
        b2bc2, bdv2 = bcast_row(pb, "b2bc2", b2row[:, :], D, 0, p[1], ones1)
        bqt, _ = load_bias_cols(pb, "bqt", bq, 8, idn, p[2])
        bkt, _ = load_bias_cols(pb, "bkt", bk, 8, idn, p[2])
        bvt, _ = load_bias_cols(pb, "bvt", bv, 8, idn, p[2])
        bot, _ = load_bias_cols(pb, "bot", bo, 8, idn, p[2])
        bf1t, bv2 = load_bias_cols(pb, "bf1t", b1, 32, idn, p[2])
        pb.wait("vector", "q_vector", bv2)
        mski = sbuf(pb, "mski", (128, 256), I32)
        msk = sbuf(pb, "msk", (128, 256))
        mskt = sbuf(pb, "mskt", (128, 256))
        pb.op("gpsimd", lambda E: E.iota(mski[:, 0:128], [[-1, 128]], base=128,
                                         channel_multiplier=1))
        gv = pb.op("gpsimd", lambda E: E.iota(mski[:, 128:256], [[-1, 128]], base=0,
                                              channel_multiplier=1))
        pb.wait("vector", "q_gpsimd", gv)
        pb.op("vector", lambda E: E.tensor_scalar(msk[:], mski[:], 0, None, ALU.is_lt))
        pb.op("vector", lambda E: E.tensor_scalar(mskt[:], mski[:], 128, None, ALU.is_gt))
        pb.op("vector", lambda E: E.tensor_tensor(msk[:], msk[:], mskt[:], ALU.add))
        mv = pb.op("vector", lambda E: E.tensor_scalar(msk[:], msk[:], NEG, None, ALU.mult))
        pb.wait("scalar", "q_vector", mv)
        xc = sbuf(pb, "xc", (128, D))
        xc2 = sbuf(pb, "xc2", (128, D))
        hc = sbuf(pb, "hc", (128, D))
        stats = sbuf(pb, "stats", (128, 8))
        sm = sbuf(pb, "sm", (128, 4))

        with pb.scope():
            h1T = sbuf(pb, "h1T", (128, 8, TK))
            wslab = sbuf(pb, "wslab", (128, 8, 256))
            kg = sbuf(pb, "kg", (128, 2, TK))
            vg = sbuf(pb, "vg", (128, 2, TK))
            qg = vg
            vtm = sbuf(pb, "vtm", (128, TK // 128, 256))
            ctxg = sbuf(pb, "ctxg", (128, 2, T))
            ptile = sbuf(pb, "ptile", (128, 256))
            ptt = sbuf(pb, "ptt", (128, 256))

            for c in range(TK // 128):
                r0 = c * 128
                pb.wait("sync", "q_vector", pb.tick("vector"))
                pb.wait("sync", "q_scalar", pb.tick("scalar"))
                lv = pb.dma("sync", xc[:], x_in[r0:r0 + 128, :], "xld_s0")
                pb.wait("vector", "xld_s0", lv)
                pb.wait("vector", "xist", pb.cur("xist"))
                pb.wait("scalar", "xld_s0", lv)
                xiv = emit_ln_chunk(pb, xc[:], xc2[:], stats, hc)
                pb.wait("gpsimd", "q_vector", xiv)
                pb.dma("gpsimd", xi_d[r0:r0 + 128, :], xc2[:], "xist")
                pb.wait("scalar", "q_vector", xiv)
                nv = emit_ln_chunk(pb, xc2[:], hc[:], stats, xc)
                pb.op("vector", lambda E: E.tensor_tensor(hc[:], hc[:], g1bc[:], ALU.mult))
                hv = pb.op("vector", lambda E: E.tensor_tensor(hc[:], hc[:], b1bc[:], ALU.add))
                pb.wait("tensor", "q_vector", hv)
                for dch in range(8):
                    bank = p[dch % 2]
                    tv = pb.op("tensor", lambda E, bank=bank, c=c, dch=dch:
                               E.transpose(bank[:, 0:128], hc[:, dch * 128:(dch + 1) * 128], idn[:]))
                    pb.wait("vector", "q_tensor", tv)
                    ev = pb.op("vector", lambda E, bank=bank, c=c, dch=dch:
                               E.tensor_copy(h1T[:, dch, c * 128:(c + 1) * 128], bank[:, 0:128]))
                    pb.wait("tensor", "q_vector", ev)
            pb.phase()

            for g in range(4):
                pb.phase()
                for n_t, Wmat, bias_t in (("k", Wk, bkt), ("v", Wv, bvt), ("q", Wq, bqt)):
                    pb.wait("sync", "q_tensor", pb.tick("tensor"))
                    lvw = pb.dma("sync", wslab[:],
                                 Wmat[:, g * 256:(g + 1) * 256].rearrange("(c p) o -> p c o", p=128),
                                 "wqkv_ld")
                    pb.wait("tensor", "wqkv_ld", lvw)
                    src_off = halo if n_t == "q" else 0
                    n_tok = T if n_t == "q" else TK
                    dst = {"k": kg, "v": vg, "q": qg}[n_t]
                    scl = 0.125 if n_t == "q" else None
                    for oc2 in range(2):
                        for tt in range((n_tok + 511) // 512):
                            t0 = tt * 512
                            bt = min(512, n_tok - t0)
                            bank = p[2 + (tt + oc2) % 2]
                            mmv = 0
                            for k in range(8):
                                mmv = pb.op("tensor", lambda E, k=k, o=oc2, t0=t0, bt=bt, bank=bank, so=src_off:
                                            E.matmul(bank[:, 0:bt], wslab[:, k, o * 128:(o + 1) * 128],
                                                     h1T[:, k, so + t0:so + t0 + bt],
                                                     start=(k == 0), stop=(k == 7)))
                            pb.wait("vector", "q_tensor", mmv)
                            vvv = pb.op("vector", lambda E, o=oc2, t0=t0, bt=bt, bank=bank, g=g,
                                        dd=dst, bias_t=bias_t, scl=scl:
                                        E.tensor_scalar(dd[:, o, t0:t0 + bt], bank[:, 0:bt],
                                                        bias_t[:, g * 2 + o:g * 2 + o + 1], scl,
                                                        ALU.add, ALU.mult if scl else ALU.bypass))
                            pb.wait("tensor", "q_vector", vvv)
                    if n_t == "v":
                        for c in range(TK // 128):
                            for oc2 in range(2):
                                bank = p[6 + oc2]
                                tv = pb.op("tensor", lambda E, c=c, o=oc2, bank=bank:
                                           E.transpose(bank[:, 0:128], vg[:, o, c * 128:(c + 1) * 128], idn[:]))
                                pb.wait("vector", "q_tensor", tv)
                                ev = pb.op("vector", lambda E, c=c, o=oc2, bank=bank:
                                           E.tensor_copy(vtm[:, c, o * 128:(o + 1) * 128], bank[:, 0:128]))
                                pb.wait("tensor", "q_vector", ev)
                for hh in range(4):
                    oc2 = hh // 2
                    row0 = (hh % 2) * 64
                    for it in range(T // 128):
                        i0 = it * 128
                        kb1 = i0 + halo
                        kb0 = kb1 - 128
                        two = kb0 >= 0
                        bank_s = p[2]
                        pb.wait("tensor", "q_scalar", pb.tick("scalar"))
                        if two:
                            pb.op("tensor", lambda E, q0=i0, kb=kb0, o=oc2, r=row0, bank=bank_s:
                                  E.matmul(bank[:, 0:128], qg[r:r + 64, o, q0:q0 + 128],
                                           kg[r:r + 64, o, kb:kb + 128], start=True, stop=True))
                        mmv = pb.op("tensor", lambda E, q0=i0, kb=kb1, o=oc2, r=row0, bank=bank_s:
                                    E.matmul(bank[:, 128:256], qg[r:r + 64, o, q0:q0 + 128],
                                             kg[r:r + 64, o, kb:kb + 128], start=True, stop=True))
                        pb.wait("vector", "q_tensor", mmv)
                        c0 = 0 if two else 128
                        cw = 256 - c0
                        pb.op("vector", lambda E, bank=bank_s, c0=c0, cw=cw:
                              E.tensor_tensor(bank[:, c0:c0 + cw], bank[:, c0:c0 + cw],
                                              msk[:, c0:c0 + cw], ALU.add))
                        pb.op("vector", lambda E, bank=bank_s, c0=c0, cw=cw:
                              E.reduce_max(sm[:, 0:1], bank[:, c0:c0 + cw], axis=AX.X))
                        mv2 = pb.op("vector", lambda E:
                                    E.tensor_scalar(sm[:, 1:2], sm[:, 0:1], -1.0, None, ALU.mult))
                        pb.wait("scalar", "q_vector", mv2)
                        sv = pb.op("scalar", lambda E, bank=bank_s, c0=c0, cw=cw:
                                   E.activation(ptile[:, c0:c0 + cw], bank[:, c0:c0 + cw], AF.Exp,
                                                bias=sm[:, 1:2], accum_out=sm[:, 2:3]))
                        pb.wait("vector", "q_scalar", sv)
                        pb.op("vector", lambda E: E.reciprocal(sm[:, 3:4], sm[:, 2:3]))
                        nv = pb.op("vector", lambda E, c0=c0, cw=cw:
                                   E.tensor_scalar(ptile[:, c0:c0 + cw], ptile[:, c0:c0 + cw],
                                                   sm[:, 3:4], None, ALU.mult))
                        pb.wait("tensor", "q_vector", nv)
                        bank_o = p[5][row0:row0 + 64, 0:128]
                        first = True
                        for half in range(0 if two else 1, 2):
                            bank_t = p[3 + (half % 2)]
                            tv = pb.op("tensor", lambda E, h2=half, bank_t=bank_t:
                                       E.transpose(bank_t[:, 0:128], ptile[:, h2 * 128:(h2 + 1) * 128],
                                                   idn[:]))
                            pb.wait("vector", "q_tensor", tv)
                            ev = pb.op("vector", lambda E, h2=half, bank_t=bank_t:
                                       E.tensor_copy(ptt[:, h2 * 128:(h2 + 1) * 128], bank_t[:, 0:128]))
                            pb.wait("tensor", "q_vector", ev)
                            kb = kb0 if half == 0 else kb1
                            pb.op("tensor", lambda E, kb=kb, h2=half, hh=hh, bank=bank_o, first=first:
                                  E.matmul(bank, vtm[:, kb // 128, hh * 64:(hh + 1) * 64],
                                           ptt[:, h2 * 128:(h2 + 1) * 128],
                                           start=first, stop=(h2 == 1)))
                            first = False
                        mmv = pb.tick("tensor")
                        pb.wait("vector", "q_tensor", mmv)
                        cv = pb.op("vector", lambda E, hh=hh, i0=i0, bank=bank_o, oc2=oc2:
                                   E.tensor_copy(ctxg[(hh % 2) * 64:(hh % 2) * 64 + 64, oc2, i0:i0 + 128],
                                                 bank))
                        pb.wait("tensor", "q_vector", cv)
                pb.wait("gpsimd", "q_vector", pb.tick("vector"))
                for oc2 in range(2):
                    pb.dma("gpsimd", ctx_d[g * 256 + oc2 * 128:g * 256 + (oc2 + 1) * 128, :],
                           ctxg[:, oc2, :], "ctxst")
            pb.phase()

        with pb.scope():
            woslab = sbuf(pb, "woslab", (128, 8, 128))
            ctxc = sbuf(pb, "ctxc", (128, 8, 512))
            y1 = sbuf(pb, "y1", (128, 8, 512))
            for tt in range(T // 512):
                t0 = tt * 512
                for cch in range(8):
                    pb.wait("sync", "q_tensor", pb.tick("tensor"))
                    pb.dma("sync", ctxc[:, cch, :], ctx_d[cch * 128:(cch + 1) * 128,
                                                          t0:t0 + 512], "ctx_ld")
                for och in range(8):
                    pb.wait("sync", "q_tensor", pb.tick("tensor"))
                    lvw = pb.dma("sync", woslab[:],
                                 Wo[:, och * 128:(och + 1) * 128].rearrange("(c p) o -> p c o", p=128),
                                 "wo_ld")
                    pb.wait("tensor", "wo_ld", pb.cur("wo_ld"))
                    pb.wait("tensor", "ctx_ld", pb.cur("ctx_ld"))
                    bank = p[och % 2]
                    pb.wait("tensor", "q_vector", pb.tick("vector"))
                    mmv = 0
                    for k in range(8):
                        mmv = pb.op("tensor", lambda E, k=k, bank=bank:
                                    E.matmul(bank[:, 0:512], woslab[:, k, :], ctxc[:, k, :],
                                             start=(k == 0), stop=(k == 7)))
                    pb.wait("vector", "q_tensor", mmv)
                    vv = pb.op("vector", lambda E, o=och, bank=bank:
                               E.tensor_scalar(y1[:, o, :], bank[:, 0:512], bot[:, o:o + 1], None,
                                               ALU.add))
                pb.wait("tensor", "q_vector", pb.tick("vector"))
                for c in range(4):
                    r0 = t0 + c * 128
                    pb.wait("gpsimd", "q_vector", pb.tick("vector"))
                    lv = pb.dma("gpsimd", xc[:], xi_d[halo + r0:halo + r0 + 128, :], "xild_s0")
                    for och in range(8):
                        bank = p[2 + och % 2]
                        tv = pb.op("tensor", lambda E, bank=bank, c=c, o=och:
                                   E.transpose(bank[:, 0:128], y1[:, o, c * 128:(c + 1) * 128], idn[:]))
                        pb.wait("vector", "q_tensor", tv)
                        if och == 0:
                            pb.wait("vector", "xild_s0", pb.cur("xild_s0"))
                            pb.wait("vector", "xmst", pb.cur("xmst"))
                        vv = pb.op("vector", lambda E, bank=bank, c=c, o=och:
                                   E.tensor_tensor(xc2[:, o * 128:(o + 1) * 128], bank[:, 0:128],
                                                   xc[:, o * 128:(o + 1) * 128], ALU.add))
                        pb.wait("tensor", "q_vector", vv)
                    pb.wait("gpsimd", "q_vector", pb.tick("vector"))
                    pb.dma("gpsimd", xmid_d[r0:r0 + 128, :], xc2[:], "xmst")
            pb.phase()

        with pb.scope():
            h2T = sbuf(pb, "h2T", (128, 8, T))
            w1slab = sbuf(pb, "w1slab", (128, 2, 8, 128))
            gelc = sbuf(pb, "gelc", (128, 2, 512))
            for c in range(T // 128):
                r0 = c * 128
                pb.wait("sync", "q_vector", pb.tick("vector"))
                pb.wait("sync", "q_scalar", pb.tick("scalar"))
                lv = pb.dma("sync", xc[:], xmid_d[r0:r0 + 128, :], "xm_s0")
                pb.wait("vector", "xm_s0", lv)
                pb.wait("scalar", "xm_s0", lv)
                nv = emit_ln_chunk(pb, xc[:], hc[:], stats, xc2)
                pb.op("vector", lambda E: E.tensor_tensor(hc[:], hc[:], g2bc[:], ALU.mult))
                hv = pb.op("vector", lambda E: E.tensor_tensor(hc[:], hc[:], b2bc[:], ALU.add))
                pb.wait("tensor", "q_vector", hv)
                for dch in range(8):
                    bank = p[dch % 2]
                    tv = pb.op("tensor", lambda E, bank=bank, c=c, dch=dch:
                               E.transpose(bank[:, 0:128], hc[:, dch * 128:(dch + 1) * 128], idn[:]))
                    pb.wait("vector", "q_tensor", tv)
                    ev = pb.op("vector", lambda E, bank=bank, c=c, dch=dch:
                               E.tensor_copy(h2T[:, dch, c * 128:(c + 1) * 128], bank[:, 0:128]))
                    pb.wait("tensor", "q_vector", ev)
            pb.phase()
            for f in range(32):
                sl = f % 2
                pb.wait("sync", "q_tensor", pb.tick("tensor"))
                lvw = pb.dma("sync", w1slab[:, sl, :, :],
                             W1[:, f * 128:(f + 1) * 128].rearrange("(c p) o -> p c o", p=128),
                             "w1_s%d" % sl)
                pb.wait("tensor", "w1_s%d" % sl, pb.cur("w1_s%d" % sl))
                for tt in range(T // 512):
                    t0 = tt * 512
                    bank = p[2 + tt % 2]
                    pb.wait("tensor", "q_scalar", pb.tick("scalar"))
                    mmv = 0
                    for k in range(8):
                        mmv = pb.op("tensor", lambda E, k=k, f=f, t0=t0, bank=bank, sl=sl:
                                    E.matmul(bank[:, 0:512], w1slab[:, sl, k, :],
                                             h2T[:, k, t0:t0 + 512], start=(k == 0), stop=(k == 7)))
                    pb.wait("scalar", "q_tensor", mmv)
                    pb.wait("scalar", "hgst", pb.cur("hgst"))
                    gv2 = pb.op("scalar", lambda E, f=f, tt=tt, bank=bank:
                                E.activation(gelc[:, tt % 2, :], bank[:, 0:512], AF.Gelu_apprx_tanh,
                                             bias=bf1t[:, f:f + 1]))
                    pb.wait("gpsimd", "q_scalar", gv2)
                    pb.dma("gpsimd", hg_d[f * 128:(f + 1) * 128, t0:t0 + 512],
                           gelc[:, tt % 2, :], "hgst")
            pb.phase()

        with pb.scope():
            hgt = sbuf(pb, "hgt", (128, 32, 512))
            w2slab = sbuf(pb, "w2slab", (128, 2, 32, 128))
            youtf = sbuf(pb, "youtf", (128, 8, 512))
            for tt in range(T // 512):
                t0 = tt * 512
                pb.wait("sync", "q_tensor", pb.tick("tensor"))
                for fch in range(32):
                    pb.dma("sync", hgt[:, fch, :], hg_d[fch * 128:(fch + 1) * 128, t0:t0 + 512],
                           "hg_ld")
                for och in range(8):
                    sl = och % 2
                    pb.wait("sync", "q_tensor", pb.tick("tensor"))
                    pb.dma("sync", w2slab[:, sl, :, :],
                           W2[:, och * 128:(och + 1) * 128].rearrange("(c p) o -> p c o", p=128),
                           "w2_s%d" % sl)
                    pb.wait("tensor", "w2_s%d" % sl, pb.cur("w2_s%d" % sl))
                    pb.wait("tensor", "hg_ld", pb.cur("hg_ld"))
                    bank = p[4 + och % 2]
                    pb.wait("tensor", "q_vector", pb.tick("vector"))
                    mmv = 0
                    for k in range(32):
                        mmv = pb.op("tensor", lambda E, k=k, bank=bank, sl=sl:
                                    E.matmul(bank[:, 0:512], w2slab[:, sl, k, :], hgt[:, k, :],
                                             start=(k == 0), stop=(k == 31)))
                    pb.wait("vector", "q_tensor", mmv)
                    vv = pb.op("vector", lambda E, o=och, bank=bank:
                               E.tensor_copy(youtf[:, o, :], bank[:, 0:512]))
                pb.wait("tensor", "q_vector", pb.tick("vector"))
                for c in range(4):
                    r0 = t0 + c * 128
                    pb.wait("gpsimd", "q_vector", pb.tick("vector"))
                    lvm = pb.dma("gpsimd", xc[:], xmid_d[r0:r0 + 128, :], "fm_s0")
                    lvx = pb.dma("gpsimd", xc2[:], x_in[halo + r0:halo + r0 + 128, :], "fx_s0")
                    for och in range(8):
                        bank = p[och % 2]
                        tv = pb.op("tensor", lambda E, bank=bank, c=c, o=och:
                                   E.transpose(bank[:, 0:128], youtf[:, o, c * 128:(c + 1) * 128],
                                               idn[:]))
                        pb.wait("vector", "q_tensor", tv)
                        if och == 0:
                            pb.wait("vector", "fm_s0", pb.cur("fm_s0"))
                            pb.wait("vector", "fx_s0", pb.cur("fx_s0"))
                            pb.wait("vector", "yst", pb.cur("yst"))
                        pb.op("vector", lambda E, bank=bank, o=och:
                              E.tensor_tensor(hc[:, o * 128:(o + 1) * 128], bank[:, 0:128],
                                              xc[:, o * 128:(o + 1) * 128], ALU.add))
                        vv = pb.op("vector", lambda E, bank=bank, o=och:
                                   E.tensor_tensor(hc[:, o * 128:(o + 1) * 128],
                                                   hc[:, o * 128:(o + 1) * 128],
                                                   xc2[:, o * 128:(o + 1) * 128], ALU.add))
                        pb.wait("tensor", "q_vector", vv)
                    fv = pb.op("vector", lambda E: E.tensor_tensor(hc[:], hc[:], b2bc2[:], ALU.add))
                    pb.wait("gpsimd", "q_vector", fv)
                    pb.dma("gpsimd", y[r0:r0 + 128, :], hc[:], "yst")
        pb.wait("gpsimd", "yst", pb.cur("yst"))
        pb.build()
    return nc


def build_moe():
    S = SEQ
    nc = bass.Bass("TRN2", target_bir_lowering=False, debug=False)
    x_in = nc.dram_tensor("x", [S, D], F32, kind="ExternalInput").ap()
    gw = nc.dram_tensor("gw", [D, NE], F32, kind="ExternalInput").ap()
    Wg = nc.dram_tensor("Wg", [4, D, NI], F32, kind="ExternalInput").ap()
    Wu = nc.dram_tensor("Wu", [4, D, NI], F32, kind="ExternalInput").ap()
    Wd = nc.dram_tensor("Wd", [4, NI, D], F32, kind="ExternalInput").ap()
    y = nc.dram_tensor("y", [S, D], F32, kind="ExternalOutput").ap()
    xsel_d = nc.dram_tensor("xsel_d", [4, D, KSEL], F32, kind="Internal").ap()
    gat_d = nc.dram_tensor("gat_d", [4, KSEL], F32, kind="Internal").ap()
    ctbh_d = nc.dram_tensor("ctbh_d", [4, KSEL, D], BF16, kind="Internal").ap()
    ctbl_d = nc.dram_tensor("ctbl_d", [4, KSEL, D], BF16, kind="Internal").ap()

    est = contextlib.ExitStack()
    with est:
        pb = PB(nc, est)
        idn = make_identity(pb)
        p = psum8(pb)
        xc = sbuf(pb, "xc", (128, D))
        hc = sbuf(pb, "hc", (128, D))
        stats = sbuf(pb, "stats", (128, 8))
        sm = sbuf(pb, "sm", (128, 4))
        logits_tm = sbuf(pb, "logits_tm", (128, 32, NE))
        aff_tm = sbuf(pb, "aff_tm", (128, 32, NE))
        affh_tm = sbuf(pb, "affh_tm", (128, 32, NE), BF16)
        affl_tm = sbuf(pb, "affl_tm", (128, 32, NE), BF16)
        slots_tm = sbuf(pb, "slots_tm", (128, 32, 4))
        slots_em = sbuf(pb, "slots_em", (8, S))
        srow = sbuf(pb, "srow", (128, 2, 512))
        scol = sbuf(pb, "scol", (128, 8))
        ones1 = sbuf(pb, "ones1", (1, 128))

        with pb.scope():
            t1 = sbuf(pb, "t1i", (128, 2, 512), I32)
            t2 = sbuf(pb, "t2i", (128, 8), I32)
            pb.op("gpsimd", lambda E: E.iota(t1[:], [[512, 2], [1, 512]], base=0,
                                             channel_multiplier=0))
            gv = pb.op("gpsimd", lambda E: E.iota(t2[:], [[128, 8]], base=0,
                                                  channel_multiplier=1))
            pb.wait("vector", "q_gpsimd", gv)
            pb.op("vector", lambda E: E.tensor_copy(srow[:], t1[:]))
            pb.op("vector", lambda E: E.tensor_copy(scol[:], t2[:]))
        ov = pb.op("vector", lambda E: E.memset(ones1[:], 1.0))
        pb.wait("tensor", "q_vector", ov)

        with pb.scope():
            h_hi = sbuf(pb, "h_hi", (128, 32, D), BF16)
            h_lo = sbuf(pb, "h_lo", (128, 32, D), BF16)
            gwt = sbuf(pb, "gwt", (128, 8, NE))
            pb.wait("sync", "q_vector", pb.tick("vector"))
            pb.dma("sync", gwt[:], gw.rearrange("(c p) e -> p c e", p=128), "gwld")
            pb.wait("tensor", "gwld", pb.cur("gwld"))
            for c in range(S // 128):
                r0 = c * 128
                pb.wait("sync", "q_vector", pb.tick("vector"))
                pb.wait("sync", "q_scalar", pb.tick("scalar"))
                lv = pb.dma("sync", xc[:], x_in[r0:r0 + 128, :], "xld_s0")
                pb.wait("vector", "xld_s0", lv)
                pb.wait("scalar", "xld_s0", lv)
                hv = emit_ln_chunk(pb, xc[:], hc[:], stats, hc[:])
                pb.op("vector", lambda E, c=c: E.tensor_copy(h_hi[:, c, :], hc[:]))
                pb.op("vector", lambda E, c=c: E.tensor_tensor(xc[:], hc[:], h_hi[:, c, :],
                                                               ALU.subtract))
                hv2 = pb.op("vector", lambda E, c=c: E.tensor_copy(h_lo[:, c, :], xc[:]))
                pb.wait("tensor", "q_vector", hv2)
                bank_l = p[2]
                for dch in range(8):
                    bank = p[dch % 2]
                    tv = pb.op("tensor", lambda E, bank=bank, dch=dch:
                               E.transpose(bank[:, 0:128], hc[:, dch * 128:(dch + 1) * 128],
                                           idn[:]))
                    pb.wait("vector", "q_tensor", tv)
                    ev = pb.op("vector", lambda E, bank=bank, dch=dch:
                               E.tensor_copy(hc[:, dch * 128:(dch + 1) * 128], bank[:, 0:128]))
                    pb.wait("tensor", "q_vector", ev)
                    pb.op("tensor", lambda E, bank=bank_l, dch=dch:
                          E.matmul(bank[:, 0:NE], hc[:, dch * 128:(dch + 1) * 128],
                                   gwt[:, dch, :], start=(dch == 0), stop=(dch == 7)))
                mmv = pb.tick("tensor")
                pb.wait("vector", "q_tensor", mmv)
                lvv = pb.op("vector", lambda E, c=c, bank=bank_l:
                            E.tensor_copy(logits_tm[:, c, :], bank[:, 0:NE]))
                pb.wait("tensor", "q_vector", lvv)

            pb.phase()
            with pb.scope():
                a_em = sbuf(pb, "a_em", (8, S))
                m_em = sbuf(pb, "m_em", (8, S), BF16)
                ones8 = sbuf(pb, "ones8", (8, S), BF16)
                th = sbuf(pb, "th", (8, 8))
                pb.op("vector", lambda E: E.memset(ones8[:], 1.0))
                for c in range(32):
                    pb.op("vector", lambda E, c=c: E.reduce_max(sm[:, 0:1], logits_tm[:, c, :],
                                                                axis=AX.X))
                    mv = pb.op("vector", lambda E: E.tensor_scalar(sm[:, 1:2], sm[:, 0:1], -1.0,
                                                                   None, ALU.mult))
                    pb.wait("scalar", "q_vector", mv)
                    sv = pb.op("scalar", lambda E, c=c:
                               E.activation(aff_tm[:, c, :], logits_tm[:, c, :], AF.Exp,
                                            bias=sm[:, 1:2], accum_out=sm[:, 2:3]))
                    pb.wait("vector", "q_scalar", sv)
                    pb.op("vector", lambda E: E.reciprocal(sm[:, 3:4], sm[:, 2:3]))
                    pb.op("vector", lambda E, c=c:
                          E.tensor_scalar(aff_tm[:, c, :], aff_tm[:, c, :], sm[:, 3:4],
                                          None, ALU.mult))
                    pb.op("vector", lambda E, c=c: E.tensor_copy(affh_tm[:, c, :],
                                                                 aff_tm[:, c, :]))
                    av2 = pb.op("vector", lambda E, c=c:
                                E.scalar_tensor_tensor(affl_tm[:, c, :], affh_tm[:, c, :], -1.0,
                                                       aff_tm[:, c, :], ALU.mult, ALU.add))
                    pb.wait("tensor", "q_vector", av2)
                    bank = p[3]
                    tv = pb.op("tensor", lambda E, c=c, bank=bank:
                               E.transpose(bank[0:NE, 0:128], aff_tm[:, c, :], idn[:]))
                    pb.wait("vector", "q_tensor", tv)
                    ev = pb.op("vector", lambda E, c=c, bank=bank:
                               E.tensor_copy(a_em[:, c * 128:(c + 1) * 128], bank[0:NE, 0:128]))
                    pb.wait("tensor", "q_vector", ev)
                pb.op("vector", lambda E: E.memset(th[:, 0:1], 0.0))
                pb.op("vector", lambda E: E.memset(th[:, 1:2], 1.0))
                for it in range(40):
                    pb.op("vector", lambda E: E.tensor_tensor(th[:, 2:3], th[:, 0:1], th[:, 1:2],
                                                              ALU.add))
                    pb.op("vector", lambda E: E.tensor_scalar(th[:, 2:3], th[:, 2:3], 0.5, None,
                                                              ALU.mult))
                    pb.op("vector", lambda E: E.tensor_scalar(m_em[:], a_em[:], th[:, 2:3], None,
                                                              ALU.is_ge))
                    pb.op("vector", lambda E: E.reduce_sum(th[:, 3:4], m_em[:], axis=AX.X))
                    pb.op("vector", lambda E: E.tensor_scalar(th[:, 4:5], th[:, 3:4], float(KSEL),
                                                              None, ALU.is_ge))
                    pb.op("vector", lambda E: E.tensor_scalar(th[:, 5:6], th[:, 4:5], -1.0, 1.0,
                                                              ALU.mult, ALU.add))
                    pb.op("vector", lambda E: E.tensor_tensor(th[:, 6:7], th[:, 0:1], th[:, 5:6],
                                                              ALU.mult))
                    pb.op("vector", lambda E: E.tensor_tensor(th[:, 7:8], th[:, 2:3], th[:, 4:5],
                                                              ALU.mult))
                    pb.op("vector", lambda E: E.tensor_tensor(th[:, 0:1], th[:, 6:7], th[:, 7:8],
                                                              ALU.add))
                    pb.op("vector", lambda E: E.tensor_tensor(th[:, 6:7], th[:, 1:2], th[:, 4:5],
                                                              ALU.mult))
                    pb.op("vector", lambda E: E.tensor_tensor(th[:, 7:8], th[:, 2:3], th[:, 5:6],
                                                              ALU.mult))
                    pb.op("vector", lambda E: E.tensor_tensor(th[:, 1:2], th[:, 6:7], th[:, 7:8],
                                                              ALU.add))
                pb.op("vector", lambda E: E.tensor_scalar(m_em[:], a_em[:], th[:, 0:1], None,
                                                          ALU.is_ge))
                pb.op("vector", lambda E: E.tensor_tensor_scan(slots_em[:], ones8[:], m_em[:],
                                                               0.0, ALU.mult, ALU.add))
                pb.op("vector", lambda E: E.tensor_scalar(a_em[:], slots_em[:], float(KSEL),
                                                          None, ALU.is_le))
                pb.op("vector", lambda E: E.tensor_tensor(m_em[:], m_em[:], a_em[:], ALU.mult))
                pb.op("vector", lambda E: E.tensor_scalar(slots_em[:], slots_em[:], 1.0, None,
                                                          ALU.subtract))
                pb.op("vector", lambda E: E.tensor_tensor(slots_em[:], slots_em[:], m_em[:],
                                                          ALU.mult))
                pb.op("vector", lambda E: E.tensor_scalar(a_em[:], m_em[:], -1.0, 1.0, ALU.mult,
                                                          ALU.add))
                pb.op("vector", lambda E: E.tensor_scalar(a_em[:], a_em[:], -7.0, None, ALU.mult))
                sev = pb.op("vector", lambda E: E.tensor_tensor(slots_em[:], slots_em[:], a_em[:],
                                                                ALU.add))
                pb.wait("tensor", "q_vector", sev)
                for c in range(32):
                    bank = p[3]
                    tv = pb.op("tensor", lambda E, c=c, bank=bank:
                               E.transpose(bank[:, 0:4], slots_em[0:4, c * 128:(c + 1) * 128],
                                           idn[0:4, 0:4]))
                    pb.wait("vector", "q_tensor", tv)
                    ev = pb.op("vector", lambda E, c=c, bank=bank:
                               E.tensor_copy(slots_tm[:, c, :], bank[:, 0:4]))
                    pb.wait("tensor", "q_vector", ev)
            pb.phase()

            with pb.scope():
                selT = sbuf(pb, "selT", (128, 32, 512), BF16)
                xso = sbuf(pb, "xso", (128, 2, 512))
                gatro = sbuf(pb, "gatro", (1, 512))
                for el in range(4):
                    for st in range(2):
                        bv = 0
                        for c in range(32):
                            bv = pb.op("vector", lambda E, el=el, st=st, c=c:
                                       E.tensor_scalar(selT[:, c, :], srow[:, st, :],
                                                       slots_tm[:, c, el:el + 1], None,
                                                       ALU.is_equal))
                        pb.wait("tensor", "q_vector", bv)
                        for c in range(32):
                            for dch in range(8):
                                for hl in range(2):
                                    hsrc = h_hi if hl == 0 else h_lo
                                    pb.op("tensor", lambda E, c=c, dch=dch, hsrc=hsrc, hl=hl:
                                          E.matmul(p[dch][:, 0:512],
                                                   hsrc[:, c, dch * 128:(dch + 1) * 128],
                                                   selT[:, c, :],
                                                   start=(c == 0 and hl == 0),
                                                   stop=(c == 31 and hl == 1)))
                        mmv = pb.tick("tensor")
                        pb.wait("vector", "q_tensor", mmv)
                        for dch in range(8):
                            sl = dch % 2
                            pb.wait("vector", "xsst_s%d" % sl, pb.cur("xsst_s%d" % sl))
                            vv = pb.op("vector", lambda E, dch=dch, sl=sl:
                                       E.tensor_copy(xso[:, sl, :], p[dch][:, 0:512]))
                            pb.wait("gpsimd", "q_vector", vv)
                            pb.dma("gpsimd", xsel_d[el, dch * 128:(dch + 1) * 128,
                                                    st * 512:(st + 1) * 512],
                                   xso[:, sl, :], "xsst_s%d" % sl)
                        pb.wait("tensor", "q_vector", pb.tick("vector"))
                        for c in range(32):
                            for hl in range(2):
                                asrc = affh_tm if hl == 0 else affl_tm
                                pb.op("tensor", lambda E, c=c, el=el, asrc=asrc, hl=hl:
                                      E.matmul(p[0][0:1, 0:512], asrc[:, c, el:el + 1],
                                               selT[:, c, :],
                                               start=(c == 0 and hl == 0),
                                               stop=(c == 31 and hl == 1)))
                        mmv = pb.tick("tensor")
                        pb.wait("vector", "q_tensor", mmv)
                        pb.wait("vector", "gtst", pb.cur("gtst"))
                        gvv = pb.op("vector", lambda E: E.tensor_copy(gatro[:], p[0][0:1, 0:512]))
                        pb.wait("gpsimd", "q_vector", gvv)
                        pb.dma("gpsimd",
                               gat_d[el].rearrange("(a s) -> a s", a=1)[:, st * 512:(st + 1) * 512],
                               gatro[:], "gtst")
        pb.phase()

        with pb.scope():
            xselb = sbuf(pb, "xselb", (128, 8, KSEL))
            wslab = sbuf(pb, "wslab", (128, 2, 8, 128))
            wdsl = sbuf(pb, "wdsl", (128, 2, 512))
            gvbuf = sbuf(pb, "gvbuf", (128, KSEL))
            hblk = sbuf(pb, "hblk", (128, 16, KSEL))
            gatc = sbuf(pb, "gatc", (128, 8))
            gro = sbuf(pb, "gro", (1, KSEL))
            ctf = sbuf(pb, "ctf", (128, 512))
            cth = sbuf(pb, "cth", (128, 2, 512), BF16)
            ctl = sbuf(pb, "ctl", (128, 2, 512), BF16)
            for el in range(4):
                pb.phase()
                pb.dma("sync", xselb[:], xsel_d[el].rearrange("(c p) s -> p c s", p=128), "xsl")
                lvg = pb.dma("sync", gro[:], gat_d[el].rearrange("(a s) -> a s", a=1), "gtl")
                pb.wait("tensor", "xsl", pb.cur("xsl"))
                pb.wait("tensor", "gtl", pb.cur("gtl"))
                for sc in range(8):
                    bank = p[6]
                    tv = pb.op("tensor", lambda E, sc=sc, bank=bank:
                               E.transpose(bank[:, 0:1], gro[:, sc * 128:(sc + 1) * 128],
                                           idn[0:1, 0:1]))
                    pb.wait("vector", "q_tensor", tv)
                    ev = pb.op("vector", lambda E, sc=sc, bank=bank:
                               E.tensor_copy(gatc[:, sc:sc + 1], bank[:, 0:1]))
                    pb.wait("tensor", "q_vector", ev)
                for ic in range(16):
                    sl = ic % 2
                    pb.wait("sync", "q_tensor", pb.tick("tensor"))
                    pb.dma("sync", wslab[:, sl, :, :],
                           Wg[el, :, ic * 128:(ic + 1) * 128].rearrange("(c p) i -> p c i", p=128),
                           "wg_s%d" % sl)
                    pb.wait("tensor", "wg_s%d" % sl, pb.cur("wg_s%d" % sl))
                    pb.wait("tensor", "q_scalar", pb.tick("scalar"))
                    pb.wait("scalar", "q_vector", pb.tick("vector"))
                    for st in range(2):
                        bank = p[2 + st]
                        mmv = 0
                        for k in range(8):
                            mmv = pb.op("tensor", lambda E, k=k, st=st, bank=bank, sl=sl:
                                        E.matmul(bank[:, 0:512], wslab[:, sl, k, :],
                                                 xselb[:, k, st * 512:(st + 1) * 512],
                                                 start=(k == 0), stop=(k == 7)))
                        pb.wait("scalar", "q_tensor", mmv)
                        pb.op("scalar", lambda E, st=st, bank=bank:
                              E.activation(gvbuf[:, st * 512:(st + 1) * 512], bank[:, 0:512],
                                           AF.Silu))
                    sv = pb.tick("scalar")
                    pb.wait("sync", "q_tensor", pb.tick("tensor"))
                    pb.dma("sync", wslab[:, sl, :, :],
                           Wu[el, :, ic * 128:(ic + 1) * 128].rearrange("(c p) i -> p c i", p=128),
                           "wu_s%d" % sl)
                    pb.wait("tensor", "wu_s%d" % sl, pb.cur("wu_s%d" % sl))
                    pb.wait("tensor", "q_scalar", sv)
                    for st in range(2):
                        bank = p[4 + st]
                        mmv = 0
                        for k in range(8):
                            mmv = pb.op("tensor", lambda E, k=k, st=st, bank=bank, sl=sl:
                                        E.matmul(bank[:, 0:512], wslab[:, sl, k, :],
                                                 xselb[:, k, st * 512:(st + 1) * 512],
                                                 start=(k == 0), stop=(k == 7)))
                        pb.wait("vector", "q_tensor", mmv)
                        pb.wait("vector", "q_scalar", sv)
                        pb.op("vector", lambda E, ic=ic, st=st, bank=bank:
                              E.tensor_tensor(hblk[:, ic, st * 512:(st + 1) * 512],
                                              gvbuf[:, st * 512:(st + 1) * 512],
                                              bank[:, 0:512], ALU.mult))
                    pb.wait("tensor", "q_vector", pb.tick("vector"))
                for dh in range(2):
                    pb.phase()
                    for ic in range(16):
                        sl = ic % 2
                        pb.wait("sync", "q_tensor", pb.tick("tensor"))
                        pb.dma("sync", wdsl[:, sl, :],
                               Wd[el, ic * 128:(ic + 1) * 128, dh * 512:(dh + 1) * 512],
                               "wd_s%d" % sl)
                        pb.wait("tensor", "wd_s%d" % sl, pb.cur("wd_s%d" % sl))
                        for sg in range(8):
                            pb.op("tensor", lambda E, ic=ic, sg=sg, sl=sl:
                                  E.matmul(p[sg][:, 0:512], hblk[:, ic, sg * 128:(sg + 1) * 128],
                                           wdsl[:, sl, :], start=(ic == 0), stop=(ic == 15)))
                    mmv = pb.tick("tensor")
                    pb.wait("vector", "q_tensor", mmv)
                    for sg in range(8):
                        sl = sg % 2
                        pb.wait("vector", "cst_s%d" % sl, pb.cur("cst_s%d" % sl))
                        pb.op("vector", lambda E, sg=sg:
                              E.tensor_scalar(ctf[:], p[sg][:, 0:512], gatc[:, sg:sg + 1], 0.5,
                                              ALU.mult, ALU.mult))
                        pb.op("vector", lambda E, sl=sl: E.tensor_copy(cth[:, sl, :], ctf[:]))
                        vv = pb.op("vector", lambda E, sl=sl:
                                   E.scalar_tensor_tensor(ctl[:, sl, :], cth[:, sl, :],
                                                          -1.0, ctf[:], ALU.mult, ALU.add))
                        pb.wait("gpsimd", "q_vector", vv)
                        pb.dma("gpsimd", ctbh_d[el, sg * 128:(sg + 1) * 128,
                                                dh * 512:(dh + 1) * 512], cth[:, sl, :],
                               "cst_s%d" % sl)
                        pb.dma("gpsimd", ctbl_d[el, sg * 128:(sg + 1) * 128,
                                                dh * 512:(dh + 1) * 512], ctl[:, sl, :],
                               "cst_s%d" % sl)
        pb.phase()

        with pb.scope():
            ctbh = sbuf(pb, "ctbh", (128, 2, 8, D), BF16)
            ctbl = sbuf(pb, "ctbl", (128, 2, 8, D), BF16)
            sbc = sbuf(pb, "sbc", (128, 2, S))
            selS = sbuf(pb, "selS", (128, 2, 8, 512), BF16)
            yo = sbuf(pb, "yo", (128, D))
            yi = sbuf(pb, "yi", (128, D))
            eh = sbuf(pb, "eh", (8, 512))
            ti = sbuf(pb, "ti", (8, 512), I32)
            tp = sbuf(pb, "tp", (8, 1), I32)
            tpf = sbuf(pb, "tpf", (8, 1))
            pb.op("gpsimd", lambda E: E.iota(ti[:], [[1, 4], [0, 128]], base=0,
                                             channel_multiplier=0))
            gv = pb.op("gpsimd", lambda E: E.iota(tp[:], [[0, 1]], base=0,
                                                  channel_multiplier=1))
            pb.wait("vector", "q_gpsimd", gv)
            pb.op("vector", lambda E: E.tensor_copy(tpf[:], tp[:]))
            ev0 = pb.op("vector", lambda E: E.tensor_scalar(eh[:], ti[:], tpf[:, 0:1], None,
                                                            ALU.is_equal))
            pb.wait("tensor", "q_vector", ev0)
            for grp in range(2):
                pb.phase()
                for gi in range(2):
                    el = grp * 2 + gi
                    pb.dma("sync", ctbh[:, gi, :, :],
                           ctbh_d[el].rearrange("(g p) d -> p g d", p=128), "ctl_ld")
                    pb.dma("sync", ctbl[:, gi, :, :],
                           ctbl_d[el].rearrange("(g p) d -> p g d", p=128), "ctl_ld")
                    for c0 in range(0, S, 512):
                        bank = p[6]
                        tv = pb.op("tensor", lambda E, el=el, c0=c0, bank=bank:
                                   E.matmul(bank[:, 0:512], eh[0:8, el * 128:(el + 1) * 128],
                                            slots_em[0:8, c0:c0 + 512], start=True, stop=True))
                        pb.wait("vector", "q_tensor", tv)
                        cv = pb.op("vector", lambda E, gi=gi, c0=c0, bank=bank:
                                   E.tensor_copy(sbc[:, gi, c0:c0 + 512], bank[:, 0:512]))
                        pb.wait("tensor", "q_vector", cv)
                pb.wait("tensor", "ctl_ld", pb.cur("ctl_ld"))
                for tg in range(8):
                    bv = 0
                    for gi in range(2):
                        for sc in range(8):
                            bv = pb.op("vector", lambda E, gi=gi, sc=sc, tg=tg:
                                       E.tensor_scalar(selS[:, gi, sc, :],
                                                       sbc[:, gi, tg * 512:(tg + 1) * 512],
                                                       scol[:, sc:sc + 1], None, ALU.is_equal))
                    pb.wait("tensor", "q_vector", bv)
                    for tc in range(4):
                        for dt in range(2):
                            bank = p[tc * 2 + dt]
                            nmm = 0
                            for gi in range(2):
                                for sc in range(8):
                                    for hl in range(2):
                                        csrc = ctbh if hl == 0 else ctbl
                                        pb.op("tensor", lambda E, gi=gi, sc=sc, hl=hl, tc=tc,
                                              dt=dt, bank=bank, csrc=csrc, nmm=nmm:
                                              E.matmul(bank[:, 0:512],
                                                       selS[:, gi, sc, tc * 128:(tc + 1) * 128],
                                                       csrc[:, gi, sc, dt * 512:(dt + 1) * 512],
                                                       start=(nmm == 0), stop=(nmm == 31)))
                                        nmm += 1
                    mmv = pb.tick("tensor")
                    pb.wait("vector", "q_tensor", mmv)
                    for tc in range(4):
                        r0 = tg * 512 + tc * 128
                        if grp == 1:
                            pb.wait("sync", "q_vector", pb.tick("vector"))
                            lv = pb.dma("sync", yi[:], y[r0:r0 + 128, :], "yrd_s0")
                            pb.wait("vector", "yrd_s0", lv)
                        pb.wait("vector", "yst", pb.cur("yst"))
                        for dt in range(2):
                            bank = p[tc * 2 + dt]
                            if grp == 0:
                                pb.op("vector", lambda E, dt=dt, bank=bank:
                                      E.tensor_copy(yo[:, dt * 512:(dt + 1) * 512],
                                                    bank[:, 0:512]))
                            else:
                                pb.op("vector", lambda E, dt=dt, bank=bank:
                                      E.tensor_tensor(yo[:, dt * 512:(dt + 1) * 512],
                                                      bank[:, 0:512],
                                                      yi[:, dt * 512:(dt + 1) * 512], ALU.add))
                        vv = pb.tick("vector")
                        pb.wait("gpsimd", "q_vector", vv)
                        pb.dma("gpsimd", y[r0:r0 + 128, :], yo[:], "yst")
        pb.wait("gpsimd", "yst", pb.cur("yst"))
        pb.build()
    return nc


def build_l1attn(N, NQ, q0):
    H, hd = NH, HD
    NC = N // 128
    NJT = N // 512
    NCQ = NQ // 128
    SW = 64 + N + 64
    nc = bass.Bass("TRN2", target_bir_lowering=False, debug=False)
    x3 = nc.dram_tensor("x3", [N, D], F32, kind="ExternalInput").ap()
    pa = nc.dram_tensor("pa", [N, D], F32, kind="ExternalInput").ap()
    pbx = nc.dram_tensor("pb", [N, D], F32, kind="ExternalInput").ap()
    wqkvT = nc.dram_tensor("wqkvT", [D, 3 * D], F32, kind="ExternalInput").ap()
    wprojT = nc.dram_tensor("wprojT", [D, D], F32, kind="ExternalInput").ap()
    bproj = nc.dram_tensor("bproj", [D], F32, kind="ExternalInput").ap()
    rpeq = nc.dram_tensor("rpeq", [H, hd, NB], F32, kind="ExternalInput").ap()
    rpek = nc.dram_tensor("rpek", [H, hd, NB], F32, kind="ExternalInput").ap()
    rpev = nc.dram_tensor("rpev", [H, NB, hd], F32, kind="ExternalInput").ap()
    y = nc.dram_tensor("y", [NQ, D], F32, kind="ExternalOutput").ap()
    x4_d = nc.dram_tensor("x4_d", [N, D], F32, kind="Internal").ap()
    hT_d = nc.dram_tensor("hT_d", [D, N], F32, kind="Internal").ap()
    qkT_d = nc.dram_tensor("qkT_d", [2 * D, N], F32, kind="Internal").ap()
    vt_d = nc.dram_tensor("vt_d", [N, D], F32, kind="Internal").ap()
    kldbm_d = nc.dram_tensor("kldbm_d", [NB, N + 128], F32, kind="Internal").ap()
    band_d = nc.dram_tensor("band_d", [128, 256], F32, kind="Internal").ap()
    ctx_d = nc.dram_tensor("ctx_d", [D, NQ], F32, kind="Internal").ap()

    est = contextlib.ExitStack()
    with est:
        pb = PB(nc, est)
        idn = make_identity(pb)
        p = psum8(pb)
        ones1 = sbuf(pb, "ones1", (1, 128))
        onescol = sbuf(pb, "onescol", (128, 1))
        pb.op("vector", lambda E: E.memset(ones1[:], 1.0))
        ov = pb.op("vector", lambda E: E.memset(onescol[:], 1.0))
        Tm = {}
        Tl = {}
        dclf = sbuf(pb, "dclf", (128, 512))
        with pb.scope():
            dcli = sbuf(pb, "dcli", (128, 512), I32)
            gv = pb.op("gpsimd", lambda E: E.iota(dcli[:], [[1, 512]], base=0,
                                                  channel_multiplier=-1))
            pb.wait("vector", "q_gpsimd", gv)
            pb.op("vector", lambda E: E.tensor_copy(dclf[:], dcli[:]))
        for s in range(-1, 4):
            Tm[s] = sbuf(pb, "tm_s%d" % (s + 1), (128, 512))
            pb.op("vector", lambda E, s=s: E.tensor_scalar(Tm[s][:], dclf[:],
                                                           float(32 + 128 * s), None, ALU.is_ge))
        for s in range(0, 5):
            Tl[s] = sbuf(pb, "tl_s%d" % s, (128, 512))
            mv = pb.op("vector", lambda E, s=s: E.tensor_scalar(Tl[s][:], dclf[:],
                                                                float(-32 + 128 * s), None,
                                                                ALU.is_le))
        pb.wait("tensor", "q_vector", mv)
        pb.wait("sync", "q_vector", mv)

        with pb.scope():
            xc = sbuf(pb, "p0xc", (128, D))
            x2c = sbuf(pb, "p0x2", (128, D))
            hc = sbuf(pb, "p0hc", (128, D))
            stats = sbuf(pb, "p0st", (128, 8))
            for c in range(NC):
                r0 = c * 128
                pb.wait("sync", "q_vector", pb.tick("vector"))
                pb.wait("sync", "q_scalar", pb.tick("scalar"))
                pb.wait("sync", "x4st", pb.cur("x4st"))
                pb.wait("sync", "hTst", pb.cur("hTst"))
                lv1 = pb.dma("sync", xc[:], x3[r0:r0 + 128, :], "xl1")
                lv2 = pb.dma("sync", x2c[:], pa[r0:r0 + 128, :], "xl2")
                pb.wait("vector", "xl1", lv1)
                pb.wait("vector", "xl2", lv2)
                a1 = pb.op("vector", lambda E: E.tensor_tensor(xc[:], xc[:], x2c[:], ALU.add))
                pb.wait("sync", "q_vector", a1)
                lv3 = pb.dma("sync", x2c[:], pbx[r0:r0 + 128, :], "xl2")
                pb.wait("vector", "xl2", lv3)
                av = pb.op("vector", lambda E: E.tensor_tensor(xc[:], xc[:], x2c[:], ALU.add))
                pb.wait("gpsimd", "q_vector", av)
                pb.wait("scalar", "q_vector", av)
                pb.dma("gpsimd", x4_d[r0:r0 + 128, :], xc[:], "x4st")
                emit_ln_chunk(pb, xc[:], hc[:], stats, hc[:])
                pb.wait("tensor", "q_vector", pb.tick("vector"))
                pb.wait("tensor", "hTst", pb.cur("hTst"))
                for dch in range(8):
                    bank = p[dch % 2]
                    tv = pb.op("tensor", lambda E, bank=bank, dch=dch:
                               E.transpose(bank[:, 0:128], hc[:, dch * 128:(dch + 1) * 128],
                                           idn[:]))
                    pb.wait("vector", "q_tensor", tv)
                    ev = pb.op("vector", lambda E, bank=bank, dch=dch:
                               E.tensor_copy(x2c[:, dch * 128:(dch + 1) * 128], bank[:, 0:128]))
                    pb.wait("tensor", "q_vector", ev)
                    pb.wait("gpsimd", "q_vector", ev)
                    pb.dma("gpsimd", hT_d[dch * 128:(dch + 1) * 128, r0:r0 + 128],
                           x2c[:, dch * 128:(dch + 1) * 128], "hTst")
        pb.phase()

        with pb.scope():
            wsl = sbuf(pb, "p1w", (128, 8, 512))
            htb = sbuf(pb, "p1h", (128, 2, 8, 128))
            ev1 = sbuf(pb, "p1e", (128, 2, 512))
            tsb = sbuf(pb, "p1t", (128, 2, 128))
            for ft in range(6):
                pb.wait("sync", "q_tensor", pb.tick("tensor"))
                pb.dma("sync", wsl[:],
                       wqkvT[:, ft * 512:(ft + 1) * 512].rearrange("(c p) f -> p c f", p=128),
                       "p1wl")
                pb.wait("tensor", "p1wl", pb.cur("p1wl"))
                for tc in range(NC):
                    sl = tc % 2
                    pb.wait("sync", "q_tensor", pb.tick("tensor"))
                    pb.dma("sync", htb[:, sl, :, :],
                           hT_d.rearrange("(c p) n -> p c n", p=128)[:, :,
                                                                    tc * 128:(tc + 1) * 128],
                           "p1hl_s%d" % sl)
                    pb.wait("tensor", "p1hl_s%d" % sl, pb.cur("p1hl_s%d" % sl))
                    pb.wait("tensor", "q_vector", pb.tick("vector"))
                    bank = p[sl]
                    for k in range(8):
                        mmv = pb.op("tensor", lambda E, k=k, sl=sl, bank=bank:
                                    E.matmul(bank[:, 0:512], htb[:, sl, k, :], wsl[:, k, :],
                                             start=(k == 0), stop=(k == 7)))
                    pb.wait("vector", "q_tensor", mmv)
                    pb.wait("vector", "vst", pb.cur("vst"))
                    if ft < 2:
                        vv = pb.op("vector", lambda E, sl=sl, bank=bank:
                                   E.tensor_scalar(ev1[:, sl, :], bank[:, 0:512], 0.125, None,
                                                   ALU.mult))
                    else:
                        vv = pb.op("vector", lambda E, sl=sl, bank=bank:
                                   E.tensor_copy(ev1[:, sl, :], bank[:, 0:512]))
                    if ft >= 4:
                        pb.wait("gpsimd", "q_vector", vv)
                        pb.dma("gpsimd", vt_d[tc * 128:(tc + 1) * 128,
                                              (ft - 4) * 512:(ft - 3) * 512],
                               ev1[:, sl, :], "vst")
                    else:
                        pb.wait("tensor", "q_vector", vv)
                        for sub in range(4):
                            tb = p[2 + sub % 2]
                            pb.wait("tensor", "q_vector", pb.tick("vector"))
                            tv = pb.op("tensor", lambda E, sl=sl, sub=sub, tb=tb:
                                       E.transpose(tb[:, 0:128],
                                                   ev1[:, sl, sub * 128:(sub + 1) * 128],
                                                   idn[:]))
                            pb.wait("vector", "q_tensor", tv)
                            pb.wait("vector", "qkst_s%d" % (sub % 2),
                                    pb.cur("qkst_s%d" % (sub % 2)))
                            ev = pb.op("vector", lambda E, sub=sub, tb=tb:
                                       E.tensor_copy(tsb[:, sub % 2, :], tb[:, 0:128]))
                            pb.wait("gpsimd", "q_vector", ev)
                            pb.dma("gpsimd",
                                   qkT_d[ft * 512 + sub * 128:ft * 512 + (sub + 1) * 128,
                                         tc * 128:(tc + 1) * 128],
                                   tsb[:, sub % 2, :], "qkst_s%d" % (sub % 2))
        pb.phase()

        kfm = sbuf(pb, "kfm", (64, N))
        qfm = sbuf(pb, "qfm", (64, NQ))
        vtm = sbuf(pb, "vtm", (128, NC, 64))
        rpk = sbuf(pb, "rpk", (64, NB))
        rpq = sbuf(pb, "rpq", (64, NB))
        vaug = sbuf(pb, "vaug", (65, 64))
        qex_lo = sbuf(pb, "qex_lo", (2, NQ))
        qex_hi = sbuf(pb, "qex_hi", (2, NQ))
        kex_lo = sbuf(pb, "kex_lo", (2, N))
        kex_hi = sbuf(pb, "kex_hi", (2, N))
        dq = sbuf(pb, "dq", (128, NCQ))
        dkr = sbuf(pb, "dkr", (1, N))
        qld = sbuf(pb, "qld", (128, NCQ, NB))
        srow = sbuf(pb, "srow_sc", (128, SW))
        qlt = sbuf(pb, "qlt", (128, NB))
        qcols = sbuf(pb, "qcols", (128, 5))
        bt = sbuf(pb, "bt", (128, NB))
        kbT = sbuf(pb, "kbT", (NB, 128))
        kb = sbuf(pb, "kb", (128, NB))
        tmp5 = sbuf(pb, "tmp5", (128, 512))
        mxs = sbuf(pb, "mxs", (128, 8))
        rss = sbuf(pb, "rss", (128, NJT))
        scc = sbuf(pb, "scc", (128, 8))
        pext = sbuf(pb, "pext", (128, 65))
        ptx = sbuf(pb, "ptx", (65, 128))
        ptile = sbuf(pb, "ptile", (128, 2, 128))
        ctxs = sbuf(pb, "ctxs", (128, 64))
        ctxT = sbuf(pb, "ctxT", (64, 128))

        def cls_tile(i0, j0):
            if j0 + 511 <= i0 - 32:
                return ("lo", None)
            if j0 >= i0 + 159:
                return ("hi", None)
            return ("st", (i0 - j0) // 128)

        for h in range(H):
            pb.phase()
            pb.dma("sync", kfm[:], qkT_d[D + h * 64:D + (h + 1) * 64, :], "hkl")
            pb.dma("sync", qfm[:], qkT_d[h * 64:(h + 1) * 64, q0:q0 + NQ], "hkl")
            pb.dma("sync", vtm[:],
                   vt_d.rearrange("(c p) f -> p c f", p=128)[:, :, h * 64:(h + 1) * 64], "hkl")
            pb.dma("sync", rpk[:], rpek[h], "hkl")
            pb.dma("sync", rpq[:], rpeq[h], "hkl")
            pb.dma("sync", vaug[0:63, :], rpev[h], "hkl")
            pb.dma("sync", vaug[63:64, :], rpev[h, 0:1, :], "hkl")
            lv = pb.dma("sync", vaug[64:65, :], rpev[h, 62:63, :], "hkl")
            pb.wait("tensor", "hkl", lv)
            pb.wait("vector", "hkl", lv)
            for ci in range(NCQ):
                mmv = pb.op("tensor", lambda E, ci=ci:
                            E.matmul(p[5][:, 0:NB], qfm[:, ci * 128:(ci + 1) * 128], rpk[:],
                                     start=True, stop=True))
                pb.wait("vector", "q_tensor", mmv)
                vv = pb.op("vector", lambda E: E.tensor_copy(qlt[:], p[5][:, 0:NB]))
                pb.op("vector", lambda E, ci=ci:
                      E.tensor_scalar(qld[:, ci, :], qlt[:], qlt[:, 0:1], None, ALU.subtract))
                pb.op("vector", lambda E, ci=ci:
                      E.tensor_tensor(dq[:, ci:ci + 1], qlt[:, 62:63], qlt[:, 0:1],
                                      ALU.subtract))
                pb.op("vector", lambda E: E.tensor_copy(qcols[:, 0:1], onescol[:]))
                pb.op("vector", lambda E: E.tensor_copy(qcols[:, 1:2], qlt[:, 0:1]))
                pb.op("vector", lambda E: E.tensor_copy(qcols[:, 2:3], onescol[:]))
                vv = pb.op("vector", lambda E: E.tensor_copy(qcols[:, 3:4], qlt[:, 62:63]))
                pb.wait("tensor", "q_vector", vv)
                tv = pb.op("tensor", lambda E: E.transpose(p[4][0:2, 0:128], qcols[:, 0:2],
                                                           idn[:]))
                tv = pb.op("tensor", lambda E: E.transpose(p[4][0:2, 128:256], qcols[:, 2:4],
                                                           idn[:]))
                pb.wait("vector", "q_tensor", tv)
                pb.op("vector", lambda E, ci=ci:
                      E.tensor_copy(qex_lo[0:2, ci * 128:(ci + 1) * 128], p[4][0:2, 0:128]))
                vv = pb.op("vector", lambda E, ci=ci:
                           E.tensor_copy(qex_hi[0:2, ci * 128:(ci + 1) * 128],
                                         p[4][0:2, 128:256]))
                pb.wait("tensor", "q_vector", vv)
            for cj in range(NC):
                mmv = pb.op("tensor", lambda E, cj=cj:
                            E.matmul(p[5][:, 0:NB], kfm[:, cj * 128:(cj + 1) * 128], rpq[:],
                                     start=True, stop=True))
                pb.wait("vector", "q_tensor", mmv)
                pb.wait("vector", "kldst", pb.cur("kldst"))
                vv = pb.op("vector", lambda E: E.tensor_scalar(qlt[:], p[5][:, 0:NB], 0.125,
                                                               None, ALU.mult))
                pb.op("vector", lambda E: E.tensor_scalar(bt[:], qlt[:], qlt[:, 0:1], None,
                                                          ALU.subtract))
                pb.op("vector", lambda E: E.tensor_copy(qcols[:, 0:1], qlt[:, 0:1]))
                pb.op("vector", lambda E: E.tensor_copy(qcols[:, 1:2], onescol[:]))
                pb.op("vector", lambda E: E.tensor_copy(qcols[:, 2:3], qlt[:, 62:63]))
                pb.op("vector", lambda E: E.tensor_copy(qcols[:, 3:4], onescol[:]))
                vv = pb.op("vector", lambda E: E.tensor_tensor(qcols[:, 4:5], qlt[:, 62:63],
                                                               qlt[:, 0:1], ALU.subtract))
                pb.wait("tensor", "q_vector", vv)
                tv = pb.op("tensor", lambda E: E.transpose(p[4][0:63, 0:128], bt[:], idn[:]))
                tv = pb.op("tensor", lambda E: E.transpose(p[4][0:2, 128:256], qcols[:, 0:2],
                                                           idn[:]))
                tv = pb.op("tensor", lambda E: E.transpose(p[4][0:2, 256:384], qcols[:, 2:4],
                                                           idn[:]))
                tv = pb.op("tensor", lambda E: E.transpose(p[4][0:1, 384:512], qcols[:, 4:5],
                                                           idn[:]))
                pb.wait("vector", "q_tensor", tv)
                vv = pb.op("vector", lambda E: E.tensor_copy(kbT[:, :], p[4][0:63, 0:128]))
                pb.op("vector", lambda E, cj=cj:
                      E.tensor_copy(kex_lo[0:2, cj * 128:(cj + 1) * 128], p[4][0:2, 128:256]))
                pb.op("vector", lambda E, cj=cj:
                      E.tensor_copy(kex_hi[0:2, cj * 128:(cj + 1) * 128], p[4][0:2, 256:384]))
                vv = pb.op("vector", lambda E, cj=cj:
                           E.tensor_copy(dkr[0:1, cj * 128:(cj + 1) * 128], p[4][0:1, 384:512]))
                pb.wait("gpsimd", "q_vector", vv)
                pb.wait("tensor", "q_vector", vv)
                pb.dma("gpsimd", kldbm_d[:, 64 + cj * 128:64 + (cj + 1) * 128], kbT[:, :],
                       "kldst")
            for ci in range(NCQ):
                i0 = q0 + ci * 128
                kinds = [cls_tile(i0, jt * 512) for jt in range(NJT)]
                pb.wait("vector", "q_tensor", pb.tick("tensor"))
                for jt in range(NJT):
                    kind, s = kinds[jt]
                    bank = p[jt % 2]
                    qx = qex_hi if kind == "hi" else qex_lo
                    kx = kex_hi if kind == "hi" else kex_lo
                    pb.op("tensor", lambda E, jt=jt, ci=ci, bank=bank:
                          E.matmul(bank[:, 0:512], qfm[:, ci * 128:(ci + 1) * 128],
                                   kfm[:, jt * 512:(jt + 1) * 512], start=True, stop=False))
                    mmv = pb.op("tensor", lambda E, jt=jt, ci=ci, bank=bank, qx=qx, kx=kx:
                                E.matmul(bank[:, 0:512], qx[0:2, ci * 128:(ci + 1) * 128],
                                         kx[0:2, jt * 512:(jt + 1) * 512], start=False,
                                         stop=True))
                    pb.wait("vector", "q_tensor", mmv)
                    vv = pb.op("vector", lambda E, jt=jt, bank=bank:
                               E.tensor_copy(srow[:, 64 + jt * 512:64 + (jt + 1) * 512],
                                             bank[:, 0:512]))
                for jt in range(NJT):
                    kind, s = kinds[jt]
                    if kind != "st" or 32 + 128 * s > 511:
                        continue
                    sl = slice(64 + jt * 512, 64 + (jt + 1) * 512)
                    pb.op("vector", lambda E, sl=sl, s=s, ci=ci:
                          E.scalar_tensor_tensor(srow[:, sl], Tm[s][:], dq[:, ci:ci + 1],
                                                 srow[:, sl], ALU.mult, ALU.add))
                    pb.wait("tensor", "q_vector", pb.tick("vector"))
                    mmv = pb.op("tensor", lambda E, jt=jt:
                                E.matmul(p[5][:, 0:512], ones1[:, 0:128],
                                         dkr[0:1, jt * 512:(jt + 1) * 512], start=True,
                                         stop=True))
                    pb.wait("vector", "q_tensor", mmv)
                    pb.op("vector", lambda E, s=s: E.tensor_tensor(tmp5[:], p[5][:, 0:512],
                                                                   Tm[s][:], ALU.mult))
                    pb.op("vector", lambda E, sl=sl: E.tensor_tensor(srow[:, sl], srow[:, sl],
                                                                     tmp5[:], ALU.add))
                vv = pb.tick("vector")
                pb.wait("sync", "q_vector", vv)
                pb.wait("sync", "bre", pb.cur("bre"))
                pb.wait("sync", "pxd", pb.cur("pxd"))
                bw = pb.dma("sync", band_d[:, :], srow[:, i0:i0 + 256], "bwd")
                pb.wait("sync", "bwd", bw)
                sh_r = bass.AP(tensor=band_d.tensor, offset=33, ap=[[257, 128], [1, 63]])
                br = pb.dma("sync", bt[:], sh_r, "brd")
                sh_k = bass.AP(tensor=kldbm_d.tensor, offset=33 + i0,
                               ap=[[N + 129, 63], [1, 128]])
                pb.wait("sync", "q_vector", pb.tick("vector"))
                pb.wait("sync", "kldst", pb.cur("kldst"))
                kl = pb.dma("sync", kbT[:, :], sh_k, "kbl")
                pb.wait("tensor", "kbl", kl)
                pb.wait("tensor", "q_vector", pb.tick("vector"))
                tv = pb.op("tensor", lambda E: E.transpose(p[4][:, 0:63], kbT[:, :],
                                                           idn[0:63, 0:63]))
                pb.wait("vector", "q_tensor", tv)
                pb.op("vector", lambda E: E.tensor_copy(kb[:], p[4][:, 0:63]))
                pb.wait("vector", "brd", br)
                pb.op("vector", lambda E, ci=ci: E.tensor_tensor(bt[:], bt[:], qld[:, ci, :],
                                                                 ALU.add))
                vv = pb.op("vector", lambda E: E.tensor_tensor(bt[:], bt[:], kb[:], ALU.add))
                pb.wait("sync", "q_vector", vv)
                sh_w = bass.AP(tensor=band_d.tensor, offset=33, ap=[[257, 128], [1, 63]])
                bwr = pb.dma("sync", sh_w, bt[:], "bwr")
                pb.wait("sync", "bwr", bwr)
                bre = pb.dma("sync", srow[:, i0:i0 + 256], band_d[:, :], "bre")
                pb.wait("vector", "bre", bre)
                pb.wait("scalar", "bre", bre)
                for jt in range(NJT):
                    pb.op("vector", lambda E, jt=jt:
                          E.reduce_max(mxs[:, jt:jt + 1],
                                       srow[:, 64 + jt * 512:64 + (jt + 1) * 512], axis=AX.X))
                pb.op("vector", lambda E: E.reduce_max(scc[:, 0:1], mxs[:, 0:NJT], axis=AX.X))
                mv = pb.op("vector", lambda E: E.tensor_scalar(scc[:, 1:2], scc[:, 0:1], -1.0,
                                                               None, ALU.mult))
                pb.wait("scalar", "q_vector", mv)
                for jt in range(NJT):
                    sv = pb.op("scalar", lambda E, jt=jt:
                               E.activation(srow[:, 64 + jt * 512:64 + (jt + 1) * 512],
                                            srow[:, 64 + jt * 512:64 + (jt + 1) * 512],
                                            AF.Exp, bias=scc[:, 1:2],
                                            accum_out=rss[:, jt:jt + 1]))
                pb.wait("vector", "q_scalar", sv)
                pb.op("vector", lambda E: E.reduce_sum(scc[:, 2:3], rss[:, 0:NJT], axis=AX.X))
                rv = pb.op("vector", lambda E: E.reciprocal(scc[:, 3:4], scc[:, 2:3]))
                if i0 == 0:
                    rv = pb.op("vector", lambda E: E.memset(srow[:, 0:64], 0.0))
                if i0 == N - 128:
                    rv = pb.op("vector", lambda E: E.memset(srow[:, 64 + N:64 + N + 64], 0.0))
                pb.op("vector", lambda E: E.memset(scc[:, 4:5], 0.0))
                pb.op("vector", lambda E: E.memset(scc[:, 5:6], 0.0))
                for jt in range(NJT):
                    kind, s = kinds[jt]
                    if kind == "lo":
                        pb.op("vector", lambda E, jt=jt:
                              E.tensor_tensor(scc[:, 4:5], scc[:, 4:5], rss[:, jt:jt + 1],
                                              ALU.add))
                    elif kind == "hi":
                        pb.op("vector", lambda E, jt=jt:
                              E.tensor_tensor(scc[:, 5:6], scc[:, 5:6], rss[:, jt:jt + 1],
                                              ALU.add))
                    else:
                        sl = slice(64 + jt * 512, 64 + (jt + 1) * 512)
                        if -32 + 128 * s >= -127:
                            pb.op("vector", lambda E, sl=sl, s=s:
                                  E.tensor_tensor(tmp5[:], srow[:, sl], Tl[s][:], ALU.mult))
                            pb.op("vector", lambda E: E.reduce_sum(scc[:, 6:7], tmp5[:],
                                                                   axis=AX.X))
                            pb.op("vector", lambda E: E.tensor_tensor(scc[:, 4:5], scc[:, 4:5],
                                                                      scc[:, 6:7], ALU.add))
                        if 32 + 128 * s <= 511:
                            pb.op("vector", lambda E, sl=sl, s=s:
                                  E.tensor_tensor(tmp5[:], srow[:, sl], Tm[s][:], ALU.mult))
                            pb.op("vector", lambda E: E.reduce_sum(scc[:, 6:7], tmp5[:],
                                                                   axis=AX.X))
                            pb.op("vector", lambda E: E.tensor_tensor(scc[:, 5:6], scc[:, 5:6],
                                                                      scc[:, 6:7], ALU.add))
                vv = pb.tick("vector")
                pb.wait("sync", "q_vector", vv)
                pb.wait("sync", "q_scalar", pb.tick("scalar"))
                px = pb.dma("sync", band_d[:, :], srow[:, i0:i0 + 256], "pxd")
                pb.wait("sync", "pxd", px)
                pxr = pb.dma("sync", pext[:, 0:63], sh_r, "pxr")
                pb.wait("vector", "pxr", pxr)
                pb.op("vector", lambda E: E.tensor_copy(pext[:, 63:64], scc[:, 4:5]))
                vv = pb.op("vector", lambda E: E.tensor_copy(pext[:, 64:65], scc[:, 5:6]))
                pb.wait("tensor", "q_vector", vv)
                tv = pb.op("tensor", lambda E: E.transpose(p[4][0:65, 0:128], pext[:], idn[:]))
                pb.wait("vector", "q_tensor", tv)
                vv = pb.op("vector", lambda E: E.tensor_copy(ptx[:, :], p[4][0:65, 0:128]))
                pb.wait("tensor", "q_vector", vv)
                for jc in range(NC):
                    tb = p[2 + jc % 2]
                    tv = pb.op("tensor", lambda E, jc=jc, tb=tb:
                               E.transpose(tb[:, 0:128], srow[:, 64 + jc * 128:64 + (jc + 1) * 128],
                                           idn[:]))
                    pb.wait("vector", "q_tensor", tv)
                    ev = pb.op("vector", lambda E, jc=jc, tb=tb:
                               E.tensor_copy(ptile[:, jc % 2, :], tb[:, 0:128]))
                    pb.wait("tensor", "q_vector", ev)
                    pb.op("tensor", lambda E, jc=jc:
                          E.matmul(p[7][:, 0:64], ptile[:, jc % 2, :], vtm[:, jc, :],
                                   start=(jc == 0), stop=False))
                mmv = pb.op("tensor", lambda E: E.matmul(p[7][:, 0:64], ptx[:, :], vaug[:],
                                                         start=False, stop=True))
                pb.wait("vector", "q_tensor", mmv)
                vv = pb.op("vector", lambda E: E.tensor_scalar(ctxs[:], p[7][:, 0:64],
                                                               scc[:, 3:4], None, ALU.mult))
                pb.wait("tensor", "q_vector", vv)
                tv = pb.op("tensor", lambda E: E.transpose(p[4][0:64, 0:128], ctxs[:], idn[:]))
                pb.wait("vector", "q_tensor", tv)
                pb.wait("vector", "ctxst", pb.cur("ctxst"))
                vv = pb.op("vector", lambda E: E.tensor_copy(ctxT[:, :], p[4][0:64, 0:128]))
                pb.wait("gpsimd", "q_vector", vv)
                pb.dma("gpsimd", ctx_d[h * 64:(h + 1) * 64, ci * 128:(ci + 1) * 128],
                       ctxT[:, :], "ctxst")
        pb.phase()

        with pb.scope():
            wpsl = sbuf(pb, "p3w", (128, 8, D))
            cxsl = sbuf(pb, "p3c", (128, 2, 8, 128))
            x4c = sbuf(pb, "p3x", (128, 2, D))
            yo = sbuf(pb, "p3y", (128, 2, 512))
            bprow = sbuf(pb, "p3br", (1, D))
            pb.dma("sync", wpsl[:], wprojT.rearrange("(c p) o -> p c o", p=128), "p3wl")
            lv = pb.dma("sync", bprow[:], bproj.rearrange("(a o) -> a o", a=1), "p3wl")
            pb.wait("tensor", "p3wl", lv)
            pb.wait("vector", "p3wl", lv)
            bpb, _ = bcast_row(pb, "p3bb", bprow[:], D, 0, p[6], ones1)
            for tc in range(NCQ):
                sl = tc % 2
                pb.wait("sync", "q_tensor", pb.tick("tensor"))
                pb.wait("sync", "q_vector", pb.tick("vector"))
                pb.dma("sync", cxsl[:, sl, :, :],
                       ctx_d.rearrange("(c p) n -> p c n", p=128)[:, :, tc * 128:(tc + 1) * 128],
                       "p3cl_s%d" % sl)
                lv = pb.dma("sync", x4c[:, sl, :], x4_d[q0 + tc * 128:q0 + (tc + 1) * 128, :],
                            "p3xl_s%d" % sl)
                pb.wait("tensor", "p3cl_s%d" % sl, pb.cur("p3cl_s%d" % sl))
                pb.wait("vector", "p3xl_s%d" % sl, lv)
                for ot in range(2):
                    bank = p[4 + ot]
                    for k in range(8):
                        mmv = pb.op("tensor", lambda E, k=k, sl=sl, ot=ot, bank=bank:
                                    E.matmul(bank[:, 0:512], cxsl[:, sl, k, :],
                                             wpsl[:, k, ot * 512:(ot + 1) * 512],
                                             start=(k == 0), stop=(k == 7)))
                    pb.wait("vector", "q_tensor", mmv)
                    pb.wait("vector", "yst_s%d" % ot, pb.cur("yst_s%d" % ot))
                    pb.op("vector", lambda E, sl=sl, ot=ot, bank=bank:
                          E.tensor_tensor(yo[:, ot % 2, :], bank[:, 0:512],
                                          x4c[:, sl, ot * 512:(ot + 1) * 512], ALU.add))
                    vv = pb.op("vector", lambda E, sl=sl, ot=ot:
                               E.tensor_tensor(yo[:, ot % 2, :], yo[:, ot % 2, :],
                                               bpb[:, ot * 512:(ot + 1) * 512], ALU.add))
                    pb.wait("gpsimd", "q_vector", vv)
                    pb.dma("gpsimd", y[tc * 128:(tc + 1) * 128, ot * 512:(ot + 1) * 512],
                           yo[:, ot % 2, :], "yst_s%d" % ot)
        pb.wait("gpsimd", "yst_s0", pb.cur("yst_s0"))
        pb.wait("gpsimd", "yst_s1", pb.cur("yst_s1"))
        pb.build()
    return nc


def build_add3(T):
    nc = bass.Bass("TRN2", target_bir_lowering=False, debug=False)
    a = nc.dram_tensor("a", [T, D], F32, kind="ExternalInput").ap()
    b = nc.dram_tensor("b", [T, D], F32, kind="ExternalInput").ap()
    c = nc.dram_tensor("c", [T, D], F32, kind="ExternalInput").ap()
    y = nc.dram_tensor("y", [T, D], F32, kind="ExternalOutput").ap()
    est = contextlib.ExitStack()
    with est:
        pb = PB(nc, est)
        xc = sbuf(pb, "xc", (128, 2, D))
        x2 = sbuf(pb, "x2", (128, 2, D))
        for ch in range(T // 128):
            r0 = ch * 128
            sl = ch % 2
            pb.wait("sync", "q_vector", pb.tick("vector"))
            pb.wait("sync", "yst_s%d" % sl, pb.cur("yst_s%d" % sl))
            l1 = pb.dma("sync", xc[:, sl, :], a[r0:r0 + 128, :], "al_s%d" % sl)
            l2 = pb.dma("sync", x2[:, sl, :], b[r0:r0 + 128, :], "bl_s%d" % sl)
            pb.wait("vector", "al_s%d" % sl, l1)
            pb.wait("vector", "bl_s%d" % sl, l2)
            a1 = pb.op("vector", lambda E, sl=sl: E.tensor_tensor(xc[:, sl, :], xc[:, sl, :],
                                                                  x2[:, sl, :], ALU.add))
            pb.wait("sync", "q_vector", a1)
            l3 = pb.dma("sync", x2[:, sl, :], c[r0:r0 + 128, :], "bl_s%d" % sl)
            pb.wait("vector", "bl_s%d" % sl, l3)
            vv = pb.op("vector", lambda E, sl=sl: E.tensor_tensor(xc[:, sl, :], xc[:, sl, :],
                                                                  x2[:, sl, :], ALU.add))
            pb.wait("gpsimd", "q_vector", vv)
            pb.dma("gpsimd", y[r0:r0 + 128, :], xc[:, sl, :], "yst_s%d" % sl)
        pb.wait("gpsimd", "yst_s0", pb.cur("yst_s0"))
        pb.wait("gpsimd", "yst_s1", pb.cur("yst_s1"))
        pb.build()
    return nc


_CACHE = {}


def _get(key, builder, *args):
    if key not in _CACHE:
        _CACHE[key] = builder(*args)
    return _CACHE[key]


def run_l0ffn(x_slices, w):
    T = x_slices[0].shape[0]
    nc = _get(("l0ffn", T), build_l0ffn, T)
    in_maps = [{"x": np.ascontiguousarray(xs),
                "U1": w["L0_ffn_U1"], "V1": w["L0_ffn_V1"], "b1": w["L0_ffn_b1"],
                "U2": w["L0_ffn_U2"], "V2": w["L0_ffn_V2"], "b2": w["L0_ffn_b2"]}
               for xs in x_slices]
    res = run_bass_kernel_spmd(nc, in_maps, core_ids=list(range(len(in_maps))))
    return [r["y"] for r in res.results]


def _l0attn_map(xslice, w):
    return {"x": np.ascontiguousarray(xslice),
            "Wq": w["L0_attn_Wq"], "Wk": w["L0_attn_Wk"], "Wv": w["L0_attn_Wv"],
            "Wo": w["L0_attn_Wo"], "W1": w["L0_attn_W1"], "W2": w["L0_attn_W2"],
            "bq": w["L0_attn_bq"], "bk": w["L0_attn_bk"], "bv": w["L0_attn_bv"],
            "bo": w["L0_attn_bo"], "b1": w["L0_attn_b1"], "b2": w["L0_attn_b2"],
            "lnp": np.stack([w["L0_attn_ln1_w"], w["L0_attn_ln1_b"],
                             w["L0_attn_ln2_w"], w["L0_attn_ln2_b"]])}


def run_l0attn(x1, w):
    nc0 = _get(("l0attn", 0), build_l0attn, HALF, 0)
    nc1 = _get(("l0attn", 128), build_l0attn, HALF, 128)
    m0 = [_l0attn_map(x1[b, 0:HALF], w) for b in range(B)]
    m1 = [_l0attn_map(x1[b, HALF - 128:SEQ], w) for b in range(B)]
    r0 = run_bass_kernel_spmd(nc0, m0, core_ids=[0, 1, 2, 3])
    r1 = run_bass_kernel_spmd(nc1, m1, core_ids=[4, 5, 6, 7])
    x2 = np.empty_like(x1)
    for b in range(B):
        x2[b, 0:HALF] = r0.results[b]["y"]
        x2[b, HALF:SEQ] = r1.results[b]["y"]
    return x2


_MOE_PERM = {0: [0, 1, 2, 3, 4, 5, 6, 7], 1: [4, 5, 6, 7, 0, 1, 2, 3]}


def run_moe(x, w):
    nc = _get(("moe",), build_moe)
    maps = []
    for b in range(B):
        for eg in range(2):
            sl = slice(0, 4) if eg == 0 else slice(4, 8)
            maps.append({"x": np.ascontiguousarray(x[b]),
                         "gw": np.ascontiguousarray(w["L1_ffn_gate_w"][:, _MOE_PERM[eg]]),
                         "Wg": np.ascontiguousarray(w["L1_ffn_Wg"][sl]),
                         "Wu": np.ascontiguousarray(w["L1_ffn_Wu"][sl]),
                         "Wd": np.ascontiguousarray(w["L1_ffn_Wd"][sl])})
    res = run_bass_kernel_spmd(nc, maps, core_ids=list(range(8)))
    return [[res.results[b * 2 + eg]["y"] for eg in range(2)] for b in range(B)]


def run_l1attn(x3, parts, w):
    wqkvT = np.ascontiguousarray(w["L1_attn_w_qkv"].T)
    wprojT = np.ascontiguousarray(w["L1_attn_w_proj"].T)
    rpeqR = np.ascontiguousarray(w["L1_attn_rpe_q_table"][:, :, ::-1])
    x5 = np.empty_like(x3)
    for hi, q0 in enumerate((0, HALF)):
        nc = _get(("l1attn", q0), build_l1attn, SEQ, HALF, q0)
        maps = [{"x3": np.ascontiguousarray(x3[b]),
                 "pa": np.ascontiguousarray(parts[b][0]),
                 "pb": np.ascontiguousarray(parts[b][1]),
                 "wqkvT": wqkvT, "wprojT": wprojT, "bproj": w["L1_attn_b_proj"],
                 "rpeq": rpeqR, "rpek": w["L1_attn_rpe_k_table"],
                 "rpev": w["L1_attn_rpe_v_table"]} for b in range(B)]
        res = run_bass_kernel_spmd(nc, maps, core_ids=[0 + 4 * hi, 1 + 4 * hi,
                                                       2 + 4 * hi, 3 + 4 * hi])
        for b in range(B):
            x5[b, q0:q0 + HALF] = res.results[b]["y"]
    return x5


def run_add3(x5, parts):
    nc = _get(("add3",), build_add3, HALF)
    maps = []
    for b in range(B):
        for h in range(2):
            sl = slice(h * HALF, (h + 1) * HALF)
            maps.append({"a": np.ascontiguousarray(x5[b, sl]),
                         "b": np.ascontiguousarray(parts[b][0][sl]),
                         "c": np.ascontiguousarray(parts[b][1][sl])})
    res = run_bass_kernel_spmd(nc, maps, core_ids=list(range(8)))
    out = np.empty_like(x5)
    for b in range(B):
        for h in range(2):
            out[b, h * HALF:(h + 1) * HALF] = res.results[b * 2 + h]["y"]
    return out


def kernel(**inputs):
    w = inputs
    x0 = np.asarray(w["L0_attn_x"], dtype=np.float32)
    xf = x0.reshape(-1, D)
    ys = run_l0ffn([xf[i * HALF:(i + 1) * HALF] for i in range(8)], w)
    x1 = np.concatenate(ys, 0).reshape(B, SEQ, D)
    x2 = run_l0attn(x1, w)
    ys = run_l0ffn([x2.reshape(-1, D)[i * HALF:(i + 1) * HALF] for i in range(8)], w)
    x3 = np.concatenate(ys, 0).reshape(B, SEQ, D)
    parts1 = run_moe(x3, w)
    x5 = run_l1attn(x3, parts1, w)
    parts2 = run_moe(x5, w)
    x6 = run_add3(x5, parts2)
    return x6.astype(np.float32)


# version 02
# speedup vs baseline: 1.4937x; 1.4937x over previous
import math
import contextlib
import numpy as np
import concourse.bass as bass
import concourse.mybir as mybir
from concourse.bass_utils import run_bass_kernel_spmd

F32 = mybir.dt.float32
I32 = mybir.dt.int32
U32 = mybir.dt.uint32
BF16 = mybir.dt.bfloat16
AF = mybir.ActivationFunctionType
ALU = mybir.AluOpType
AX = mybir.AxisListType

D = 1024
SEQ = 4096
B = 4
HALF = 2048
WIN = 128
NH = 16
HD = 64
DFF = 4096
RANK = 256
NE = 8
NI = 2048
KSEL = 1024
NB = 63
LNEPS = float(1e-5)
NEG = float(-1e30)


class PB:
    ENGINES = ("sync", "scalar", "vector", "tensor", "gpsimd")
    SELF_SERIAL = ("scalar", "vector", "gpsimd")

    def __init__(self, nc, est):
        self.nc = nc
        self.ops = {e: [] for e in self.ENGINES}
        self.H = {}
        self.V = {}
        self.est = est
        self.scopes = [est]
        self.prefix = ""

    @contextlib.contextmanager
    def scope(self):
        st = contextlib.ExitStack()
        self.scopes.append(st)
        try:
            yield st
        finally:
            self.scopes.pop()
            st.close()

    def sem(self, name):
        if name not in self.H:
            self.H[name] = self.est.enter_context(self.nc.semaphore(name))
            self.V[name] = 0
        return name

    def bump(self, name, amt):
        self.sem(name)
        self.V[name] += amt
        return self.V[name]

    def cur(self, name):
        self.sem(name)
        return self.V[name]

    def emit(self, eng, fn):
        self.ops[eng].append(fn)

    def wait(self, eng, name, val=None):
        v = self.cur(name) if val is None else val
        h = self.H[name]
        if v > 0:
            self.emit(eng, lambda E, h=h, v=v: E.wait_ge(h, v))

    def op(self, eng, fn, chain=True):
        q = "q_" + eng
        if eng in self.SELF_SERIAL and chain and self.cur(q) > 0:
            self.wait(eng, q, self.cur(q))
        v = self.bump(q, 1)
        h = self.H[q]
        self.emit(eng, lambda E, fn=fn, h=h: fn(E).then_inc(h, 1))
        return v

    def tick(self, eng):
        return self.cur("q_" + eng)

    def phase(self, engines=None, dma_sems=True):
        engines = engines or self.ENGINES
        ticks = {e: self.cur("q_" + e) for e in self.ENGINES}
        dmas = [(n, self.V[n]) for n in list(self.H)
                if not n.startswith("q_") and self.V[n] > 0] if dma_sems else []
        for e in engines:
            for o in self.ENGINES:
                if o != e and ticks[o] > 0:
                    self.wait(e, "q_" + o, ticks[o])
            for n, v in dmas:
                self.wait(e, n, v)

    def dma(self, eng, out, in_, sem):
        v = self.bump(sem, 16)
        h = self.H[sem]
        self.emit(eng, lambda E, o=out, i=in_, h=h: E.dma_start(out=o, in_=i).then_inc(h, 16))
        return v

    def build(self):
        with self.nc.Block() as blk:
            for eng in self.ENGINES:
                ops = self.ops[eng]
                if not ops:
                    continue
                dec = getattr(blk, eng)

                def runner(E, ops=ops):
                    for f in ops:
                        f(E)
                dec(runner)


def sbuf(pb, name, shape, dt=F32):
    return pb.scopes[-1].enter_context(pb.nc.sbuf_tensor(pb.prefix + name, list(shape), dt))


def psum8(pb):
    return [pb.est.enter_context(pb.nc.psum_tensor("psum%d" % i, [128, 512], F32))
            for i in range(8)]


def make_identity(pb):
    di = sbuf(pb, "idn_i", (128, 128), I32)
    idn = sbuf(pb, "idn_f", (128, 128), F32)
    gv = pb.op("gpsimd", lambda E, di=di: E.iota(di[:], [[-1, 128]], base=0, channel_multiplier=1))
    pb.wait("vector", "q_gpsimd", gv)
    vv = pb.op("vector", lambda E, di=di, idn=idn: E.tensor_scalar(idn[:], di[:], 0, None, ALU.is_equal))
    pb.idn_ready = vv
    return idn


def load_bias_cols(pb, name, dram_vec, nch, idn, bank):
    tmp = sbuf(pb, name + "_r", (nch, 128))
    out = sbuf(pb, name, (128, nch))
    lv = pb.dma("sync", tmp[:], dram_vec.rearrange("(c p) -> c p", p=128), name + "_ld")
    pb.wait("tensor", name + "_ld", lv)
    pb.wait("tensor", "q_vector", pb.tick("vector"))
    tv = pb.op("tensor", lambda E: E.transpose(bank[0:128, 0:nch], tmp[:], idn[0:nch, 0:nch]))
    pb.wait("vector", "q_tensor", tv)
    vv = pb.op("vector", lambda E: E.tensor_copy(out[:], bank[:, 0:nch]))
    return out, vv


def emit_ln_chunk(pb, x_chunk, h_chunk, stats, sq_scratch, dim=D):
    mu = stats[:, 0:1]
    sq = stats[:, 1:2]
    rstd = stats[:, 2:3]
    nbias = stats[:, 3:4]
    var = stats[:, 4:5]
    pb.wait("scalar", "q_vector", pb.tick("vector"))
    pb.wait("vector", "q_scalar", pb.tick("scalar"))
    pb.op("vector", lambda E: E.reduce_sum(mu, x_chunk, axis=AX.X))
    pb.op("vector", lambda E: E.tensor_scalar(mu, mu, 1.0 / dim, None, ALU.mult))
    sv = pb.op("scalar", lambda E: E.activation(sq_scratch[:], x_chunk, AF.Square, accum_out=sq))
    pb.wait("vector", "q_scalar", sv)
    pb.op("vector", lambda E: E.tensor_scalar(var, sq, 1.0 / dim, None, ALU.mult))
    pb.op("vector", lambda E: E.tensor_tensor(sq, mu, mu, ALU.mult))
    pb.op("vector", lambda E: E.tensor_tensor(var, var, sq, ALU.subtract))
    vv = pb.op("vector", lambda E: E.tensor_scalar(var, var, LNEPS, None, ALU.add))
    pb.wait("scalar", "q_vector", vv)
    sv = pb.op("scalar", lambda E: E.activation(rstd, var, AF.Sqrt))
    pb.wait("vector", "q_scalar", sv)
    pb.op("vector", lambda E: E.reciprocal(rstd, rstd))
    pb.op("vector", lambda E: E.tensor_tensor(nbias, mu, rstd, ALU.mult))
    pb.op("vector", lambda E: E.tensor_scalar(nbias, nbias, -1.0, None, ALU.mult))
    return pb.op("vector", lambda E: E.tensor_scalar(h_chunk, x_chunk, rstd, nbias, ALU.mult, ALU.add))


def emit_l0ffn(pb, p, idn, x_in, U1, V1, b1, U2, V2, b2, y, T, tag=""):
    nc = pb.nc
    _oldpfx = pb.prefix
    pb.prefix = _oldpfx + tag
    with pb.scope():
        u1 = sbuf(pb, "u1", (128, 8, RANK))
        v1 = sbuf(pb, "v1", (128, 2, DFF))
        u2 = sbuf(pb, "u2", (128, 32, RANK))
        v2 = sbuf(pb, "v2", (128, 2, D))
        pb.dma("sync", u1[:], U1.rearrange("(c p) r -> p c r", p=128), "wld")
        pb.dma("sync", v1[:], V1.rearrange("(c p) f -> p c f", p=128), "wld")
        pb.dma("sync", u2[:], U2.rearrange("(c p) r -> p c r", p=128), "wld")
        pb.dma("sync", v2[:], V2.rearrange("(c p) o -> p c o", p=128), "wld")
        b1t, b1v = load_bias_cols(pb, "b1t", b1, 32, idn, p[7])
        b2t, b2v = load_bias_cols(pb, "b2t", b2, 8, idn, p[6])
        pb.wait("tensor", "q_vector", b2v)
        pb.wait("scalar", "q_vector", b2v)
        wld = pb.cur("wld")

        xc = sbuf(pb, "xc", (128, 4, D))
        hc = sbuf(pb, "hc", (128, 4, D))
        stats = sbuf(pb, "stats", (128, 8))
        sqs = sbuf(pb, "sqs", (128, D))
        hT = sbuf(pb, "hT", (128, 8, 512))
        Pfm = sbuf(pb, "Pfm", (128, 2, 512))
        Hc = sbuf(pb, "Hc", (128, 2, 512))
        Qfm = sbuf(pb, "Qfm", (128, 2, 512))
        dfm = sbuf(pb, "dfm", (128, 8, 512))
        oc = sbuf(pb, "oc", (128, 4, D))

        n_blk = (T + 511) // 512
        oev_last = 0
        hv = 0
        for blk_i in range(n_blk):
            t0 = blk_i * 512
            bt = min(512, T - t0)
            nch = bt // 128
            pb.wait("sync", "q_vector", oev_last)
            pb.wait("tensor", "q_vector", oev_last)
            for c in range(nch):
                r0 = t0 + c * 128
                lv = pb.dma("sync", xc[:, c, :], x_in[r0:r0 + 128, :], "xld_s%d" % c)
                pb.wait("vector", "xld_s%d" % c, lv)
                pb.wait("scalar", "xld_s%d" % c, lv)
                hv = emit_ln_chunk(pb, xc[:, c, :], hc[:, c, :], stats, sqs)
                pb.wait("tensor", "q_vector", hv)
                for dch in range(8):
                    bank = p[dch % 2]
                    tv = pb.op("tensor", lambda E, bank=bank, c=c, dch=dch:
                               E.transpose(bank[:, 0:128], hc[:, c, dch * 128:(dch + 1) * 128], idn[:]))
                    pb.wait("vector", "q_tensor", tv)
                    ev = pb.op("vector", lambda E, bank=bank, c=c, dch=dch:
                               E.tensor_copy(hT[:, dch, c * 128:(c + 1) * 128], bank[:, 0:128]))
                    pb.wait("tensor", "q_vector", ev)
            pb.wait("tensor", "wld", wld)
            mmP = 0
            for rch in range(2):
                bank = p[2 + rch]
                for k in range(8):
                    mmP = pb.op("tensor", lambda E, r=rch, k=k, bank=bank, bt=bt:
                                E.matmul(bank[:, 0:bt], u1[:, k, r * 128:(r + 1) * 128],
                                         hT[:, k, 0:bt], start=(k == 0), stop=(k == 7)))
            pb.wait("vector", "q_tensor", mmP)
            pev = 0
            for rch in range(2):
                pev = pb.op("vector", lambda E, r=rch, bank=p[2 + rch], bt=bt:
                            E.tensor_copy(Pfm[:, r, 0:bt], bank[:, 0:bt]))
            pb.wait("tensor", "q_vector", pev)
            gel = {}
            mmQ = {}
            for f in range(32):
                bank = p[f % 2]
                mmH = 0
                for rch in range(2):
                    mmH = pb.op("tensor", lambda E, f=f, r=rch, bank=bank, bt=bt:
                                E.matmul(bank[:, 0:bt], v1[:, r, f * 128:(f + 1) * 128],
                                         Pfm[:, r, 0:bt], start=(r == 0), stop=(r == 1)))
                pb.wait("scalar", "q_tensor", mmH)
                if f >= 2:
                    pb.wait("scalar", "q_tensor", mmQ[f - 2])
                hbuf = Hc[:, f % 2, 0:bt]
                gel[f] = pb.op("scalar", lambda E, f=f, bank=bank, hb=hbuf, bt=bt:
                               E.activation(hb, bank[:, 0:bt], AF.Gelu_apprx_tanh,
                                            bias=b1t[:, f:f + 1]))
                pb.wait("tensor", "q_scalar", gel[f])
                for rch in range(2):
                    bank2 = p[4 + rch]
                    mmQ[f] = pb.op("tensor", lambda E, f=f, r=rch, hb=hbuf, bank2=bank2, bt=bt:
                                   E.matmul(bank2[:, 0:bt], u2[:, f, r * 128:(r + 1) * 128], hb,
                                            start=(f == 0), stop=(f == 31)))
            pb.wait("vector", "q_tensor", mmQ[31])
            qev = 0
            for rch in range(2):
                qev = pb.op("vector", lambda E, r=rch, bank=p[4 + rch], bt=bt:
                            E.tensor_copy(Qfm[:, r, 0:bt], bank[:, 0:bt]))
            pb.wait("tensor", "q_vector", qev)
            dev = {}
            for och in range(8):
                bank = p[6 + och % 2]
                if och >= 2:
                    pb.wait("tensor", "q_vector", dev[och - 2])
                mmD = 0
                for rch in range(2):
                    mmD = pb.op("tensor", lambda E, o=och, r=rch, bank=bank, bt=bt:
                                E.matmul(bank[:, 0:bt], v2[:, r, o * 128:(o + 1) * 128],
                                         Qfm[:, r, 0:bt], start=(r == 0), stop=(r == 1)))
                pb.wait("vector", "q_tensor", mmD)
                dev[och] = pb.op("vector", lambda E, o=och, bank=bank, bt=bt:
                                 E.tensor_scalar(dfm[:, o, 0:bt], bank[:, 0:bt],
                                                 b2t[:, o:o + 1], None, ALU.add))
            pb.wait("gpsimd", "q_vector", hv)
            pb.wait("tensor", "q_vector", dev[7])
            for c in range(nch):
                r0 = t0 + c * 128
                lv2 = pb.dma("gpsimd", xc[:, c, :], x_in[r0:r0 + 128, :], "xld2_s%d" % c)
                pb.wait("tensor", "q_vector", oev_last)
                oev = {}
                for och in range(8):
                    bank = p[och % 2]
                    if och >= 2:
                        pb.wait("tensor", "q_vector", oev[och - 2])
                    tv = pb.op("tensor", lambda E, bank=bank, c=c, o=och:
                               E.transpose(bank[:, 0:128], dfm[:, o, c * 128:(c + 1) * 128], idn[:]))
                    pb.wait("vector", "q_tensor", tv)
                    if och == 0:
                        pb.wait("vector", "xld2_s%d" % c, lv2)
                    oev[och] = pb.op("vector", lambda E, bank=bank, c=c, o=och:
                                     E.scalar_tensor_tensor(oc[:, c, o * 128:(o + 1) * 128],
                                                            bank[:, 0:128], 0.5,
                                                            xc[:, c, o * 128:(o + 1) * 128],
                                                            ALU.mult, ALU.add))
                oev_last = oev[7]
                pb.wait("gpsimd", "q_vector", oev_last)
                pb.dma("gpsimd", y[r0:r0 + 128, :], oc[:, c, :], "yst")
        pb.wait("gpsimd", "yst", pb.cur("yst"))
    pb.prefix = _oldpfx


def build_l0ffn(T):
    nc = bass.Bass("TRN2", target_bir_lowering=False, debug=False)
    x_in = nc.dram_tensor("x", [T, D], F32, kind="ExternalInput").ap()
    U1 = nc.dram_tensor("U1", [D, RANK], F32, kind="ExternalInput").ap()
    V1 = nc.dram_tensor("V1", [RANK, DFF], F32, kind="ExternalInput").ap()
    b1 = nc.dram_tensor("b1", [DFF], F32, kind="ExternalInput").ap()
    U2 = nc.dram_tensor("U2", [DFF, RANK], F32, kind="ExternalInput").ap()
    V2 = nc.dram_tensor("V2", [RANK, D], F32, kind="ExternalInput").ap()
    b2 = nc.dram_tensor("b2", [D], F32, kind="ExternalInput").ap()
    y = nc.dram_tensor("y", [T, D], F32, kind="ExternalOutput").ap()
    est = contextlib.ExitStack()
    with est:
        pb = PB(nc, est)
        idn = make_identity(pb)
        p = psum8(pb)
        emit_l0ffn(pb, p, idn, x_in, U1, V1, b1, U2, V2, b2, y, T)
        pb.build()
    return nc


def bcast_row(pb, name, row_ap, width, idn_ready_tick, bank, ones1):
    out = sbuf(pb, name, (128, width))
    vv = 0
    for c0 in range(0, width, 512):
        w = min(512, width - c0)
        tv = pb.op("tensor", lambda E, c0=c0, w=w:
                   E.matmul(bank[:, 0:w], ones1[:, 0:128], row_ap[:, c0:c0 + w],
                            start=True, stop=True))
        pb.wait("vector", "q_tensor", tv)
        vv = pb.op("vector", lambda E, c0=c0, w=w:
                   E.tensor_copy(out[:, c0:c0 + w], bank[:, 0:w]))
        pb.wait("tensor", "q_vector", vv)
    return out, vv


def emit_l0attn(pb, p, idn, x_in, Wq, Wk, Wv, Wo, W1, W2, bq, bk, bv, bo, b1, b2, lnp, y, T, halo, tag=""):
    nc = pb.nc
    _oldpfx = pb.prefix
    pb.prefix = _oldpfx + tag
    TK = T + halo
    with pb.scope():
        xi_d = nc.dram_tensor(pb.prefix + "xi_d", [TK, D], F32, kind="Internal").ap()
        xmid_d = nc.dram_tensor(pb.prefix + "xmid_d", [T, D], F32, kind="Internal").ap()
        ctx_d = nc.dram_tensor(pb.prefix + "ctx_d", [D, T], F32, kind="Internal").ap()
        hg_d = nc.dram_tensor(pb.prefix + "hg_d", [DFF, T], F32, kind="Internal").ap()
        ones1 = sbuf(pb, "ones1", (1, 128))
        ov = pb.op("vector", lambda E: E.memset(ones1[:], 1.0))
        pb.wait("tensor", "q_vector", ov)
        lnrows = []
        for i in range(4):
            r = sbuf(pb, "lnrow%d" % i, (1, D))
            lv = pb.dma("sync", r[:], lnp[i:i + 1, :], "lnld")
            lnrows.append(r)
        pb.wait("tensor", "lnld", pb.cur("lnld"))
        pb.wait("tensor", "q_vector", pb.idn_ready)
        g1bc, _ = bcast_row(pb, "g1bc", lnrows[0][:, :], D, pb.idn_ready, p[0], ones1)
        b1bc, _ = bcast_row(pb, "b1bc", lnrows[1][:, :], D, pb.idn_ready, p[0], ones1)
        g2bc, _ = bcast_row(pb, "g2bc", lnrows[2][:, :], D, pb.idn_ready, p[0], ones1)
        b2bc, bcv = bcast_row(pb, "b2bc", lnrows[3][:, :], D, pb.idn_ready, p[0], ones1)
        b2row = sbuf(pb, "b2row", (1, D))
        lvb = pb.dma("sync", b2row[:], b2.rearrange("(a d) -> a d", a=1), "b2r_ld")
        pb.wait("tensor", "b2r_ld", lvb)
        b2bc2, bdv2 = bcast_row(pb, "b2bc2", b2row[:, :], D, 0, p[1], ones1)
        bqt, _ = load_bias_cols(pb, "bqt", bq, 8, idn, p[2])
        bkt, _ = load_bias_cols(pb, "bkt", bk, 8, idn, p[2])
        bvt, _ = load_bias_cols(pb, "bvt", bv, 8, idn, p[2])
        bot, _ = load_bias_cols(pb, "bot", bo, 8, idn, p[2])
        bf1t, bv2 = load_bias_cols(pb, "bf1t", b1, 32, idn, p[2])
        pb.wait("vector", "q_vector", bv2)
        mski = sbuf(pb, "mski", (128, 256), I32)
        msk = sbuf(pb, "msk", (128, 256))
        mskt = sbuf(pb, "mskt", (128, 256))
        pb.op("gpsimd", lambda E: E.iota(mski[:, 0:128], [[-1, 128]], base=128,
                                         channel_multiplier=1))
        gv = pb.op("gpsimd", lambda E: E.iota(mski[:, 128:256], [[-1, 128]], base=0,
                                              channel_multiplier=1))
        pb.wait("vector", "q_gpsimd", gv)
        pb.op("vector", lambda E: E.tensor_scalar(msk[:], mski[:], 0, None, ALU.is_lt))
        pb.op("vector", lambda E: E.tensor_scalar(mskt[:], mski[:], 128, None, ALU.is_gt))
        pb.op("vector", lambda E: E.tensor_tensor(msk[:], msk[:], mskt[:], ALU.add))
        mv = pb.op("vector", lambda E: E.tensor_scalar(msk[:], msk[:], NEG, None, ALU.mult))
        pb.wait("scalar", "q_vector", mv)
        xc = sbuf(pb, "xc", (128, D))
        xc2 = sbuf(pb, "xc2", (128, D))
        hc = sbuf(pb, "hc", (128, D))
        stats = sbuf(pb, "stats", (128, 8))
        sm = sbuf(pb, "sm", (128, 4))

        with pb.scope():
            h1T = sbuf(pb, "h1T", (128, 8, TK))
            wslab = sbuf(pb, "wslab", (128, 8, 256))
            kg = sbuf(pb, "kg", (128, 2, TK))
            vg = sbuf(pb, "vg", (128, 2, TK))
            qg = vg
            vtm = sbuf(pb, "vtm", (128, TK // 128, 256))
            ctxg = sbuf(pb, "ctxg", (128, 2, T))
            ptile = sbuf(pb, "ptile", (128, 256))
            ptt = sbuf(pb, "ptt", (128, 256))

            for c in range(TK // 128):
                r0 = c * 128
                pb.wait("sync", "q_vector", pb.tick("vector"))
                pb.wait("sync", "q_scalar", pb.tick("scalar"))
                lv = pb.dma("sync", xc[:], x_in[r0:r0 + 128, :], "xld_s0")
                pb.wait("vector", "xld_s0", lv)
                pb.wait("vector", "xist", pb.cur("xist"))
                pb.wait("scalar", "xld_s0", lv)
                xiv = emit_ln_chunk(pb, xc[:], xc2[:], stats, hc)
                pb.wait("gpsimd", "q_vector", xiv)
                pb.dma("gpsimd", xi_d[r0:r0 + 128, :], xc2[:], "xist")
                pb.wait("scalar", "q_vector", xiv)
                nv = emit_ln_chunk(pb, xc2[:], hc[:], stats, xc)
                pb.op("vector", lambda E: E.tensor_tensor(hc[:], hc[:], g1bc[:], ALU.mult))
                hv = pb.op("vector", lambda E: E.tensor_tensor(hc[:], hc[:], b1bc[:], ALU.add))
                pb.wait("tensor", "q_vector", hv)
                for dch in range(8):
                    bank = p[dch % 2]
                    tv = pb.op("tensor", lambda E, bank=bank, c=c, dch=dch:
                               E.transpose(bank[:, 0:128], hc[:, dch * 128:(dch + 1) * 128], idn[:]))
                    pb.wait("vector", "q_tensor", tv)
                    ev = pb.op("vector", lambda E, bank=bank, c=c, dch=dch:
                               E.tensor_copy(h1T[:, dch, c * 128:(c + 1) * 128], bank[:, 0:128]))
                    pb.wait("tensor", "q_vector", ev)
            pb.phase()

            for g in range(4):
                pb.phase()
                for n_t, Wmat, bias_t in (("k", Wk, bkt), ("v", Wv, bvt), ("q", Wq, bqt)):
                    pb.wait("sync", "q_tensor", pb.tick("tensor"))
                    lvw = pb.dma("sync", wslab[:],
                                 Wmat[:, g * 256:(g + 1) * 256].rearrange("(c p) o -> p c o", p=128),
                                 "wqkv_ld")
                    pb.wait("tensor", "wqkv_ld", lvw)
                    src_off = halo if n_t == "q" else 0
                    n_tok = T if n_t == "q" else TK
                    dst = {"k": kg, "v": vg, "q": qg}[n_t]
                    scl = 0.125 if n_t == "q" else None
                    for oc2 in range(2):
                        for tt in range((n_tok + 511) // 512):
                            t0 = tt * 512
                            bt = min(512, n_tok - t0)
                            bank = p[2 + (tt + oc2) % 2]
                            mmv = 0
                            for k in range(8):
                                mmv = pb.op("tensor", lambda E, k=k, o=oc2, t0=t0, bt=bt, bank=bank, so=src_off:
                                            E.matmul(bank[:, 0:bt], wslab[:, k, o * 128:(o + 1) * 128],
                                                     h1T[:, k, so + t0:so + t0 + bt],
                                                     start=(k == 0), stop=(k == 7)))
                            pb.wait("vector", "q_tensor", mmv)
                            vvv = pb.op("vector", lambda E, o=oc2, t0=t0, bt=bt, bank=bank, g=g,
                                        dd=dst, bias_t=bias_t, scl=scl:
                                        E.tensor_scalar(dd[:, o, t0:t0 + bt], bank[:, 0:bt],
                                                        bias_t[:, g * 2 + o:g * 2 + o + 1], scl,
                                                        ALU.add, ALU.mult if scl else ALU.bypass))
                            pb.wait("tensor", "q_vector", vvv)
                    if n_t == "v":
                        for c in range(TK // 128):
                            for oc2 in range(2):
                                bank = p[6 + oc2]
                                tv = pb.op("tensor", lambda E, c=c, o=oc2, bank=bank:
                                           E.transpose(bank[:, 0:128], vg[:, o, c * 128:(c + 1) * 128], idn[:]))
                                pb.wait("vector", "q_tensor", tv)
                                ev = pb.op("vector", lambda E, c=c, o=oc2, bank=bank:
                                           E.tensor_copy(vtm[:, c, o * 128:(o + 1) * 128], bank[:, 0:128]))
                                pb.wait("tensor", "q_vector", ev)
                for hh in range(4):
                    oc2 = hh // 2
                    row0 = (hh % 2) * 64
                    for it in range(T // 128):
                        i0 = it * 128
                        kb1 = i0 + halo
                        kb0 = kb1 - 128
                        two = kb0 >= 0
                        bank_s = p[2]
                        pb.wait("tensor", "q_scalar", pb.tick("scalar"))
                        if two:
                            pb.op("tensor", lambda E, q0=i0, kb=kb0, o=oc2, r=row0, bank=bank_s:
                                  E.matmul(bank[:, 0:128], qg[r:r + 64, o, q0:q0 + 128],
                                           kg[r:r + 64, o, kb:kb + 128], start=True, stop=True))
                        mmv = pb.op("tensor", lambda E, q0=i0, kb=kb1, o=oc2, r=row0, bank=bank_s:
                                    E.matmul(bank[:, 128:256], qg[r:r + 64, o, q0:q0 + 128],
                                             kg[r:r + 64, o, kb:kb + 128], start=True, stop=True))
                        pb.wait("vector", "q_tensor", mmv)
                        c0 = 0 if two else 128
                        cw = 256 - c0
                        pb.op("vector", lambda E, bank=bank_s, c0=c0, cw=cw:
                              E.tensor_tensor(bank[:, c0:c0 + cw], bank[:, c0:c0 + cw],
                                              msk[:, c0:c0 + cw], ALU.add))
                        pb.op("vector", lambda E, bank=bank_s, c0=c0, cw=cw:
                              E.reduce_max(sm[:, 0:1], bank[:, c0:c0 + cw], axis=AX.X))
                        mv2 = pb.op("vector", lambda E:
                                    E.tensor_scalar(sm[:, 1:2], sm[:, 0:1], -1.0, None, ALU.mult))
                        pb.wait("scalar", "q_vector", mv2)
                        sv = pb.op("scalar", lambda E, bank=bank_s, c0=c0, cw=cw:
                                   E.activation(ptile[:, c0:c0 + cw], bank[:, c0:c0 + cw], AF.Exp,
                                                bias=sm[:, 1:2], accum_out=sm[:, 2:3]))
                        pb.wait("vector", "q_scalar", sv)
                        pb.op("vector", lambda E: E.reciprocal(sm[:, 3:4], sm[:, 2:3]))
                        nv = pb.op("vector", lambda E, c0=c0, cw=cw:
                                   E.tensor_scalar(ptile[:, c0:c0 + cw], ptile[:, c0:c0 + cw],
                                                   sm[:, 3:4], None, ALU.mult))
                        pb.wait("tensor", "q_vector", nv)
                        bank_o = p[5][row0:row0 + 64, 0:128]
                        first = True
                        for half in range(0 if two else 1, 2):
                            bank_t = p[3 + (half % 2)]
                            tv = pb.op("tensor", lambda E, h2=half, bank_t=bank_t:
                                       E.transpose(bank_t[:, 0:128], ptile[:, h2 * 128:(h2 + 1) * 128],
                                                   idn[:]))
                            pb.wait("vector", "q_tensor", tv)
                            ev = pb.op("vector", lambda E, h2=half, bank_t=bank_t:
                                       E.tensor_copy(ptt[:, h2 * 128:(h2 + 1) * 128], bank_t[:, 0:128]))
                            pb.wait("tensor", "q_vector", ev)
                            kb = kb0 if half == 0 else kb1
                            pb.op("tensor", lambda E, kb=kb, h2=half, hh=hh, bank=bank_o, first=first:
                                  E.matmul(bank, vtm[:, kb // 128, hh * 64:(hh + 1) * 64],
                                           ptt[:, h2 * 128:(h2 + 1) * 128],
                                           start=first, stop=(h2 == 1)))
                            first = False
                        mmv = pb.tick("tensor")
                        pb.wait("vector", "q_tensor", mmv)
                        cv = pb.op("vector", lambda E, hh=hh, i0=i0, bank=bank_o, oc2=oc2:
                                   E.tensor_copy(ctxg[(hh % 2) * 64:(hh % 2) * 64 + 64, oc2, i0:i0 + 128],
                                                 bank))
                        pb.wait("tensor", "q_vector", cv)
                pb.wait("gpsimd", "q_vector", pb.tick("vector"))
                for oc2 in range(2):
                    pb.dma("gpsimd", ctx_d[g * 256 + oc2 * 128:g * 256 + (oc2 + 1) * 128, :],
                           ctxg[:, oc2, :], "ctxst")
            pb.phase()

        with pb.scope():
            woslab = sbuf(pb, "woslab", (128, 8, 128))
            ctxc = sbuf(pb, "ctxc", (128, 8, 512))
            y1 = sbuf(pb, "y1", (128, 8, 512))
            for tt in range(T // 512):
                t0 = tt * 512
                for cch in range(8):
                    pb.wait("sync", "q_tensor", pb.tick("tensor"))
                    pb.dma("sync", ctxc[:, cch, :], ctx_d[cch * 128:(cch + 1) * 128,
                                                          t0:t0 + 512], "ctx_ld")
                for och in range(8):
                    pb.wait("sync", "q_tensor", pb.tick("tensor"))
                    lvw = pb.dma("sync", woslab[:],
                                 Wo[:, och * 128:(och + 1) * 128].rearrange("(c p) o -> p c o", p=128),
                                 "wo_ld")
                    pb.wait("tensor", "wo_ld", pb.cur("wo_ld"))
                    pb.wait("tensor", "ctx_ld", pb.cur("ctx_ld"))
                    bank = p[och % 2]
                    pb.wait("tensor", "q_vector", pb.tick("vector"))
                    mmv = 0
                    for k in range(8):
                        mmv = pb.op("tensor", lambda E, k=k, bank=bank:
                                    E.matmul(bank[:, 0:512], woslab[:, k, :], ctxc[:, k, :],
                                             start=(k == 0), stop=(k == 7)))
                    pb.wait("vector", "q_tensor", mmv)
                    vv = pb.op("vector", lambda E, o=och, bank=bank:
                               E.tensor_scalar(y1[:, o, :], bank[:, 0:512], bot[:, o:o + 1], None,
                                               ALU.add))
                pb.wait("tensor", "q_vector", pb.tick("vector"))
                for c in range(4):
                    r0 = t0 + c * 128
                    pb.wait("gpsimd", "q_vector", pb.tick("vector"))
                    lv = pb.dma("gpsimd", xc[:], xi_d[halo + r0:halo + r0 + 128, :], "xild_s0")
                    for och in range(8):
                        bank = p[2 + och % 2]
                        tv = pb.op("tensor", lambda E, bank=bank, c=c, o=och:
                                   E.transpose(bank[:, 0:128], y1[:, o, c * 128:(c + 1) * 128], idn[:]))
                        pb.wait("vector", "q_tensor", tv)
                        if och == 0:
                            pb.wait("vector", "xild_s0", pb.cur("xild_s0"))
                            pb.wait("vector", "xmst", pb.cur("xmst"))
                        vv = pb.op("vector", lambda E, bank=bank, c=c, o=och:
                                   E.tensor_tensor(xc2[:, o * 128:(o + 1) * 128], bank[:, 0:128],
                                                   xc[:, o * 128:(o + 1) * 128], ALU.add))
                        pb.wait("tensor", "q_vector", vv)
                    pb.wait("gpsimd", "q_vector", pb.tick("vector"))
                    pb.dma("gpsimd", xmid_d[r0:r0 + 128, :], xc2[:], "xmst")
            pb.phase()

        with pb.scope():
            h2T = sbuf(pb, "h2T", (128, 8, T))
            w1slab = sbuf(pb, "w1slab", (128, 2, 8, 128))
            gelc = sbuf(pb, "gelc", (128, 2, 512))
            for c in range(T // 128):
                r0 = c * 128
                pb.wait("sync", "q_vector", pb.tick("vector"))
                pb.wait("sync", "q_scalar", pb.tick("scalar"))
                lv = pb.dma("sync", xc[:], xmid_d[r0:r0 + 128, :], "xm_s0")
                pb.wait("vector", "xm_s0", lv)
                pb.wait("scalar", "xm_s0", lv)
                nv = emit_ln_chunk(pb, xc[:], hc[:], stats, xc2)
                pb.op("vector", lambda E: E.tensor_tensor(hc[:], hc[:], g2bc[:], ALU.mult))
                hv = pb.op("vector", lambda E: E.tensor_tensor(hc[:], hc[:], b2bc[:], ALU.add))
                pb.wait("tensor", "q_vector", hv)
                for dch in range(8):
                    bank = p[dch % 2]
                    tv = pb.op("tensor", lambda E, bank=bank, c=c, dch=dch:
                               E.transpose(bank[:, 0:128], hc[:, dch * 128:(dch + 1) * 128], idn[:]))
                    pb.wait("vector", "q_tensor", tv)
                    ev = pb.op("vector", lambda E, bank=bank, c=c, dch=dch:
                               E.tensor_copy(h2T[:, dch, c * 128:(c + 1) * 128], bank[:, 0:128]))
                    pb.wait("tensor", "q_vector", ev)
            pb.phase()
            for f in range(32):
                sl = f % 2
                pb.wait("sync", "q_tensor", pb.tick("tensor"))
                lvw = pb.dma("sync", w1slab[:, sl, :, :],
                             W1[:, f * 128:(f + 1) * 128].rearrange("(c p) o -> p c o", p=128),
                             "w1_s%d" % sl)
                pb.wait("tensor", "w1_s%d" % sl, pb.cur("w1_s%d" % sl))
                for tt in range(T // 512):
                    t0 = tt * 512
                    bank = p[2 + tt % 2]
                    pb.wait("tensor", "q_scalar", pb.tick("scalar"))
                    mmv = 0
                    for k in range(8):
                        mmv = pb.op("tensor", lambda E, k=k, f=f, t0=t0, bank=bank, sl=sl:
                                    E.matmul(bank[:, 0:512], w1slab[:, sl, k, :],
                                             h2T[:, k, t0:t0 + 512], start=(k == 0), stop=(k == 7)))
                    pb.wait("scalar", "q_tensor", mmv)
                    pb.wait("scalar", "hgst", pb.cur("hgst"))
                    gv2 = pb.op("scalar", lambda E, f=f, tt=tt, bank=bank:
                                E.activation(gelc[:, tt % 2, :], bank[:, 0:512], AF.Gelu_apprx_tanh,
                                             bias=bf1t[:, f:f + 1]))
                    pb.wait("gpsimd", "q_scalar", gv2)
                    pb.dma("gpsimd", hg_d[f * 128:(f + 1) * 128, t0:t0 + 512],
                           gelc[:, tt % 2, :], "hgst")
            pb.phase()

        with pb.scope():
            hgt = sbuf(pb, "hgt", (128, 32, 512))
            w2slab = sbuf(pb, "w2slab", (128, 2, 32, 128))
            youtf = sbuf(pb, "youtf", (128, 8, 512))
            for tt in range(T // 512):
                t0 = tt * 512
                pb.wait("sync", "q_tensor", pb.tick("tensor"))
                for fch in range(32):
                    pb.dma("sync", hgt[:, fch, :], hg_d[fch * 128:(fch + 1) * 128, t0:t0 + 512],
                           "hg_ld")
                for och in range(8):
                    sl = och % 2
                    pb.wait("sync", "q_tensor", pb.tick("tensor"))
                    pb.dma("sync", w2slab[:, sl, :, :],
                           W2[:, och * 128:(och + 1) * 128].rearrange("(c p) o -> p c o", p=128),
                           "w2_s%d" % sl)
                    pb.wait("tensor", "w2_s%d" % sl, pb.cur("w2_s%d" % sl))
                    pb.wait("tensor", "hg_ld", pb.cur("hg_ld"))
                    bank = p[4 + och % 2]
                    pb.wait("tensor", "q_vector", pb.tick("vector"))
                    mmv = 0
                    for k in range(32):
                        mmv = pb.op("tensor", lambda E, k=k, bank=bank, sl=sl:
                                    E.matmul(bank[:, 0:512], w2slab[:, sl, k, :], hgt[:, k, :],
                                             start=(k == 0), stop=(k == 31)))
                    pb.wait("vector", "q_tensor", mmv)
                    vv = pb.op("vector", lambda E, o=och, bank=bank:
                               E.tensor_copy(youtf[:, o, :], bank[:, 0:512]))
                pb.wait("tensor", "q_vector", pb.tick("vector"))
                for c in range(4):
                    r0 = t0 + c * 128
                    pb.wait("gpsimd", "q_vector", pb.tick("vector"))
                    lvm = pb.dma("gpsimd", xc[:], xmid_d[r0:r0 + 128, :], "fm_s0")
                    lvx = pb.dma("gpsimd", xc2[:], x_in[halo + r0:halo + r0 + 128, :], "fx_s0")
                    for och in range(8):
                        bank = p[och % 2]
                        tv = pb.op("tensor", lambda E, bank=bank, c=c, o=och:
                                   E.transpose(bank[:, 0:128], youtf[:, o, c * 128:(c + 1) * 128],
                                               idn[:]))
                        pb.wait("vector", "q_tensor", tv)
                        if och == 0:
                            pb.wait("vector", "fm_s0", pb.cur("fm_s0"))
                            pb.wait("vector", "fx_s0", pb.cur("fx_s0"))
                            pb.wait("vector", "yst", pb.cur("yst"))
                        pb.op("vector", lambda E, bank=bank, o=och:
                              E.tensor_tensor(hc[:, o * 128:(o + 1) * 128], bank[:, 0:128],
                                              xc[:, o * 128:(o + 1) * 128], ALU.add))
                        vv = pb.op("vector", lambda E, bank=bank, o=och:
                                   E.tensor_tensor(hc[:, o * 128:(o + 1) * 128],
                                                   hc[:, o * 128:(o + 1) * 128],
                                                   xc2[:, o * 128:(o + 1) * 128], ALU.add))
                        pb.wait("tensor", "q_vector", vv)
                    fv = pb.op("vector", lambda E: E.tensor_tensor(hc[:], hc[:], b2bc2[:], ALU.add))
                    pb.wait("gpsimd", "q_vector", fv)
                    pb.dma("gpsimd", y[r0:r0 + 128, :], hc[:], "yst")
        pb.wait("gpsimd", "yst", pb.cur("yst"))
    pb.prefix = _oldpfx


def build_l0attn(T, halo):
    TK = T + halo
    nc = bass.Bass("TRN2", target_bir_lowering=False, debug=False)
    x_in = nc.dram_tensor("x", [TK, D], F32, kind="ExternalInput").ap()
    Wq = nc.dram_tensor("Wq", [D, D], F32, kind="ExternalInput").ap()
    Wk = nc.dram_tensor("Wk", [D, D], F32, kind="ExternalInput").ap()
    Wv = nc.dram_tensor("Wv", [D, D], F32, kind="ExternalInput").ap()
    Wo = nc.dram_tensor("Wo", [D, D], F32, kind="ExternalInput").ap()
    W1 = nc.dram_tensor("W1", [D, DFF], F32, kind="ExternalInput").ap()
    W2 = nc.dram_tensor("W2", [DFF, D], F32, kind="ExternalInput").ap()
    bq = nc.dram_tensor("bq", [D], F32, kind="ExternalInput").ap()
    bk = nc.dram_tensor("bk", [D], F32, kind="ExternalInput").ap()
    bv = nc.dram_tensor("bv", [D], F32, kind="ExternalInput").ap()
    bo = nc.dram_tensor("bo", [D], F32, kind="ExternalInput").ap()
    b1 = nc.dram_tensor("b1", [DFF], F32, kind="ExternalInput").ap()
    b2 = nc.dram_tensor("b2", [D], F32, kind="ExternalInput").ap()
    lnp = nc.dram_tensor("lnp", [4, D], F32, kind="ExternalInput").ap()
    y = nc.dram_tensor("y", [T, D], F32, kind="ExternalOutput").ap()
    est = contextlib.ExitStack()
    with est:
        pb = PB(nc, est)
        idn = make_identity(pb)
        p = psum8(pb)
        emit_l0attn(pb, p, idn, x_in, Wq, Wk, Wv, Wo, W1, W2, bq, bk, bv, bo, b1, b2, lnp, y, T, halo)
        pb.build()
    return nc


def emit_moe(pb, p, idn, x_in, gw, Wg, Wu, Wd, y, tag=""):
    nc = pb.nc
    _oldpfx = pb.prefix
    pb.prefix = _oldpfx + tag
    S = SEQ
    with pb.scope():
        xsel_d = nc.dram_tensor(pb.prefix + "xsel_d", [4, D, KSEL], F32, kind="Internal").ap()
        gat_d = nc.dram_tensor(pb.prefix + "gat_d", [4, KSEL], F32, kind="Internal").ap()
        ctbh_d = nc.dram_tensor(pb.prefix + "ctbh_d", [4, KSEL, D], BF16, kind="Internal").ap()
        ctbl_d = nc.dram_tensor(pb.prefix + "ctbl_d", [4, KSEL, D], BF16, kind="Internal").ap()
        xc = sbuf(pb, "xc", (128, D))
        hc = sbuf(pb, "hc", (128, D))
        stats = sbuf(pb, "stats", (128, 8))
        sm = sbuf(pb, "sm", (128, 4))
        logits_tm = sbuf(pb, "logits_tm", (128, 32, NE))
        aff_tm = sbuf(pb, "aff_tm", (128, 32, NE))
        affh_tm = sbuf(pb, "affh_tm", (128, 32, NE), BF16)
        affl_tm = sbuf(pb, "affl_tm", (128, 32, NE), BF16)
        slots_tm = sbuf(pb, "slots_tm", (128, 32, 4))
        slots_em = sbuf(pb, "slots_em", (8, S))
        srow = sbuf(pb, "srow", (128, 2, 512))
        scol = sbuf(pb, "scol", (128, 8))
        ones1 = sbuf(pb, "ones1", (1, 128))

        with pb.scope():
            t1 = sbuf(pb, "t1i", (128, 2, 512), I32)
            t2 = sbuf(pb, "t2i", (128, 8), I32)
            pb.op("gpsimd", lambda E: E.iota(t1[:], [[512, 2], [1, 512]], base=0,
                                             channel_multiplier=0))
            gv = pb.op("gpsimd", lambda E: E.iota(t2[:], [[128, 8]], base=0,
                                                  channel_multiplier=1))
            pb.wait("vector", "q_gpsimd", gv)
            pb.op("vector", lambda E: E.tensor_copy(srow[:], t1[:]))
            pb.op("vector", lambda E: E.tensor_copy(scol[:], t2[:]))
        ov = pb.op("vector", lambda E: E.memset(ones1[:], 1.0))
        pb.wait("tensor", "q_vector", ov)

        with pb.scope():
            h_hi = sbuf(pb, "h_hi", (128, 32, D), BF16)
            h_lo = sbuf(pb, "h_lo", (128, 32, D), BF16)
            gwt = sbuf(pb, "gwt", (128, 8, NE))
            pb.wait("sync", "q_vector", pb.tick("vector"))
            pb.dma("sync", gwt[:], gw.rearrange("(c p) e -> p c e", p=128), "gwld")
            pb.wait("tensor", "gwld", pb.cur("gwld"))
            for c in range(S // 128):
                r0 = c * 128
                pb.wait("sync", "q_vector", pb.tick("vector"))
                pb.wait("sync", "q_scalar", pb.tick("scalar"))
                lv = pb.dma("sync", xc[:], x_in[r0:r0 + 128, :], "xld_s0")
                pb.wait("vector", "xld_s0", lv)
                pb.wait("scalar", "xld_s0", lv)
                hv = emit_ln_chunk(pb, xc[:], hc[:], stats, hc[:])
                pb.op("vector", lambda E, c=c: E.tensor_copy(h_hi[:, c, :], hc[:]))
                pb.op("vector", lambda E, c=c: E.tensor_tensor(xc[:], hc[:], h_hi[:, c, :],
                                                               ALU.subtract))
                hv2 = pb.op("vector", lambda E, c=c: E.tensor_copy(h_lo[:, c, :], xc[:]))
                pb.wait("tensor", "q_vector", hv2)
                bank_l = p[2]
                for dch in range(8):
                    bank = p[dch % 2]
                    tv = pb.op("tensor", lambda E, bank=bank, dch=dch:
                               E.transpose(bank[:, 0:128], hc[:, dch * 128:(dch + 1) * 128],
                                           idn[:]))
                    pb.wait("vector", "q_tensor", tv)
                    ev = pb.op("vector", lambda E, bank=bank, dch=dch:
                               E.tensor_copy(hc[:, dch * 128:(dch + 1) * 128], bank[:, 0:128]))
                    pb.wait("tensor", "q_vector", ev)
                    pb.op("tensor", lambda E, bank=bank_l, dch=dch:
                          E.matmul(bank[:, 0:NE], hc[:, dch * 128:(dch + 1) * 128],
                                   gwt[:, dch, :], start=(dch == 0), stop=(dch == 7)))
                mmv = pb.tick("tensor")
                pb.wait("vector", "q_tensor", mmv)
                lvv = pb.op("vector", lambda E, c=c, bank=bank_l:
                            E.tensor_copy(logits_tm[:, c, :], bank[:, 0:NE]))
                pb.wait("tensor", "q_vector", lvv)

            pb.phase()
            with pb.scope():
                a_em = sbuf(pb, "a_em", (8, S))
                m_em = sbuf(pb, "m_em", (8, S), BF16)
                ones8 = sbuf(pb, "ones8", (8, S), BF16)
                th = sbuf(pb, "th", (8, 8))
                pb.op("vector", lambda E: E.memset(ones8[:], 1.0))
                for c in range(32):
                    pb.op("vector", lambda E, c=c: E.reduce_max(sm[:, 0:1], logits_tm[:, c, :],
                                                                axis=AX.X))
                    mv = pb.op("vector", lambda E: E.tensor_scalar(sm[:, 1:2], sm[:, 0:1], -1.0,
                                                                   None, ALU.mult))
                    pb.wait("scalar", "q_vector", mv)
                    sv = pb.op("scalar", lambda E, c=c:
                               E.activation(aff_tm[:, c, :], logits_tm[:, c, :], AF.Exp,
                                            bias=sm[:, 1:2], accum_out=sm[:, 2:3]))
                    pb.wait("vector", "q_scalar", sv)
                    pb.op("vector", lambda E: E.reciprocal(sm[:, 3:4], sm[:, 2:3]))
                    pb.op("vector", lambda E, c=c:
                          E.tensor_scalar(aff_tm[:, c, :], aff_tm[:, c, :], sm[:, 3:4],
                                          None, ALU.mult))
                    pb.op("vector", lambda E, c=c: E.tensor_copy(affh_tm[:, c, :],
                                                                 aff_tm[:, c, :]))
                    av2 = pb.op("vector", lambda E, c=c:
                                E.scalar_tensor_tensor(affl_tm[:, c, :], affh_tm[:, c, :], -1.0,
                                                       aff_tm[:, c, :], ALU.mult, ALU.add))
                    pb.wait("tensor", "q_vector", av2)
                    bank = p[3]
                    tv = pb.op("tensor", lambda E, c=c, bank=bank:
                               E.transpose(bank[0:NE, 0:128], aff_tm[:, c, :], idn[:]))
                    pb.wait("vector", "q_tensor", tv)
                    ev = pb.op("vector", lambda E, c=c, bank=bank:
                               E.tensor_copy(a_em[:, c * 128:(c + 1) * 128], bank[0:NE, 0:128]))
                    pb.wait("tensor", "q_vector", ev)
                pb.op("vector", lambda E: E.memset(th[:, 0:1], 0.0))
                pb.op("vector", lambda E: E.memset(th[:, 1:2], 1.0))
                for it in range(40):
                    pb.op("vector", lambda E: E.tensor_tensor(th[:, 2:3], th[:, 0:1], th[:, 1:2],
                                                              ALU.add))
                    pb.op("vector", lambda E: E.tensor_scalar(th[:, 2:3], th[:, 2:3], 0.5, None,
                                                              ALU.mult))
                    pb.op("vector", lambda E: E.tensor_scalar(m_em[:], a_em[:], th[:, 2:3], None,
                                                              ALU.is_ge))
                    pb.op("vector", lambda E: E.reduce_sum(th[:, 3:4], m_em[:], axis=AX.X))
                    pb.op("vector", lambda E: E.tensor_scalar(th[:, 4:5], th[:, 3:4], float(KSEL),
                                                              None, ALU.is_ge))
                    pb.op("vector", lambda E: E.tensor_scalar(th[:, 5:6], th[:, 4:5], -1.0, 1.0,
                                                              ALU.mult, ALU.add))
                    pb.op("vector", lambda E: E.tensor_tensor(th[:, 6:7], th[:, 0:1], th[:, 5:6],
                                                              ALU.mult))
                    pb.op("vector", lambda E: E.tensor_tensor(th[:, 7:8], th[:, 2:3], th[:, 4:5],
                                                              ALU.mult))
                    pb.op("vector", lambda E: E.tensor_tensor(th[:, 0:1], th[:, 6:7], th[:, 7:8],
                                                              ALU.add))
                    pb.op("vector", lambda E: E.tensor_tensor(th[:, 6:7], th[:, 1:2], th[:, 4:5],
                                                              ALU.mult))
                    pb.op("vector", lambda E: E.tensor_tensor(th[:, 7:8], th[:, 2:3], th[:, 5:6],
                                                              ALU.mult))
                    pb.op("vector", lambda E: E.tensor_tensor(th[:, 1:2], th[:, 6:7], th[:, 7:8],
                                                              ALU.add))
                pb.op("vector", lambda E: E.tensor_scalar(m_em[:], a_em[:], th[:, 0:1], None,
                                                          ALU.is_ge))
                pb.op("vector", lambda E: E.tensor_tensor_scan(slots_em[:], ones8[:], m_em[:],
                                                               0.0, ALU.mult, ALU.add))
                pb.op("vector", lambda E: E.tensor_scalar(a_em[:], slots_em[:], float(KSEL),
                                                          None, ALU.is_le))
                pb.op("vector", lambda E: E.tensor_tensor(m_em[:], m_em[:], a_em[:], ALU.mult))
                pb.op("vector", lambda E: E.tensor_scalar(slots_em[:], slots_em[:], 1.0, None,
                                                          ALU.subtract))
                pb.op("vector", lambda E: E.tensor_tensor(slots_em[:], slots_em[:], m_em[:],
                                                          ALU.mult))
                pb.op("vector", lambda E: E.tensor_scalar(a_em[:], m_em[:], -1.0, 1.0, ALU.mult,
                                                          ALU.add))
                pb.op("vector", lambda E: E.tensor_scalar(a_em[:], a_em[:], -7.0, None, ALU.mult))
                sev = pb.op("vector", lambda E: E.tensor_tensor(slots_em[:], slots_em[:], a_em[:],
                                                                ALU.add))
                pb.wait("tensor", "q_vector", sev)
                for c in range(32):
                    bank = p[3]
                    tv = pb.op("tensor", lambda E, c=c, bank=bank:
                               E.transpose(bank[:, 0:4], slots_em[0:4, c * 128:(c + 1) * 128],
                                           idn[0:4, 0:4]))
                    pb.wait("vector", "q_tensor", tv)
                    ev = pb.op("vector", lambda E, c=c, bank=bank:
                               E.tensor_copy(slots_tm[:, c, :], bank[:, 0:4]))
                    pb.wait("tensor", "q_vector", ev)
            pb.phase()

            with pb.scope():
                selT = sbuf(pb, "selT", (128, 32, 512), BF16)
                xso = sbuf(pb, "xso", (128, 2, 512))
                gatro = sbuf(pb, "gatro", (1, 512))
                for el in range(4):
                    for st in range(2):
                        bv = 0
                        for c in range(32):
                            bv = pb.op("vector", lambda E, el=el, st=st, c=c:
                                       E.tensor_scalar(selT[:, c, :], srow[:, st, :],
                                                       slots_tm[:, c, el:el + 1], None,
                                                       ALU.is_equal))
                        pb.wait("tensor", "q_vector", bv)
                        for c in range(32):
                            for dch in range(8):
                                for hl in range(2):
                                    hsrc = h_hi if hl == 0 else h_lo
                                    pb.op("tensor", lambda E, c=c, dch=dch, hsrc=hsrc, hl=hl:
                                          E.matmul(p[dch][:, 0:512],
                                                   hsrc[:, c, dch * 128:(dch + 1) * 128],
                                                   selT[:, c, :],
                                                   start=(c == 0 and hl == 0),
                                                   stop=(c == 31 and hl == 1)))
                        mmv = pb.tick("tensor")
                        pb.wait("vector", "q_tensor", mmv)
                        for dch in range(8):
                            sl = dch % 2
                            pb.wait("vector", "xsst_s%d" % sl, pb.cur("xsst_s%d" % sl))
                            vv = pb.op("vector", lambda E, dch=dch, sl=sl:
                                       E.tensor_copy(xso[:, sl, :], p[dch][:, 0:512]))
                            pb.wait("gpsimd", "q_vector", vv)
                            pb.dma("gpsimd", xsel_d[el, dch * 128:(dch + 1) * 128,
                                                    st * 512:(st + 1) * 512],
                                   xso[:, sl, :], "xsst_s%d" % sl)
                        pb.wait("tensor", "q_vector", pb.tick("vector"))
                        for c in range(32):
                            for hl in range(2):
                                asrc = affh_tm if hl == 0 else affl_tm
                                pb.op("tensor", lambda E, c=c, el=el, asrc=asrc, hl=hl:
                                      E.matmul(p[0][0:1, 0:512], asrc[:, c, el:el + 1],
                                               selT[:, c, :],
                                               start=(c == 0 and hl == 0),
                                               stop=(c == 31 and hl == 1)))
                        mmv = pb.tick("tensor")
                        pb.wait("vector", "q_tensor", mmv)
                        pb.wait("vector", "gtst", pb.cur("gtst"))
                        gvv = pb.op("vector", lambda E: E.tensor_copy(gatro[:], p[0][0:1, 0:512]))
                        pb.wait("gpsimd", "q_vector", gvv)
                        pb.dma("gpsimd",
                               gat_d[el].rearrange("(a s) -> a s", a=1)[:, st * 512:(st + 1) * 512],
                               gatro[:], "gtst")
        pb.phase()

        with pb.scope():
            xselb = sbuf(pb, "xselb", (128, 8, KSEL))
            wslab = sbuf(pb, "wslab", (128, 2, 8, 128))
            wdsl = sbuf(pb, "wdsl", (128, 2, 512))
            gvbuf = sbuf(pb, "gvbuf", (128, KSEL))
            hblk = sbuf(pb, "hblk", (128, 16, KSEL))
            gatc = sbuf(pb, "gatc", (128, 8))
            gro = sbuf(pb, "gro", (1, KSEL))
            ctf = sbuf(pb, "ctf", (128, 512))
            cth = sbuf(pb, "cth", (128, 2, 512), BF16)
            ctl = sbuf(pb, "ctl", (128, 2, 512), BF16)
            for el in range(4):
                pb.phase()
                pb.dma("sync", xselb[:], xsel_d[el].rearrange("(c p) s -> p c s", p=128), "xsl")
                lvg = pb.dma("sync", gro[:], gat_d[el].rearrange("(a s) -> a s", a=1), "gtl")
                pb.wait("tensor", "xsl", pb.cur("xsl"))
                pb.wait("tensor", "gtl", pb.cur("gtl"))
                for sc in range(8):
                    bank = p[6]
                    tv = pb.op("tensor", lambda E, sc=sc, bank=bank:
                               E.transpose(bank[:, 0:1], gro[:, sc * 128:(sc + 1) * 128],
                                           idn[0:1, 0:1]))
                    pb.wait("vector", "q_tensor", tv)
                    ev = pb.op("vector", lambda E, sc=sc, bank=bank:
                               E.tensor_copy(gatc[:, sc:sc + 1], bank[:, 0:1]))
                    pb.wait("tensor", "q_vector", ev)
                for ic in range(16):
                    sl = ic % 2
                    pb.wait("sync", "q_tensor", pb.tick("tensor"))
                    pb.dma("sync", wslab[:, sl, :, :],
                           Wg[el, :, ic * 128:(ic + 1) * 128].rearrange("(c p) i -> p c i", p=128),
                           "wg_s%d" % sl)
                    pb.wait("tensor", "wg_s%d" % sl, pb.cur("wg_s%d" % sl))
                    pb.wait("tensor", "q_scalar", pb.tick("scalar"))
                    pb.wait("scalar", "q_vector", pb.tick("vector"))
                    for st in range(2):
                        bank = p[2 + st]
                        mmv = 0
                        for k in range(8):
                            mmv = pb.op("tensor", lambda E, k=k, st=st, bank=bank, sl=sl:
                                        E.matmul(bank[:, 0:512], wslab[:, sl, k, :],
                                                 xselb[:, k, st * 512:(st + 1) * 512],
                                                 start=(k == 0), stop=(k == 7)))
                        pb.wait("scalar", "q_tensor", mmv)
                        pb.op("scalar", lambda E, st=st, bank=bank:
                              E.activation(gvbuf[:, st * 512:(st + 1) * 512], bank[:, 0:512],
                                           AF.Silu))
                    sv = pb.tick("scalar")
                    pb.wait("sync", "q_tensor", pb.tick("tensor"))
                    pb.dma("sync", wslab[:, sl, :, :],
                           Wu[el, :, ic * 128:(ic + 1) * 128].rearrange("(c p) i -> p c i", p=128),
                           "wu_s%d" % sl)
                    pb.wait("tensor", "wu_s%d" % sl, pb.cur("wu_s%d" % sl))
                    pb.wait("tensor", "q_scalar", sv)
                    for st in range(2):
                        bank = p[4 + st]
                        mmv = 0
                        for k in range(8):
                            mmv = pb.op("tensor", lambda E, k=k, st=st, bank=bank, sl=sl:
                                        E.matmul(bank[:, 0:512], wslab[:, sl, k, :],
                                                 xselb[:, k, st * 512:(st + 1) * 512],
                                                 start=(k == 0), stop=(k == 7)))
                        pb.wait("vector", "q_tensor", mmv)
                        pb.wait("vector", "q_scalar", sv)
                        pb.op("vector", lambda E, ic=ic, st=st, bank=bank:
                              E.tensor_tensor(hblk[:, ic, st * 512:(st + 1) * 512],
                                              gvbuf[:, st * 512:(st + 1) * 512],
                                              bank[:, 0:512], ALU.mult))
                    pb.wait("tensor", "q_vector", pb.tick("vector"))
                for dh in range(2):
                    pb.phase()
                    for ic in range(16):
                        sl = ic % 2
                        pb.wait("sync", "q_tensor", pb.tick("tensor"))
                        pb.dma("sync", wdsl[:, sl, :],
                               Wd[el, ic * 128:(ic + 1) * 128, dh * 512:(dh + 1) * 512],
                               "wd_s%d" % sl)
                        pb.wait("tensor", "wd_s%d" % sl, pb.cur("wd_s%d" % sl))
                        for sg in range(8):
                            pb.op("tensor", lambda E, ic=ic, sg=sg, sl=sl:
                                  E.matmul(p[sg][:, 0:512], hblk[:, ic, sg * 128:(sg + 1) * 128],
                                           wdsl[:, sl, :], start=(ic == 0), stop=(ic == 15)))
                    mmv = pb.tick("tensor")
                    pb.wait("vector", "q_tensor", mmv)
                    for sg in range(8):
                        sl = sg % 2
                        pb.wait("vector", "cst_s%d" % sl, pb.cur("cst_s%d" % sl))
                        pb.op("vector", lambda E, sg=sg:
                              E.tensor_scalar(ctf[:], p[sg][:, 0:512], gatc[:, sg:sg + 1], 0.5,
                                              ALU.mult, ALU.mult))
                        pb.op("vector", lambda E, sl=sl: E.tensor_copy(cth[:, sl, :], ctf[:]))
                        vv = pb.op("vector", lambda E, sl=sl:
                                   E.scalar_tensor_tensor(ctl[:, sl, :], cth[:, sl, :],
                                                          -1.0, ctf[:], ALU.mult, ALU.add))
                        pb.wait("gpsimd", "q_vector", vv)
                        pb.dma("gpsimd", ctbh_d[el, sg * 128:(sg + 1) * 128,
                                                dh * 512:(dh + 1) * 512], cth[:, sl, :],
                               "cst_s%d" % sl)
                        pb.dma("gpsimd", ctbl_d[el, sg * 128:(sg + 1) * 128,
                                                dh * 512:(dh + 1) * 512], ctl[:, sl, :],
                               "cst_s%d" % sl)
        pb.phase()

        with pb.scope():
            ctbh = sbuf(pb, "ctbh", (128, 2, 8, D), BF16)
            ctbl = sbuf(pb, "ctbl", (128, 2, 8, D), BF16)
            sbc = sbuf(pb, "sbc", (128, 2, S))
            selS = sbuf(pb, "selS", (128, 2, 8, 512), BF16)
            yo = sbuf(pb, "yo", (128, D))
            yi = sbuf(pb, "yi", (128, D))
            eh = sbuf(pb, "eh", (8, 512))
            ti = sbuf(pb, "ti", (8, 512), I32)
            tp = sbuf(pb, "tp", (8, 1), I32)
            tpf = sbuf(pb, "tpf", (8, 1))
            pb.op("gpsimd", lambda E: E.iota(ti[:], [[1, 4], [0, 128]], base=0,
                                             channel_multiplier=0))
            gv = pb.op("gpsimd", lambda E: E.iota(tp[:], [[0, 1]], base=0,
                                                  channel_multiplier=1))
            pb.wait("vector", "q_gpsimd", gv)
            pb.op("vector", lambda E: E.tensor_copy(tpf[:], tp[:]))
            ev0 = pb.op("vector", lambda E: E.tensor_scalar(eh[:], ti[:], tpf[:, 0:1], None,
                                                            ALU.is_equal))
            pb.wait("tensor", "q_vector", ev0)
            for grp in range(2):
                pb.phase()
                for gi in range(2):
                    el = grp * 2 + gi
                    pb.dma("sync", ctbh[:, gi, :, :],
                           ctbh_d[el].rearrange("(g p) d -> p g d", p=128), "ctl_ld")
                    pb.dma("sync", ctbl[:, gi, :, :],
                           ctbl_d[el].rearrange("(g p) d -> p g d", p=128), "ctl_ld")
                    for c0 in range(0, S, 512):
                        bank = p[6]
                        tv = pb.op("tensor", lambda E, el=el, c0=c0, bank=bank:
                                   E.matmul(bank[:, 0:512], eh[0:8, el * 128:(el + 1) * 128],
                                            slots_em[0:8, c0:c0 + 512], start=True, stop=True))
                        pb.wait("vector", "q_tensor", tv)
                        cv = pb.op("vector", lambda E, gi=gi, c0=c0, bank=bank:
                                   E.tensor_copy(sbc[:, gi, c0:c0 + 512], bank[:, 0:512]))
                        pb.wait("tensor", "q_vector", cv)
                pb.wait("tensor", "ctl_ld", pb.cur("ctl_ld"))
                for tg in range(8):
                    bv = 0
                    for gi in range(2):
                        for sc in range(8):
                            bv = pb.op("vector", lambda E, gi=gi, sc=sc, tg=tg:
                                       E.tensor_scalar(selS[:, gi, sc, :],
                                                       sbc[:, gi, tg * 512:(tg + 1) * 512],
                                                       scol[:, sc:sc + 1], None, ALU.is_equal))
                    pb.wait("tensor", "q_vector", bv)
                    for tc in range(4):
                        for dt in range(2):
                            bank = p[tc * 2 + dt]
                            nmm = 0
                            for gi in range(2):
                                for sc in range(8):
                                    for hl in range(2):
                                        csrc = ctbh if hl == 0 else ctbl
                                        pb.op("tensor", lambda E, gi=gi, sc=sc, hl=hl, tc=tc,
                                              dt=dt, bank=bank, csrc=csrc, nmm=nmm:
                                              E.matmul(bank[:, 0:512],
                                                       selS[:, gi, sc, tc * 128:(tc + 1) * 128],
                                                       csrc[:, gi, sc, dt * 512:(dt + 1) * 512],
                                                       start=(nmm == 0), stop=(nmm == 31)))
                                        nmm += 1
                    mmv = pb.tick("tensor")
                    pb.wait("vector", "q_tensor", mmv)
                    for tc in range(4):
                        r0 = tg * 512 + tc * 128
                        if grp == 1:
                            pb.wait("sync", "q_vector", pb.tick("vector"))
                            lv = pb.dma("sync", yi[:], y[r0:r0 + 128, :], "yrd_s0")
                            pb.wait("vector", "yrd_s0", lv)
                        pb.wait("vector", "yst", pb.cur("yst"))
                        for dt in range(2):
                            bank = p[tc * 2 + dt]
                            if grp == 0:
                                pb.op("vector", lambda E, dt=dt, bank=bank:
                                      E.tensor_copy(yo[:, dt * 512:(dt + 1) * 512],
                                                    bank[:, 0:512]))
                            else:
                                pb.op("vector", lambda E, dt=dt, bank=bank:
                                      E.tensor_tensor(yo[:, dt * 512:(dt + 1) * 512],
                                                      bank[:, 0:512],
                                                      yi[:, dt * 512:(dt + 1) * 512], ALU.add))
                        vv = pb.tick("vector")
                        pb.wait("gpsimd", "q_vector", vv)
                        pb.dma("gpsimd", y[r0:r0 + 128, :], yo[:], "yst")
        pb.wait("gpsimd", "yst", pb.cur("yst"))
    pb.prefix = _oldpfx


def build_moe():
    S = SEQ
    nc = bass.Bass("TRN2", target_bir_lowering=False, debug=False)
    x_in = nc.dram_tensor("x", [S, D], F32, kind="ExternalInput").ap()
    gw = nc.dram_tensor("gw", [D, NE], F32, kind="ExternalInput").ap()
    Wg = nc.dram_tensor("Wg", [4, D, NI], F32, kind="ExternalInput").ap()
    Wu = nc.dram_tensor("Wu", [4, D, NI], F32, kind="ExternalInput").ap()
    Wd = nc.dram_tensor("Wd", [4, NI, D], F32, kind="ExternalInput").ap()
    y = nc.dram_tensor("y", [S, D], F32, kind="ExternalOutput").ap()
    est = contextlib.ExitStack()
    with est:
        pb = PB(nc, est)
        idn = make_identity(pb)
        p = psum8(pb)
        emit_moe(pb, p, idn, x_in, gw, Wg, Wu, Wd, y)
        pb.build()
    return nc


def emit_l1attn(pb, p, idn, x3, pa, pbx, wqkvT, wprojT, bproj, rpeq, rpek, rpev, y, N, NQ,
                q0, tag="", HH=NH, resid=True):
    nc = pb.nc
    _oldpfx = pb.prefix
    pb.prefix = _oldpfx + tag
    H, hd = HH, HD
    HQ = H * 64
    QT = HQ // 512
    KCH = HQ // 128
    NC = N // 128
    NJT = N // 512
    NCQ = NQ // 128
    SW = 64 + N + 64
    with pb.scope():
        x4_d = nc.dram_tensor(pb.prefix + "x4_d", [N, D], F32, kind="Internal").ap()
        hT_d = nc.dram_tensor(pb.prefix + "hT_d", [D, N], F32, kind="Internal").ap()
        qkT_d = nc.dram_tensor(pb.prefix + "qkT_d", [2 * HQ, N], F32, kind="Internal").ap()
        vt_d = nc.dram_tensor(pb.prefix + "vt_d", [N, HQ], F32, kind="Internal").ap()
        kldbm_d = nc.dram_tensor(pb.prefix + "kldbm_d", [NB, N + 128], F32, kind="Internal").ap()
        band_d = nc.dram_tensor(pb.prefix + "band_d", [128, 256], F32, kind="Internal").ap()
        ctx_d = nc.dram_tensor(pb.prefix + "ctx_d", [HQ, NQ], F32, kind="Internal").ap()
        ones1 = sbuf(pb, "ones1", (1, 128))
        onescol = sbuf(pb, "onescol", (128, 1))
        pb.op("vector", lambda E: E.memset(ones1[:], 1.0))
        ov = pb.op("vector", lambda E: E.memset(onescol[:], 1.0))
        Tm = {}
        Tl = {}
        dclf = sbuf(pb, "dclf", (128, 512))
        with pb.scope():
            dcli = sbuf(pb, "dcli", (128, 512), I32)
            gv = pb.op("gpsimd", lambda E: E.iota(dcli[:], [[1, 512]], base=0,
                                                  channel_multiplier=-1))
            pb.wait("vector", "q_gpsimd", gv)
            pb.op("vector", lambda E: E.tensor_copy(dclf[:], dcli[:]))
        for s in range(-1, 4):
            Tm[s] = sbuf(pb, "tm_s%d" % (s + 1), (128, 512))
            pb.op("vector", lambda E, s=s: E.tensor_scalar(Tm[s][:], dclf[:],
                                                           float(32 + 128 * s), None, ALU.is_ge))
        for s in range(0, 5):
            Tl[s] = sbuf(pb, "tl_s%d" % s, (128, 512))
            mv = pb.op("vector", lambda E, s=s: E.tensor_scalar(Tl[s][:], dclf[:],
                                                                float(-32 + 128 * s), None,
                                                                ALU.is_le))
        pb.wait("tensor", "q_vector", mv)
        pb.wait("sync", "q_vector", mv)

        with pb.scope():
            xc = sbuf(pb, "p0xc", (128, D))
            x2c = sbuf(pb, "p0x2", (128, D))
            hc = sbuf(pb, "p0hc", (128, D))
            stats = sbuf(pb, "p0st", (128, 8))
            for c in range(NC):
                r0 = c * 128
                pb.wait("sync", "q_vector", pb.tick("vector"))
                pb.wait("sync", "q_scalar", pb.tick("scalar"))
                pb.wait("sync", "x4st", pb.cur("x4st"))
                pb.wait("sync", "hTst", pb.cur("hTst"))
                lv1 = pb.dma("sync", xc[:], x3[r0:r0 + 128, :], "xl1")
                pb.wait("vector", "xl1", lv1)
                pb.wait("scalar", "xl1", lv1)
                if pa is not None:
                    lv2 = pb.dma("sync", x2c[:], pa[r0:r0 + 128, :], "xl2")
                    pb.wait("vector", "xl2", lv2)
                    a1 = pb.op("vector", lambda E: E.tensor_tensor(xc[:], xc[:], x2c[:],
                                                                   ALU.add))
                    pb.wait("sync", "q_vector", a1)
                    lv3 = pb.dma("sync", x2c[:], pbx[r0:r0 + 128, :], "xl2")
                    pb.wait("vector", "xl2", lv3)
                    av = pb.op("vector", lambda E: E.tensor_tensor(xc[:], xc[:], x2c[:],
                                                                   ALU.add))
                    pb.wait("gpsimd", "q_vector", av)
                    pb.wait("scalar", "q_vector", av)
                    pb.dma("gpsimd", x4_d[r0:r0 + 128, :], xc[:], "x4st")
                emit_ln_chunk(pb, xc[:], hc[:], stats, hc[:])
                pb.wait("tensor", "q_vector", pb.tick("vector"))
                pb.wait("tensor", "hTst", pb.cur("hTst"))
                for dch in range(8):
                    bank = p[dch % 2]
                    tv = pb.op("tensor", lambda E, bank=bank, dch=dch:
                               E.transpose(bank[:, 0:128], hc[:, dch * 128:(dch + 1) * 128],
                                           idn[:]))
                    pb.wait("vector", "q_tensor", tv)
                    ev = pb.op("vector", lambda E, bank=bank, dch=dch:
                               E.tensor_copy(x2c[:, dch * 128:(dch + 1) * 128], bank[:, 0:128]))
                    pb.wait("tensor", "q_vector", ev)
                    pb.wait("gpsimd", "q_vector", ev)
                    pb.dma("gpsimd", hT_d[dch * 128:(dch + 1) * 128, r0:r0 + 128],
                           x2c[:, dch * 128:(dch + 1) * 128], "hTst")
        pb.phase()

        with pb.scope():
            wsl = sbuf(pb, "p1w", (128, 8, 512))
            htb = sbuf(pb, "p1h", (128, 2, 8, 128))
            ev1 = sbuf(pb, "p1e", (128, 2, 512))
            tsb = sbuf(pb, "p1t", (128, 2, 128))
            for ft in range(3 * QT):
                pb.wait("sync", "q_tensor", pb.tick("tensor"))
                pb.dma("sync", wsl[:],
                       wqkvT[:, ft * 512:(ft + 1) * 512].rearrange("(c p) f -> p c f", p=128),
                       "p1wl")
                pb.wait("tensor", "p1wl", pb.cur("p1wl"))
                for tc in range(NC):
                    sl = tc % 2
                    pb.wait("sync", "q_tensor", pb.tick("tensor"))
                    pb.dma("sync", htb[:, sl, :, :],
                           hT_d.rearrange("(c p) n -> p c n", p=128)[:, :,
                                                                    tc * 128:(tc + 1) * 128],
                           "p1hl_s%d" % sl)
                    pb.wait("tensor", "p1hl_s%d" % sl, pb.cur("p1hl_s%d" % sl))
                    pb.wait("tensor", "q_vector", pb.tick("vector"))
                    bank = p[sl]
                    for k in range(8):
                        mmv = pb.op("tensor", lambda E, k=k, sl=sl, bank=bank:
                                    E.matmul(bank[:, 0:512], htb[:, sl, k, :], wsl[:, k, :],
                                             start=(k == 0), stop=(k == 7)))
                    pb.wait("vector", "q_tensor", mmv)
                    pb.wait("vector", "vst", pb.cur("vst"))
                    if ft < QT:
                        vv = pb.op("vector", lambda E, sl=sl, bank=bank:
                                   E.tensor_scalar(ev1[:, sl, :], bank[:, 0:512], 0.125, None,
                                                   ALU.mult))
                    else:
                        vv = pb.op("vector", lambda E, sl=sl, bank=bank:
                                   E.tensor_copy(ev1[:, sl, :], bank[:, 0:512]))
                    if ft >= 2 * QT:
                        pb.wait("gpsimd", "q_vector", vv)
                        pb.dma("gpsimd", vt_d[tc * 128:(tc + 1) * 128,
                                              (ft - 2 * QT) * 512:(ft - 2 * QT + 1) * 512],
                               ev1[:, sl, :], "vst")
                    else:
                        pb.wait("tensor", "q_vector", vv)
                        for sub in range(4):
                            tb = p[2 + sub % 2]
                            pb.wait("tensor", "q_vector", pb.tick("vector"))
                            tv = pb.op("tensor", lambda E, sl=sl, sub=sub, tb=tb:
                                       E.transpose(tb[:, 0:128],
                                                   ev1[:, sl, sub * 128:(sub + 1) * 128],
                                                   idn[:]))
                            pb.wait("vector", "q_tensor", tv)
                            pb.wait("vector", "qkst_s%d" % (sub % 2),
                                    pb.cur("qkst_s%d" % (sub % 2)))
                            ev = pb.op("vector", lambda E, sub=sub, tb=tb:
                                       E.tensor_copy(tsb[:, sub % 2, :], tb[:, 0:128]))
                            pb.wait("gpsimd", "q_vector", ev)
                            pb.dma("gpsimd",
                                   qkT_d[ft * 512 + sub * 128:ft * 512 + (sub + 1) * 128,
                                         tc * 128:(tc + 1) * 128],
                                   tsb[:, sub % 2, :], "qkst_s%d" % (sub % 2))
        pb.phase()

        with pb.scope():
            kfm = sbuf(pb, "kfm", (64, N))
            qfm = sbuf(pb, "qfm", (64, NQ))
            vtm = sbuf(pb, "vtm", (128, NC, 64))
            rpk = sbuf(pb, "rpk", (64, NB))
            rpq = sbuf(pb, "rpq", (64, NB))
            vaug = sbuf(pb, "vaug", (65, 64))
            qex_lo = sbuf(pb, "qex_lo", (2, NQ))
            qex_hi = sbuf(pb, "qex_hi", (2, NQ))
            kex_lo = sbuf(pb, "kex_lo", (2, N))
            kex_hi = sbuf(pb, "kex_hi", (2, N))
            dq = sbuf(pb, "dq", (128, NCQ))
            dkr = sbuf(pb, "dkr", (1, N))
            qld = sbuf(pb, "qld", (128, NCQ, NB))
            srow = sbuf(pb, "srow_sc", (128, SW))
            qlt = sbuf(pb, "qlt", (128, NB))
            qcols = sbuf(pb, "qcols", (128, 5))
            bt = sbuf(pb, "bt", (128, NB))
            kbT = sbuf(pb, "kbT", (NB, 128))
            kb = sbuf(pb, "kb", (128, NB))
            tmp5 = sbuf(pb, "tmp5", (128, 512))
            mxs = sbuf(pb, "mxs", (128, 8))
            rss = sbuf(pb, "rss", (128, NJT))
            scc = sbuf(pb, "scc", (128, 8))
            pext = sbuf(pb, "pext", (128, 65))
            ptx = sbuf(pb, "ptx", (65, 128))
            ptile = sbuf(pb, "ptile", (128, 2, 128))
            ctxs = sbuf(pb, "ctxs", (128, 64))
            ctxT = sbuf(pb, "ctxT", (64, 128))

            def cls_tile(i0, j0):
                if j0 + 511 <= i0 - 32:
                    return ("lo", None)
                if j0 >= i0 + 159:
                    return ("hi", None)
                return ("st", (i0 - j0) // 128)

            for h in range(H):
                pb.phase()
                pb.dma("sync", kfm[:], qkT_d[HQ + h * 64:HQ + (h + 1) * 64, :], "hkl")
                pb.dma("sync", qfm[:], qkT_d[h * 64:(h + 1) * 64, q0:q0 + NQ], "hkl")
                pb.dma("sync", vtm[:],
                       vt_d.rearrange("(c p) f -> p c f", p=128)[:, :, h * 64:(h + 1) * 64], "hkl")
                pb.dma("sync", rpk[:], rpek[h], "hkl")
                pb.dma("sync", rpq[:], rpeq[h], "hkl")
                pb.dma("sync", vaug[0:63, :], rpev[h], "hkl")
                pb.dma("sync", vaug[63:64, :], rpev[h, 0:1, :], "hkl")
                lv = pb.dma("sync", vaug[64:65, :], rpev[h, 62:63, :], "hkl")
                pb.wait("tensor", "hkl", lv)
                pb.wait("vector", "hkl", lv)
                for ci in range(NCQ):
                    mmv = pb.op("tensor", lambda E, ci=ci:
                                E.matmul(p[5][:, 0:NB], qfm[:, ci * 128:(ci + 1) * 128], rpk[:],
                                         start=True, stop=True))
                    pb.wait("vector", "q_tensor", mmv)
                    vv = pb.op("vector", lambda E: E.tensor_copy(qlt[:], p[5][:, 0:NB]))
                    pb.op("vector", lambda E, ci=ci:
                          E.tensor_scalar(qld[:, ci, :], qlt[:], qlt[:, 0:1], None, ALU.subtract))
                    pb.op("vector", lambda E, ci=ci:
                          E.tensor_tensor(dq[:, ci:ci + 1], qlt[:, 62:63], qlt[:, 0:1],
                                          ALU.subtract))
                    pb.op("vector", lambda E: E.tensor_copy(qcols[:, 0:1], onescol[:]))
                    pb.op("vector", lambda E: E.tensor_copy(qcols[:, 1:2], qlt[:, 0:1]))
                    pb.op("vector", lambda E: E.tensor_copy(qcols[:, 2:3], onescol[:]))
                    vv = pb.op("vector", lambda E: E.tensor_copy(qcols[:, 3:4], qlt[:, 62:63]))
                    pb.wait("tensor", "q_vector", vv)
                    tv = pb.op("tensor", lambda E: E.transpose(p[4][0:2, 0:128], qcols[:, 0:2],
                                                               idn[:]))
                    tv = pb.op("tensor", lambda E: E.transpose(p[4][0:2, 128:256], qcols[:, 2:4],
                                                               idn[:]))
                    pb.wait("vector", "q_tensor", tv)
                    pb.op("vector", lambda E, ci=ci:
                          E.tensor_copy(qex_lo[0:2, ci * 128:(ci + 1) * 128], p[4][0:2, 0:128]))
                    vv = pb.op("vector", lambda E, ci=ci:
                               E.tensor_copy(qex_hi[0:2, ci * 128:(ci + 1) * 128],
                                             p[4][0:2, 128:256]))
                    pb.wait("tensor", "q_vector", vv)
                for cj in range(NC):
                    mmv = pb.op("tensor", lambda E, cj=cj:
                                E.matmul(p[5][:, 0:NB], kfm[:, cj * 128:(cj + 1) * 128], rpq[:],
                                         start=True, stop=True))
                    pb.wait("vector", "q_tensor", mmv)
                    pb.wait("vector", "kldst", pb.cur("kldst"))
                    vv = pb.op("vector", lambda E: E.tensor_scalar(qlt[:], p[5][:, 0:NB], 0.125,
                                                                   None, ALU.mult))
                    pb.op("vector", lambda E: E.tensor_scalar(bt[:], qlt[:], qlt[:, 0:1], None,
                                                              ALU.subtract))
                    pb.op("vector", lambda E: E.tensor_copy(qcols[:, 0:1], qlt[:, 0:1]))
                    pb.op("vector", lambda E: E.tensor_copy(qcols[:, 1:2], onescol[:]))
                    pb.op("vector", lambda E: E.tensor_copy(qcols[:, 2:3], qlt[:, 62:63]))
                    pb.op("vector", lambda E: E.tensor_copy(qcols[:, 3:4], onescol[:]))
                    vv = pb.op("vector", lambda E: E.tensor_tensor(qcols[:, 4:5], qlt[:, 62:63],
                                                                   qlt[:, 0:1], ALU.subtract))
                    pb.wait("tensor", "q_vector", vv)
                    tv = pb.op("tensor", lambda E: E.transpose(p[4][0:63, 0:128], bt[:], idn[:]))
                    tv = pb.op("tensor", lambda E: E.transpose(p[4][0:2, 128:256], qcols[:, 0:2],
                                                               idn[:]))
                    tv = pb.op("tensor", lambda E: E.transpose(p[4][0:2, 256:384], qcols[:, 2:4],
                                                               idn[:]))
                    tv = pb.op("tensor", lambda E: E.transpose(p[4][0:1, 384:512], qcols[:, 4:5],
                                                               idn[:]))
                    pb.wait("vector", "q_tensor", tv)
                    vv = pb.op("vector", lambda E: E.tensor_copy(kbT[:, :], p[4][0:63, 0:128]))
                    pb.op("vector", lambda E, cj=cj:
                          E.tensor_copy(kex_lo[0:2, cj * 128:(cj + 1) * 128], p[4][0:2, 128:256]))
                    pb.op("vector", lambda E, cj=cj:
                          E.tensor_copy(kex_hi[0:2, cj * 128:(cj + 1) * 128], p[4][0:2, 256:384]))
                    vv = pb.op("vector", lambda E, cj=cj:
                               E.tensor_copy(dkr[0:1, cj * 128:(cj + 1) * 128], p[4][0:1, 384:512]))
                    pb.wait("gpsimd", "q_vector", vv)
                    pb.wait("tensor", "q_vector", vv)
                    pb.dma("gpsimd", kldbm_d[:, 64 + cj * 128:64 + (cj + 1) * 128], kbT[:, :],
                           "kldst")
                for ci in range(NCQ):
                    i0 = q0 + ci * 128
                    kinds = [cls_tile(i0, jt * 512) for jt in range(NJT)]
                    pb.wait("vector", "q_tensor", pb.tick("tensor"))
                    for jt in range(NJT):
                        kind, s = kinds[jt]
                        bank = p[jt % 2]
                        qx = qex_hi if kind == "hi" else qex_lo
                        kx = kex_hi if kind == "hi" else kex_lo
                        pb.op("tensor", lambda E, jt=jt, ci=ci, bank=bank:
                              E.matmul(bank[:, 0:512], qfm[:, ci * 128:(ci + 1) * 128],
                                       kfm[:, jt * 512:(jt + 1) * 512], start=True, stop=False))
                        mmv = pb.op("tensor", lambda E, jt=jt, ci=ci, bank=bank, qx=qx, kx=kx:
                                    E.matmul(bank[:, 0:512], qx[0:2, ci * 128:(ci + 1) * 128],
                                             kx[0:2, jt * 512:(jt + 1) * 512], start=False,
                                             stop=True))
                        pb.wait("vector", "q_tensor", mmv)
                        vv = pb.op("vector", lambda E, jt=jt, bank=bank:
                                   E.tensor_copy(srow[:, 64 + jt * 512:64 + (jt + 1) * 512],
                                                 bank[:, 0:512]))
                    for jt in range(NJT):
                        kind, s = kinds[jt]
                        if kind != "st" or 32 + 128 * s > 511:
                            continue
                        sl = slice(64 + jt * 512, 64 + (jt + 1) * 512)
                        pb.op("vector", lambda E, sl=sl, s=s, ci=ci:
                              E.scalar_tensor_tensor(srow[:, sl], Tm[s][:], dq[:, ci:ci + 1],
                                                     srow[:, sl], ALU.mult, ALU.add))
                        pb.wait("tensor", "q_vector", pb.tick("vector"))
                        mmv = pb.op("tensor", lambda E, jt=jt:
                                    E.matmul(p[5][:, 0:512], ones1[:, 0:128],
                                             dkr[0:1, jt * 512:(jt + 1) * 512], start=True,
                                             stop=True))
                        pb.wait("vector", "q_tensor", mmv)
                        pb.op("vector", lambda E, s=s: E.tensor_tensor(tmp5[:], p[5][:, 0:512],
                                                                       Tm[s][:], ALU.mult))
                        pb.op("vector", lambda E, sl=sl: E.tensor_tensor(srow[:, sl], srow[:, sl],
                                                                         tmp5[:], ALU.add))
                    vv = pb.tick("vector")
                    pb.wait("sync", "q_vector", vv)
                    pb.wait("sync", "bre", pb.cur("bre"))
                    pb.wait("sync", "pxd", pb.cur("pxd"))
                    bw = pb.dma("sync", band_d[:, :], srow[:, i0:i0 + 256], "bwd")
                    pb.wait("sync", "bwd", bw)
                    sh_r = bass.AP(tensor=band_d.tensor, offset=33, ap=[[257, 128], [1, 63]])
                    br = pb.dma("sync", bt[:], sh_r, "brd")
                    sh_k = bass.AP(tensor=kldbm_d.tensor, offset=33 + i0,
                                   ap=[[N + 129, 63], [1, 128]])
                    pb.wait("sync", "q_vector", pb.tick("vector"))
                    pb.wait("sync", "kldst", pb.cur("kldst"))
                    kl = pb.dma("sync", kbT[:, :], sh_k, "kbl")
                    pb.wait("tensor", "kbl", kl)
                    pb.wait("tensor", "q_vector", pb.tick("vector"))
                    tv = pb.op("tensor", lambda E: E.transpose(p[4][:, 0:63], kbT[:, :],
                                                               idn[0:63, 0:63]))
                    pb.wait("vector", "q_tensor", tv)
                    pb.op("vector", lambda E: E.tensor_copy(kb[:], p[4][:, 0:63]))
                    pb.wait("vector", "brd", br)
                    pb.op("vector", lambda E, ci=ci: E.tensor_tensor(bt[:], bt[:], qld[:, ci, :],
                                                                     ALU.add))
                    vv = pb.op("vector", lambda E: E.tensor_tensor(bt[:], bt[:], kb[:], ALU.add))
                    pb.wait("sync", "q_vector", vv)
                    sh_w = bass.AP(tensor=band_d.tensor, offset=33, ap=[[257, 128], [1, 63]])
                    bwr = pb.dma("sync", sh_w, bt[:], "bwr")
                    pb.wait("sync", "bwr", bwr)
                    bre = pb.dma("sync", srow[:, i0:i0 + 256], band_d[:, :], "bre")
                    pb.wait("vector", "bre", bre)
                    pb.wait("scalar", "bre", bre)
                    for jt in range(NJT):
                        pb.op("vector", lambda E, jt=jt:
                              E.reduce_max(mxs[:, jt:jt + 1],
                                           srow[:, 64 + jt * 512:64 + (jt + 1) * 512], axis=AX.X))
                    pb.op("vector", lambda E: E.reduce_max(scc[:, 0:1], mxs[:, 0:NJT], axis=AX.X))
                    mv = pb.op("vector", lambda E: E.tensor_scalar(scc[:, 1:2], scc[:, 0:1], -1.0,
                                                                   None, ALU.mult))
                    pb.wait("scalar", "q_vector", mv)
                    for jt in range(NJT):
                        sv = pb.op("scalar", lambda E, jt=jt:
                                   E.activation(srow[:, 64 + jt * 512:64 + (jt + 1) * 512],
                                                srow[:, 64 + jt * 512:64 + (jt + 1) * 512],
                                                AF.Exp, bias=scc[:, 1:2],
                                                accum_out=rss[:, jt:jt + 1]))
                    pb.wait("vector", "q_scalar", sv)
                    pb.op("vector", lambda E: E.reduce_sum(scc[:, 2:3], rss[:, 0:NJT], axis=AX.X))
                    rv = pb.op("vector", lambda E: E.reciprocal(scc[:, 3:4], scc[:, 2:3]))
                    if i0 == 0:
                        rv = pb.op("vector", lambda E: E.memset(srow[:, 0:64], 0.0))
                    if i0 == N - 128:
                        rv = pb.op("vector", lambda E: E.memset(srow[:, 64 + N:64 + N + 64], 0.0))
                    pb.op("vector", lambda E: E.memset(scc[:, 4:5], 0.0))
                    pb.op("vector", lambda E: E.memset(scc[:, 5:6], 0.0))
                    for jt in range(NJT):
                        kind, s = kinds[jt]
                        if kind == "lo":
                            pb.op("vector", lambda E, jt=jt:
                                  E.tensor_tensor(scc[:, 4:5], scc[:, 4:5], rss[:, jt:jt + 1],
                                                  ALU.add))
                        elif kind == "hi":
                            pb.op("vector", lambda E, jt=jt:
                                  E.tensor_tensor(scc[:, 5:6], scc[:, 5:6], rss[:, jt:jt + 1],
                                                  ALU.add))
                        else:
                            sl = slice(64 + jt * 512, 64 + (jt + 1) * 512)
                            if -32 + 128 * s >= -127:
                                pb.op("vector", lambda E, sl=sl, s=s:
                                      E.tensor_tensor(tmp5[:], srow[:, sl], Tl[s][:], ALU.mult))
                                pb.op("vector", lambda E: E.reduce_sum(scc[:, 6:7], tmp5[:],
                                                                       axis=AX.X))
                                pb.op("vector", lambda E: E.tensor_tensor(scc[:, 4:5], scc[:, 4:5],
                                                                          scc[:, 6:7], ALU.add))
                            if 32 + 128 * s <= 511:
                                pb.op("vector", lambda E, sl=sl, s=s:
                                      E.tensor_tensor(tmp5[:], srow[:, sl], Tm[s][:], ALU.mult))
                                pb.op("vector", lambda E: E.reduce_sum(scc[:, 6:7], tmp5[:],
                                                                       axis=AX.X))
                                pb.op("vector", lambda E: E.tensor_tensor(scc[:, 5:6], scc[:, 5:6],
                                                                          scc[:, 6:7], ALU.add))
                    vv = pb.tick("vector")
                    pb.wait("sync", "q_vector", vv)
                    pb.wait("sync", "q_scalar", pb.tick("scalar"))
                    px = pb.dma("sync", band_d[:, :], srow[:, i0:i0 + 256], "pxd")
                    pb.wait("sync", "pxd", px)
                    pxr = pb.dma("sync", pext[:, 0:63], sh_r, "pxr")
                    pb.wait("vector", "pxr", pxr)
                    pb.op("vector", lambda E: E.tensor_copy(pext[:, 63:64], scc[:, 4:5]))
                    vv = pb.op("vector", lambda E: E.tensor_copy(pext[:, 64:65], scc[:, 5:6]))
                    pb.wait("tensor", "q_vector", vv)
                    tv = pb.op("tensor", lambda E: E.transpose(p[4][0:65, 0:128], pext[:], idn[:]))
                    pb.wait("vector", "q_tensor", tv)
                    vv = pb.op("vector", lambda E: E.tensor_copy(ptx[:, :], p[4][0:65, 0:128]))
                    pb.wait("tensor", "q_vector", vv)
                    for jc in range(NC):
                        tb = p[2 + jc % 2]
                        tv = pb.op("tensor", lambda E, jc=jc, tb=tb:
                                   E.transpose(tb[:, 0:128], srow[:, 64 + jc * 128:64 + (jc + 1) * 128],
                                               idn[:]))
                        pb.wait("vector", "q_tensor", tv)
                        ev = pb.op("vector", lambda E, jc=jc, tb=tb:
                                   E.tensor_copy(ptile[:, jc % 2, :], tb[:, 0:128]))
                        pb.wait("tensor", "q_vector", ev)
                        pb.op("tensor", lambda E, jc=jc:
                              E.matmul(p[7][:, 0:64], ptile[:, jc % 2, :], vtm[:, jc, :],
                                       start=(jc == 0), stop=False))
                    mmv = pb.op("tensor", lambda E: E.matmul(p[7][:, 0:64], ptx[:, :], vaug[:],
                                                             start=False, stop=True))
                    pb.wait("vector", "q_tensor", mmv)
                    vv = pb.op("vector", lambda E: E.tensor_scalar(ctxs[:], p[7][:, 0:64],
                                                                   scc[:, 3:4], None, ALU.mult))
                    pb.wait("tensor", "q_vector", vv)
                    tv = pb.op("tensor", lambda E: E.transpose(p[4][0:64, 0:128], ctxs[:], idn[:]))
                    pb.wait("vector", "q_tensor", tv)
                    pb.wait("vector", "ctxst", pb.cur("ctxst"))
                    vv = pb.op("vector", lambda E: E.tensor_copy(ctxT[:, :], p[4][0:64, 0:128]))
                    pb.wait("gpsimd", "q_vector", vv)
                    pb.dma("gpsimd", ctx_d[h * 64:(h + 1) * 64, ci * 128:(ci + 1) * 128],
                           ctxT[:, :], "ctxst")
        pb.phase()

        with pb.scope():
            wpsl = sbuf(pb, "p3w", (128, KCH, D))
            cxsl = sbuf(pb, "p3c", (128, 2, KCH, 128))
            x4c = sbuf(pb, "p3x", (128, 2, D))
            yo = sbuf(pb, "p3y", (128, 2, 512))
            bprow = sbuf(pb, "p3br", (1, D))
            pb.dma("sync", wpsl[:], wprojT.rearrange("(c p) o -> p c o", p=128), "p3wl")
            lv = pb.dma("sync", bprow[:], bproj.rearrange("(a o) -> a o", a=1), "p3wl")
            pb.wait("tensor", "p3wl", lv)
            pb.wait("vector", "p3wl", lv)
            bpb, _ = bcast_row(pb, "p3bb", bprow[:], D, 0, p[6], ones1)
            for tc in range(NCQ):
                sl = tc % 2
                pb.wait("sync", "q_tensor", pb.tick("tensor"))
                pb.wait("sync", "q_vector", pb.tick("vector"))
                pb.dma("sync", cxsl[:, sl, :, :],
                       ctx_d.rearrange("(c p) n -> p c n", p=128)[:, :, tc * 128:(tc + 1) * 128],
                       "p3cl_s%d" % sl)
                if resid:
                    x4read = x4_d if pa is not None else x3
                    lv = pb.dma("sync", x4c[:, sl, :],
                                x4read[q0 + tc * 128:q0 + (tc + 1) * 128, :], "p3xl_s%d" % sl)
                    pb.wait("vector", "p3xl_s%d" % sl, lv)
                pb.wait("tensor", "p3cl_s%d" % sl, pb.cur("p3cl_s%d" % sl))
                for ot in range(2):
                    bank = p[4 + ot]
                    for k in range(KCH):
                        mmv = pb.op("tensor", lambda E, k=k, sl=sl, ot=ot, bank=bank:
                                    E.matmul(bank[:, 0:512], cxsl[:, sl, k, :],
                                             wpsl[:, k, ot * 512:(ot + 1) * 512],
                                             start=(k == 0), stop=(k == 7)))
                    pb.wait("vector", "q_tensor", mmv)
                    pb.wait("vector", "yst_s%d" % ot, pb.cur("yst_s%d" % ot))
                    if resid:
                        pb.op("vector", lambda E, sl=sl, ot=ot, bank=bank:
                              E.tensor_tensor(yo[:, ot % 2, :], bank[:, 0:512],
                                              x4c[:, sl, ot * 512:(ot + 1) * 512], ALU.add))
                        vv = pb.op("vector", lambda E, sl=sl, ot=ot:
                                   E.tensor_tensor(yo[:, ot % 2, :], yo[:, ot % 2, :],
                                                   bpb[:, ot * 512:(ot + 1) * 512], ALU.add))
                    else:
                        vv = pb.op("vector", lambda E, sl=sl, ot=ot, bank=bank:
                                   E.tensor_tensor(yo[:, ot % 2, :], bank[:, 0:512],
                                                   bpb[:, ot * 512:(ot + 1) * 512], ALU.add))
                    pb.wait("gpsimd", "q_vector", vv)
                    pb.dma("gpsimd", y[tc * 128:(tc + 1) * 128, ot * 512:(ot + 1) * 512],
                           yo[:, ot % 2, :], "yst_s%d" % ot)
        pb.wait("gpsimd", "yst_s0", pb.cur("yst_s0"))
        pb.wait("gpsimd", "yst_s1", pb.cur("yst_s1"))
    pb.prefix = _oldpfx


def build_l1attn(N, NQ, q0):
    H, hd = NH, HD
    NC = N // 128
    NJT = N // 512
    NCQ = NQ // 128
    SW = 64 + N + 64
    nc = bass.Bass("TRN2", target_bir_lowering=False, debug=False)
    x3 = nc.dram_tensor("x3", [N, D], F32, kind="ExternalInput").ap()
    pa = nc.dram_tensor("pa", [N, D], F32, kind="ExternalInput").ap()
    pbx = nc.dram_tensor("pb", [N, D], F32, kind="ExternalInput").ap()
    wqkvT = nc.dram_tensor("wqkvT", [D, 3 * D], F32, kind="ExternalInput").ap()
    wprojT = nc.dram_tensor("wprojT", [D, D], F32, kind="ExternalInput").ap()
    bproj = nc.dram_tensor("bproj", [D], F32, kind="ExternalInput").ap()
    rpeq = nc.dram_tensor("rpeq", [H, hd, NB], F32, kind="ExternalInput").ap()
    rpek = nc.dram_tensor("rpek", [H, hd, NB], F32, kind="ExternalInput").ap()
    rpev = nc.dram_tensor("rpev", [H, NB, hd], F32, kind="ExternalInput").ap()
    y = nc.dram_tensor("y", [NQ, D], F32, kind="ExternalOutput").ap()
    est = contextlib.ExitStack()
    with est:
        pb = PB(nc, est)
        idn = make_identity(pb)
        p = psum8(pb)
        emit_l1attn(pb, p, idn, x3, pa, pbx, wqkvT, wprojT, bproj, rpeq, rpek, rpev, y, N, NQ, q0)
        pb.build()
    return nc


def build_add3(T):
    nc = bass.Bass("TRN2", target_bir_lowering=False, debug=False)
    a = nc.dram_tensor("a", [T, D], F32, kind="ExternalInput").ap()
    b = nc.dram_tensor("b", [T, D], F32, kind="ExternalInput").ap()
    c = nc.dram_tensor("c", [T, D], F32, kind="ExternalInput").ap()
    y = nc.dram_tensor("y", [T, D], F32, kind="ExternalOutput").ap()
    est = contextlib.ExitStack()
    with est:
        pb = PB(nc, est)
        xc = sbuf(pb, "xc", (128, 2, D))
        x2 = sbuf(pb, "x2", (128, 2, D))
        for ch in range(T // 128):
            r0 = ch * 128
            sl = ch % 2
            pb.wait("sync", "q_vector", pb.tick("vector"))
            pb.wait("sync", "yst_s%d" % sl, pb.cur("yst_s%d" % sl))
            l1 = pb.dma("sync", xc[:, sl, :], a[r0:r0 + 128, :], "al_s%d" % sl)
            l2 = pb.dma("sync", x2[:, sl, :], b[r0:r0 + 128, :], "bl_s%d" % sl)
            pb.wait("vector", "al_s%d" % sl, l1)
            pb.wait("vector", "bl_s%d" % sl, l2)
            a1 = pb.op("vector", lambda E, sl=sl: E.tensor_tensor(xc[:, sl, :], xc[:, sl, :],
                                                                  x2[:, sl, :], ALU.add))
            pb.wait("sync", "q_vector", a1)
            l3 = pb.dma("sync", x2[:, sl, :], c[r0:r0 + 128, :], "bl_s%d" % sl)
            pb.wait("vector", "bl_s%d" % sl, l3)
            vv = pb.op("vector", lambda E, sl=sl: E.tensor_tensor(xc[:, sl, :], xc[:, sl, :],
                                                                  x2[:, sl, :], ALU.add))
            pb.wait("gpsimd", "q_vector", vv)
            pb.dma("gpsimd", y[r0:r0 + 128, :], xc[:, sl, :], "yst_s%d" % sl)
        pb.wait("gpsimd", "yst_s0", pb.cur("yst_s0"))
        pb.wait("gpsimd", "yst_s1", pb.cur("yst_s1"))
        pb.build()
    return nc


_CACHE = {}


def _get(key, builder, *args):
    if key not in _CACHE:
        _CACHE[key] = builder(*args)
    return _CACHE[key]


def run_l0ffn(x_slices, w):
    T = x_slices[0].shape[0]
    nc = _get(("l0ffn", T), build_l0ffn, T)
    in_maps = [{"x": np.ascontiguousarray(xs),
                "U1": w["L0_ffn_U1"], "V1": w["L0_ffn_V1"], "b1": w["L0_ffn_b1"],
                "U2": w["L0_ffn_U2"], "V2": w["L0_ffn_V2"], "b2": w["L0_ffn_b2"]}
               for xs in x_slices]
    res = run_bass_kernel_spmd(nc, in_maps, core_ids=list(range(len(in_maps))))
    return [r["y"] for r in res.results]


def _l0attn_map(xslice, w):
    return {"x": np.ascontiguousarray(xslice),
            "Wq": w["L0_attn_Wq"], "Wk": w["L0_attn_Wk"], "Wv": w["L0_attn_Wv"],
            "Wo": w["L0_attn_Wo"], "W1": w["L0_attn_W1"], "W2": w["L0_attn_W2"],
            "bq": w["L0_attn_bq"], "bk": w["L0_attn_bk"], "bv": w["L0_attn_bv"],
            "bo": w["L0_attn_bo"], "b1": w["L0_attn_b1"], "b2": w["L0_attn_b2"],
            "lnp": np.stack([w["L0_attn_ln1_w"], w["L0_attn_ln1_b"],
                             w["L0_attn_ln2_w"], w["L0_attn_ln2_b"]])}


def run_l0attn(x1, w):
    nc0 = _get(("l0attn", 0), build_l0attn, HALF, 0)
    nc1 = _get(("l0attn", 128), build_l0attn, HALF, 128)
    m0 = [_l0attn_map(x1[b, 0:HALF], w) for b in range(B)]
    m1 = [_l0attn_map(x1[b, HALF - 128:SEQ], w) for b in range(B)]
    r0 = run_bass_kernel_spmd(nc0, m0, core_ids=[0, 1, 2, 3])
    r1 = run_bass_kernel_spmd(nc1, m1, core_ids=[4, 5, 6, 7])
    x2 = np.empty_like(x1)
    for b in range(B):
        x2[b, 0:HALF] = r0.results[b]["y"]
        x2[b, HALF:SEQ] = r1.results[b]["y"]
    return x2


_MOE_PERM = {0: [0, 1, 2, 3, 4, 5, 6, 7], 1: [4, 5, 6, 7, 0, 1, 2, 3]}


def run_moe(x, w):
    nc = _get(("moe",), build_moe)
    maps = []
    for b in range(B):
        for eg in range(2):
            sl = slice(0, 4) if eg == 0 else slice(4, 8)
            maps.append({"x": np.ascontiguousarray(x[b]),
                         "gw": np.ascontiguousarray(w["L1_ffn_gate_w"][:, _MOE_PERM[eg]]),
                         "Wg": np.ascontiguousarray(w["L1_ffn_Wg"][sl]),
                         "Wu": np.ascontiguousarray(w["L1_ffn_Wu"][sl]),
                         "Wd": np.ascontiguousarray(w["L1_ffn_Wd"][sl])})
    res = run_bass_kernel_spmd(nc, maps, core_ids=list(range(8)))
    return [[res.results[b * 2 + eg]["y"] for eg in range(2)] for b in range(B)]


def run_l1attn(x3, parts, w):
    wqkvT = np.ascontiguousarray(w["L1_attn_w_qkv"].T)
    wprojT = np.ascontiguousarray(w["L1_attn_w_proj"].T)
    rpeqR = np.ascontiguousarray(w["L1_attn_rpe_q_table"][:, :, ::-1])
    x5 = np.empty_like(x3)
    for hi, q0 in enumerate((0, HALF)):
        nc = _get(("l1attn", q0), build_l1attn, SEQ, HALF, q0)
        maps = [{"x3": np.ascontiguousarray(x3[b]),
                 "pa": np.ascontiguousarray(parts[b][0]),
                 "pb": np.ascontiguousarray(parts[b][1]),
                 "wqkvT": wqkvT, "wprojT": wprojT, "bproj": w["L1_attn_b_proj"],
                 "rpeq": rpeqR, "rpek": w["L1_attn_rpe_k_table"],
                 "rpev": w["L1_attn_rpe_v_table"]} for b in range(B)]
        res = run_bass_kernel_spmd(nc, maps, core_ids=[0 + 4 * hi, 1 + 4 * hi,
                                                       2 + 4 * hi, 3 + 4 * hi])
        for b in range(B):
            x5[b, q0:q0 + HALF] = res.results[b]["y"]
    return x5


def run_add3(x5, parts):
    nc = _get(("add3",), build_add3, HALF)
    maps = []
    for b in range(B):
        for h in range(2):
            sl = slice(h * HALF, (h + 1) * HALF)
            maps.append({"a": np.ascontiguousarray(x5[b, sl]),
                         "b": np.ascontiguousarray(parts[b][0][sl]),
                         "c": np.ascontiguousarray(parts[b][1][sl])})
    res = run_bass_kernel_spmd(nc, maps, core_ids=list(range(8)))
    out = np.empty_like(x5)
    for b in range(B):
        for h in range(2):
            out[b, h * HALF:(h + 1) * HALF] = res.results[b * 2 + h]["y"]
    return out


def emit_combine(pb, a, b, c, dst, T, tag=""):
    _oldpfx = pb.prefix
    pb.prefix = _oldpfx + tag
    with pb.scope():
        xc = sbuf(pb, "cxc", (128, 2, D))
        x2 = sbuf(pb, "cx2", (128, 2, D))
        for ch in range(T // 128):
            r0 = ch * 128
            sl = ch % 2
            pb.wait("sync", "q_vector", pb.tick("vector"))
            pb.wait("sync", "cyst_s%d" % sl, pb.cur("cyst_s%d" % sl))
            l1 = pb.dma("sync", xc[:, sl, :], a[r0:r0 + 128, :], "cal_s%d" % sl)
            l2 = pb.dma("sync", x2[:, sl, :], b[r0:r0 + 128, :], "cbl_s%d" % sl)
            pb.wait("vector", "cal_s%d" % sl, l1)
            pb.wait("vector", "cbl_s%d" % sl, l2)
            a1 = pb.op("vector", lambda E, sl=sl: E.tensor_tensor(xc[:, sl, :], xc[:, sl, :],
                                                                  x2[:, sl, :], ALU.add))
            pb.wait("sync", "q_vector", a1)
            l3 = pb.dma("sync", x2[:, sl, :], c[r0:r0 + 128, :], "cbl_s%d" % sl)
            pb.wait("vector", "cbl_s%d" % sl, l3)
            vv = pb.op("vector", lambda E, sl=sl: E.tensor_tensor(xc[:, sl, :], xc[:, sl, :],
                                                                  x2[:, sl, :], ALU.add))
            pb.wait("gpsimd", "q_vector", vv)
            pb.dma("gpsimd", dst[r0:r0 + 128, :], xc[:, sl, :], "cyst_s%d" % sl)
        pb.wait("gpsimd", "cyst_s0", pb.cur("cyst_s0"))
        pb.wait("gpsimd", "cyst_s1", pb.cur("cyst_s1"))
    pb.prefix = _oldpfx


def emit_cc_allgather(pb, ci, co):
    pb.phase()
    pb.op("gpsimd", lambda E: E.collective_compute(
        "AllGather", ALU.bypass, ins=[ci[:, :]], outs=[co[:, :, :]],
        replica_groups=[[0, 1], [2, 3], [4, 5], [6, 7]]))
    pb.phase()


def build_mega():
    nc = bass.Bass("TRN2", target_bir_lowering=False, debug=False)
    x0 = nc.dram_tensor("x0", [SEQ, D], F32, kind="ExternalInput").ap()
    fU1 = nc.dram_tensor("fU1", [D, RANK], F32, kind="ExternalInput").ap()
    fV1 = nc.dram_tensor("fV1", [RANK, DFF], F32, kind="ExternalInput").ap()
    fb1 = nc.dram_tensor("fb1", [DFF], F32, kind="ExternalInput").ap()
    fU2 = nc.dram_tensor("fU2", [DFF, RANK], F32, kind="ExternalInput").ap()
    fV2 = nc.dram_tensor("fV2", [RANK, D], F32, kind="ExternalInput").ap()
    fb2 = nc.dram_tensor("fb2", [D], F32, kind="ExternalInput").ap()
    aWq = nc.dram_tensor("aWq", [D, D], F32, kind="ExternalInput").ap()
    aWk = nc.dram_tensor("aWk", [D, D], F32, kind="ExternalInput").ap()
    aWv = nc.dram_tensor("aWv", [D, D], F32, kind="ExternalInput").ap()
    aWo = nc.dram_tensor("aWo", [D, D], F32, kind="ExternalInput").ap()
    aW1 = nc.dram_tensor("aW1", [D, DFF], F32, kind="ExternalInput").ap()
    aW2 = nc.dram_tensor("aW2", [DFF, D], F32, kind="ExternalInput").ap()
    abq = nc.dram_tensor("abq", [D], F32, kind="ExternalInput").ap()
    abk = nc.dram_tensor("abk", [D], F32, kind="ExternalInput").ap()
    abv = nc.dram_tensor("abv", [D], F32, kind="ExternalInput").ap()
    abo = nc.dram_tensor("abo", [D], F32, kind="ExternalInput").ap()
    ab1 = nc.dram_tensor("ab1", [DFF], F32, kind="ExternalInput").ap()
    ab2 = nc.dram_tensor("ab2", [D], F32, kind="ExternalInput").ap()
    alnp = nc.dram_tensor("alnp", [4, D], F32, kind="ExternalInput").ap()
    mgw = nc.dram_tensor("mgw", [D, NE], F32, kind="ExternalInput").ap()
    mWg = nc.dram_tensor("mWg", [4, D, NI], F32, kind="ExternalInput").ap()
    mWu = nc.dram_tensor("mWu", [4, D, NI], F32, kind="ExternalInput").ap()
    mWd = nc.dram_tensor("mWd", [4, NI, D], F32, kind="ExternalInput").ap()
    HQ8 = 512
    wqkvT = nc.dram_tensor("wqkvT", [D, 3 * HQ8], F32, kind="ExternalInput").ap()
    wprojT = nc.dram_tensor("wprojT", [HQ8, D], F32, kind="ExternalInput").ap()
    bproj = nc.dram_tensor("bproj", [D], F32, kind="ExternalInput").ap()
    rpeq = nc.dram_tensor("rpeq", [8, HD, NB], F32, kind="ExternalInput").ap()
    rpek = nc.dram_tensor("rpek", [8, HD, NB], F32, kind="ExternalInput").ap()
    rpev = nc.dram_tensor("rpev", [8, NB, HD], F32, kind="ExternalInput").ap()
    y = nc.dram_tensor("y", [SEQ, D], F32, kind="ExternalOutput").ap()

    x1_d = nc.dram_tensor("x1_d", [SEQ, D], F32, kind="Internal").ap()
    x2_d = nc.dram_tensor("x2_d", [SEQ, D], F32, kind="Internal").ap()
    x3_d = nc.dram_tensor("x3_d", [SEQ, D], F32, kind="Internal").ap()
    x4_d = nc.dram_tensor("x4c_d", [SEQ, D], F32, kind="Internal").ap()
    x5_d = nc.dram_tensor("x5_d", [SEQ, D], F32, kind="Internal").ap()
    cc1i = nc.dram_tensor("cc1i", [SEQ, D], F32, kind="Internal").ap()
    cc1o = nc.dram_tensor("cc1o", [2, SEQ, D], F32, kind="Internal").ap()
    cc2i = nc.dram_tensor("cc2i", [SEQ, D], F32, kind="Internal").ap()
    cc2o = nc.dram_tensor("cc2o", [2, SEQ, D], F32, kind="Internal").ap()
    cc3i = nc.dram_tensor("cc3i", [SEQ, D], F32, kind="Internal").ap()
    cc3o = nc.dram_tensor("cc3o", [2, SEQ, D], F32, kind="Internal").ap()

    est = contextlib.ExitStack()
    with est:
        pb = PB(nc, est)
        idn = make_identity(pb)
        p = psum8(pb)
        emit_l0ffn(pb, p, idn, x0, fU1, fV1, fb1, fU2, fV2, fb2, x1_d, SEQ, tag="s1")
        pb.phase()
        emit_l0attn(pb, p, idn, x1_d[0:HALF, :], aWq, aWk, aWv, aWo, aW1, aW2,
                    abq, abk, abv, abo, ab1, ab2, alnp, x2_d[0:HALF, :],
                    HALF, 0, tag="s2a")
        pb.phase()
        emit_l0attn(pb, p, idn, x1_d[HALF - 128:SEQ, :], aWq, aWk, aWv, aWo, aW1, aW2,
                    abq, abk, abv, abo, ab1, ab2, alnp, x2_d[HALF:SEQ, :],
                    HALF, 128, tag="s2b")
        pb.phase()
        emit_l0ffn(pb, p, idn, x2_d, fU1, fV1, fb1, fU2, fV2, fb2, x3_d, SEQ, tag="s3")
        pb.phase()
        emit_moe(pb, p, idn, x3_d, mgw, mWg, mWu, mWd, cc1i, tag="s4")
        emit_cc_allgather(pb, cc1i, cc1o)
        emit_combine(pb, x3_d, cc1o[0], cc1o[1], x4_d, SEQ, tag="c1")
        pb.phase()
        emit_l1attn(pb, p, idn, x4_d, None, None, wqkvT, wprojT, bproj,
                    rpeq, rpek, rpev, cc2i, SEQ, SEQ, 0, tag="s5", HH=8, resid=False)
        emit_cc_allgather(pb, cc2i, cc2o)
        emit_combine(pb, x4_d, cc2o[0], cc2o[1], x5_d, SEQ, tag="c2")
        pb.phase()
        emit_moe(pb, p, idn, x5_d, mgw, mWg, mWu, mWd, cc3i, tag="s6")
        emit_cc_allgather(pb, cc3i, cc3o)
        emit_combine(pb, x5_d, cc3o[0], cc3o[1], y, SEQ, tag="c3")
        pb.phase()
        pb.build()
    return nc


def kernel(**inputs):
    w = inputs
    nc = _get(("mega",), build_mega)
    x0 = np.asarray(w["L0_attn_x"], dtype=np.float32)
    wqkvT_f = np.asarray(w["L1_attn_w_qkv"], dtype=np.float32).T
    wprojT_f = np.asarray(w["L1_attn_w_proj"], dtype=np.float32).T
    rpeqR = np.asarray(w["L1_attn_rpe_q_table"], dtype=np.float32)[:, :, ::-1]
    alnp = np.stack([w["L0_attn_ln1_w"], w["L0_attn_ln1_b"],
                     w["L0_attn_ln2_w"], w["L0_attn_ln2_b"]]).astype(np.float32)
    maps = []
    for b in range(B):
        for hg in range(2):
            esl = slice(0, 4) if hg == 0 else slice(4, 8)
            hsl = slice(hg * 8, hg * 8 + 8)
            csl = slice(hg * 512, (hg + 1) * 512)
            wqkvT_hg = np.concatenate([wqkvT_f[:, csl],
                                       wqkvT_f[:, 1024 + hg * 512:1024 + (hg + 1) * 512],
                                       wqkvT_f[:, 2048 + hg * 512:2048 + (hg + 1) * 512]], 1)
            bproj_hg = (np.asarray(w["L1_attn_b_proj"], np.float32) if hg == 0
                        else np.zeros(D, np.float32))
            maps.append({
                "x0": np.ascontiguousarray(x0[b]),
                "fU1": w["L0_ffn_U1"], "fV1": w["L0_ffn_V1"], "fb1": w["L0_ffn_b1"],
                "fU2": w["L0_ffn_U2"], "fV2": w["L0_ffn_V2"], "fb2": w["L0_ffn_b2"],
                "aWq": w["L0_attn_Wq"], "aWk": w["L0_attn_Wk"], "aWv": w["L0_attn_Wv"],
                "aWo": w["L0_attn_Wo"], "aW1": w["L0_attn_W1"], "aW2": w["L0_attn_W2"],
                "abq": w["L0_attn_bq"], "abk": w["L0_attn_bk"], "abv": w["L0_attn_bv"],
                "abo": w["L0_attn_bo"], "ab1": w["L0_attn_b1"], "ab2": w["L0_attn_b2"],
                "alnp": alnp,
                "mgw": np.ascontiguousarray(
                    np.asarray(w["L1_ffn_gate_w"], np.float32)[:, _MOE_PERM[hg]]),
                "mWg": np.ascontiguousarray(w["L1_ffn_Wg"][esl]),
                "mWu": np.ascontiguousarray(w["L1_ffn_Wu"][esl]),
                "mWd": np.ascontiguousarray(w["L1_ffn_Wd"][esl]),
                "wqkvT": np.ascontiguousarray(wqkvT_hg),
                "wprojT": np.ascontiguousarray(wprojT_f[csl, :]),
                "bproj": bproj_hg,
                "rpeq": np.ascontiguousarray(rpeqR[hsl]),
                "rpek": np.ascontiguousarray(w["L1_attn_rpe_k_table"][hsl]),
                "rpev": np.ascontiguousarray(w["L1_attn_rpe_v_table"][hsl]),
            })
    res = run_bass_kernel_spmd(nc, maps, core_ids=list(range(8)))
    out = np.empty((B, SEQ, D), np.float32)
    for b in range(B):
        out[b] = res.results[2 * b]["y"]
    return out


# version 03
# speedup vs baseline: 1.7937x; 1.2008x over previous
import math
import contextlib
import numpy as np
import concourse.bass as bass
import concourse.mybir as mybir
from concourse.bass_utils import run_bass_kernel_spmd

F32 = mybir.dt.float32
I32 = mybir.dt.int32
U32 = mybir.dt.uint32
BF16 = mybir.dt.bfloat16
AF = mybir.ActivationFunctionType
ALU = mybir.AluOpType
AX = mybir.AxisListType

D = 1024
SEQ = 4096
B = 4
HALF = 2048
WIN = 128
NH = 16
HD = 64
DFF = 4096
RANK = 256
NE = 8
NI = 2048
KSEL = 1024
NB = 63
LNEPS = float(1e-5)
NEG = float(-1e30)


class PB:
    ENGINES = ("sync", "scalar", "vector", "tensor", "gpsimd")
    SELF_SERIAL = ("scalar", "vector", "gpsimd")

    def __init__(self, nc, est):
        self.nc = nc
        self.ops = {e: [] for e in self.ENGINES}
        self.H = {}
        self.V = {}
        self.est = est
        self.scopes = [est]
        self.prefix = ""

    @contextlib.contextmanager
    def scope(self):
        st = contextlib.ExitStack()
        self.scopes.append(st)
        try:
            yield st
        finally:
            self.scopes.pop()
            st.close()

    def sem(self, name):
        if name not in self.H:
            self.H[name] = self.est.enter_context(self.nc.semaphore(name))
            self.V[name] = 0
        return name

    def bump(self, name, amt):
        self.sem(name)
        self.V[name] += amt
        return self.V[name]

    def cur(self, name):
        self.sem(name)
        return self.V[name]

    def emit(self, eng, fn):
        self.ops[eng].append(fn)

    def wait(self, eng, name, val=None):
        v = self.cur(name) if val is None else val
        h = self.H[name]
        if v > 0:
            self.emit(eng, lambda E, h=h, v=v: E.wait_ge(h, v))

    def op(self, eng, fn, chain=True):
        q = "q_" + eng
        if eng in self.SELF_SERIAL and chain and self.cur(q) > 0:
            self.wait(eng, q, self.cur(q))
        v = self.bump(q, 1)
        h = self.H[q]
        self.emit(eng, lambda E, fn=fn, h=h: fn(E).then_inc(h, 1))
        return v

    def tick(self, eng):
        return self.cur("q_" + eng)

    def phase(self, engines=None, dma_sems=True):
        engines = engines or self.ENGINES
        ticks = {e: self.cur("q_" + e) for e in self.ENGINES}
        dmas = [(n, self.V[n]) for n in list(self.H)
                if not n.startswith("q_") and self.V[n] > 0] if dma_sems else []
        for e in engines:
            for o in self.ENGINES:
                if o != e and ticks[o] > 0:
                    self.wait(e, "q_" + o, ticks[o])
            for n, v in dmas:
                self.wait(e, n, v)

    def dma(self, eng, out, in_, sem):
        v = self.bump(sem, 16)
        h = self.H[sem]
        self.emit(eng, lambda E, o=out, i=in_, h=h: E.dma_start(out=o, in_=i).then_inc(h, 16))
        return v

    def build(self):
        with self.nc.Block() as blk:
            for eng in self.ENGINES:
                ops = self.ops[eng]
                if not ops:
                    continue
                dec = getattr(blk, eng)

                def runner(E, ops=ops):
                    for f in ops:
                        f(E)
                dec(runner)


def sbuf(pb, name, shape, dt=F32):
    return pb.scopes[-1].enter_context(pb.nc.sbuf_tensor(pb.prefix + name, list(shape), dt))


def psum8(pb):
    return [pb.est.enter_context(pb.nc.psum_tensor("psum%d" % i, [128, 512], F32))
            for i in range(8)]


def make_identity(pb):
    di = sbuf(pb, "idn_i", (128, 128), I32)
    idn = sbuf(pb, "idn_f", (128, 128), F32)
    gv = pb.op("gpsimd", lambda E, di=di: E.iota(di[:], [[-1, 128]], base=0, channel_multiplier=1))
    pb.wait("vector", "q_gpsimd", gv)
    vv = pb.op("vector", lambda E, di=di, idn=idn: E.tensor_scalar(idn[:], di[:], 0, None, ALU.is_equal))
    pb.idn_ready = vv
    return idn


def load_bias_cols(pb, name, dram_vec, nch, idn, bank):
    tmp = sbuf(pb, name + "_r", (nch, 128))
    out = sbuf(pb, name, (128, nch))
    lv = pb.dma("sync", tmp[:], dram_vec.rearrange("(c p) -> c p", p=128), name + "_ld")
    pb.wait("tensor", name + "_ld", lv)
    pb.wait("tensor", "q_vector", pb.tick("vector"))
    tv = pb.op("tensor", lambda E: E.transpose(bank[0:128, 0:nch], tmp[:], idn[0:nch, 0:nch]))
    pb.wait("vector", "q_tensor", tv)
    vv = pb.op("vector", lambda E: E.tensor_copy(out[:], bank[:, 0:nch]))
    return out, vv


def emit_ln_chunk(pb, x_chunk, h_chunk, stats, sq_scratch, dim=D):
    mu = stats[:, 0:1]
    sq = stats[:, 1:2]
    rstd = stats[:, 2:3]
    nbias = stats[:, 3:4]
    var = stats[:, 4:5]
    pb.wait("scalar", "q_vector", pb.tick("vector"))
    pb.wait("vector", "q_scalar", pb.tick("scalar"))
    pb.op("vector", lambda E: E.reduce_sum(mu, x_chunk, axis=AX.X))
    pb.op("vector", lambda E: E.tensor_scalar(mu, mu, 1.0 / dim, None, ALU.mult))
    sv = pb.op("scalar", lambda E: E.activation(sq_scratch[:], x_chunk, AF.Square, accum_out=sq))
    pb.wait("vector", "q_scalar", sv)
    pb.op("vector", lambda E: E.tensor_scalar(var, sq, 1.0 / dim, None, ALU.mult))
    pb.op("vector", lambda E: E.tensor_tensor(sq, mu, mu, ALU.mult))
    pb.op("vector", lambda E: E.tensor_tensor(var, var, sq, ALU.subtract))
    vv = pb.op("vector", lambda E: E.tensor_scalar(var, var, LNEPS, None, ALU.add))
    pb.wait("scalar", "q_vector", vv)
    sv = pb.op("scalar", lambda E: E.activation(rstd, var, AF.Sqrt))
    pb.wait("vector", "q_scalar", sv)
    pb.op("vector", lambda E: E.reciprocal(rstd, rstd))
    pb.op("vector", lambda E: E.tensor_tensor(nbias, mu, rstd, ALU.mult))
    pb.op("vector", lambda E: E.tensor_scalar(nbias, nbias, -1.0, None, ALU.mult))
    return pb.op("vector", lambda E: E.tensor_scalar(h_chunk, x_chunk, rstd, nbias, ALU.mult, ALU.add))


def emit_l0ffn(pb, p, idn, x_in, U1, V1, b1, U2, V2, b2, y, T, tag=""):
    nc = pb.nc
    _oldpfx = pb.prefix
    pb.prefix = _oldpfx + tag
    with pb.scope():
        u1 = sbuf(pb, "u1", (128, 8, RANK))
        v1 = sbuf(pb, "v1", (128, 2, DFF))
        u2 = sbuf(pb, "u2", (128, 32, RANK))
        v2 = sbuf(pb, "v2", (128, 2, D))
        pb.dma("sync", u1[:], U1.rearrange("(c p) r -> p c r", p=128), "wld")
        pb.dma("sync", v1[:], V1.rearrange("(c p) f -> p c f", p=128), "wld")
        pb.dma("sync", u2[:], U2.rearrange("(c p) r -> p c r", p=128), "wld")
        pb.dma("sync", v2[:], V2.rearrange("(c p) o -> p c o", p=128), "wld")
        b1t, b1v = load_bias_cols(pb, "b1t", b1, 32, idn, p[7])
        b2t, b2v = load_bias_cols(pb, "b2t", b2, 8, idn, p[6])
        pb.wait("tensor", "q_vector", b2v)
        pb.wait("scalar", "q_vector", b2v)
        wld = pb.cur("wld")

        xc = sbuf(pb, "xc", (128, 4, D))
        hc = sbuf(pb, "hc", (128, 4, D))
        stats = sbuf(pb, "stats", (128, 8))
        sqs = sbuf(pb, "sqs", (128, D))
        hT = sbuf(pb, "hT", (128, 8, 512))
        Pfm = sbuf(pb, "Pfm", (128, 2, 512))
        Hc = sbuf(pb, "Hc", (128, 2, 512))
        Qfm = sbuf(pb, "Qfm", (128, 2, 512))
        dfm = sbuf(pb, "dfm", (128, 8, 512))
        oc = sbuf(pb, "oc", (128, 4, D))

        n_blk = (T + 511) // 512
        oev_last = 0
        hv = 0
        for blk_i in range(n_blk):
            t0 = blk_i * 512
            bt = min(512, T - t0)
            nch = bt // 128
            pb.wait("sync", "q_vector", oev_last)
            pb.wait("tensor", "q_vector", oev_last)
            for c in range(nch):
                r0 = t0 + c * 128
                lv = pb.dma("sync", xc[:, c, :], x_in[r0:r0 + 128, :], "xld_s%d" % c)
                pb.wait("vector", "xld_s%d" % c, lv)
                pb.wait("scalar", "xld_s%d" % c, lv)
                hv = emit_ln_chunk(pb, xc[:, c, :], hc[:, c, :], stats, sqs)
                pb.wait("tensor", "q_vector", hv)
                for dch in range(8):
                    bank = p[dch % 2]
                    tv = pb.op("tensor", lambda E, bank=bank, c=c, dch=dch:
                               E.transpose(bank[:, 0:128], hc[:, c, dch * 128:(dch + 1) * 128], idn[:]))
                    pb.wait("vector", "q_tensor", tv)
                    ev = pb.op("vector", lambda E, bank=bank, c=c, dch=dch:
                               E.tensor_copy(hT[:, dch, c * 128:(c + 1) * 128], bank[:, 0:128]))
                    pb.wait("tensor", "q_vector", ev)
            pb.wait("tensor", "wld", wld)
            mmP = 0
            for rch in range(2):
                bank = p[2 + rch]
                for k in range(8):
                    mmP = pb.op("tensor", lambda E, r=rch, k=k, bank=bank, bt=bt:
                                E.matmul(bank[:, 0:bt], u1[:, k, r * 128:(r + 1) * 128],
                                         hT[:, k, 0:bt], start=(k == 0), stop=(k == 7)))
            pb.wait("vector", "q_tensor", mmP)
            pev = 0
            for rch in range(2):
                pev = pb.op("vector", lambda E, r=rch, bank=p[2 + rch], bt=bt:
                            E.tensor_copy(Pfm[:, r, 0:bt], bank[:, 0:bt]))
            pb.wait("tensor", "q_vector", pev)
            gel = {}
            mmQ = {}
            for f in range(32):
                bank = p[f % 2]
                mmH = 0
                for rch in range(2):
                    mmH = pb.op("tensor", lambda E, f=f, r=rch, bank=bank, bt=bt:
                                E.matmul(bank[:, 0:bt], v1[:, r, f * 128:(f + 1) * 128],
                                         Pfm[:, r, 0:bt], start=(r == 0), stop=(r == 1)))
                pb.wait("scalar", "q_tensor", mmH)
                if f >= 2:
                    pb.wait("scalar", "q_tensor", mmQ[f - 2])
                hbuf = Hc[:, f % 2, 0:bt]
                gel[f] = pb.op("scalar", lambda E, f=f, bank=bank, hb=hbuf, bt=bt:
                               E.activation(hb, bank[:, 0:bt], AF.Gelu_apprx_tanh,
                                            bias=b1t[:, f:f + 1]))
                pb.wait("tensor", "q_scalar", gel[f])
                for rch in range(2):
                    bank2 = p[4 + rch]
                    mmQ[f] = pb.op("tensor", lambda E, f=f, r=rch, hb=hbuf, bank2=bank2, bt=bt:
                                   E.matmul(bank2[:, 0:bt], u2[:, f, r * 128:(r + 1) * 128], hb,
                                            start=(f == 0), stop=(f == 31)))
            pb.wait("vector", "q_tensor", mmQ[31])
            qev = 0
            for rch in range(2):
                qev = pb.op("vector", lambda E, r=rch, bank=p[4 + rch], bt=bt:
                            E.tensor_copy(Qfm[:, r, 0:bt], bank[:, 0:bt]))
            pb.wait("tensor", "q_vector", qev)
            dev = {}
            for och in range(8):
                bank = p[6 + och % 2]
                if och >= 2:
                    pb.wait("tensor", "q_vector", dev[och - 2])
                mmD = 0
                for rch in range(2):
                    mmD = pb.op("tensor", lambda E, o=och, r=rch, bank=bank, bt=bt:
                                E.matmul(bank[:, 0:bt], v2[:, r, o * 128:(o + 1) * 128],
                                         Qfm[:, r, 0:bt], start=(r == 0), stop=(r == 1)))
                pb.wait("vector", "q_tensor", mmD)
                dev[och] = pb.op("vector", lambda E, o=och, bank=bank, bt=bt:
                                 E.tensor_scalar(dfm[:, o, 0:bt], bank[:, 0:bt],
                                                 b2t[:, o:o + 1], None, ALU.add))
            pb.wait("gpsimd", "q_vector", hv)
            pb.wait("tensor", "q_vector", dev[7])
            for c in range(nch):
                r0 = t0 + c * 128
                lv2 = pb.dma("gpsimd", xc[:, c, :], x_in[r0:r0 + 128, :], "xld2_s%d" % c)
                pb.wait("tensor", "q_vector", oev_last)
                oev = {}
                for och in range(8):
                    bank = p[och % 2]
                    if och >= 2:
                        pb.wait("tensor", "q_vector", oev[och - 2])
                    tv = pb.op("tensor", lambda E, bank=bank, c=c, o=och:
                               E.transpose(bank[:, 0:128], dfm[:, o, c * 128:(c + 1) * 128], idn[:]))
                    pb.wait("vector", "q_tensor", tv)
                    if och == 0:
                        pb.wait("vector", "xld2_s%d" % c, lv2)
                    oev[och] = pb.op("vector", lambda E, bank=bank, c=c, o=och:
                                     E.scalar_tensor_tensor(oc[:, c, o * 128:(o + 1) * 128],
                                                            bank[:, 0:128], 0.5,
                                                            xc[:, c, o * 128:(o + 1) * 128],
                                                            ALU.mult, ALU.add))
                oev_last = oev[7]
                pb.wait("gpsimd", "q_vector", oev_last)
                pb.dma("gpsimd", y[r0:r0 + 128, :], oc[:, c, :], "yst")
        pb.wait("gpsimd", "yst", pb.cur("yst"))
    pb.prefix = _oldpfx


def build_l0ffn(T):
    nc = bass.Bass("TRN2", target_bir_lowering=False, debug=False)
    x_in = nc.dram_tensor("x", [T, D], F32, kind="ExternalInput").ap()
    U1 = nc.dram_tensor("U1", [D, RANK], F32, kind="ExternalInput").ap()
    V1 = nc.dram_tensor("V1", [RANK, DFF], F32, kind="ExternalInput").ap()
    b1 = nc.dram_tensor("b1", [DFF], F32, kind="ExternalInput").ap()
    U2 = nc.dram_tensor("U2", [DFF, RANK], F32, kind="ExternalInput").ap()
    V2 = nc.dram_tensor("V2", [RANK, D], F32, kind="ExternalInput").ap()
    b2 = nc.dram_tensor("b2", [D], F32, kind="ExternalInput").ap()
    y = nc.dram_tensor("y", [T, D], F32, kind="ExternalOutput").ap()
    est = contextlib.ExitStack()
    with est:
        pb = PB(nc, est)
        idn = make_identity(pb)
        p = psum8(pb)
        emit_l0ffn(pb, p, idn, x_in, U1, V1, b1, U2, V2, b2, y, T)
        pb.build()
    return nc


def bcast_row(pb, name, row_ap, width, idn_ready_tick, bank, ones1):
    out = sbuf(pb, name, (128, width))
    vv = 0
    for c0 in range(0, width, 512):
        w = min(512, width - c0)
        tv = pb.op("tensor", lambda E, c0=c0, w=w:
                   E.matmul(bank[:, 0:w], ones1[:, 0:128], row_ap[:, c0:c0 + w],
                            start=True, stop=True))
        pb.wait("vector", "q_tensor", tv)
        vv = pb.op("vector", lambda E, c0=c0, w=w:
                   E.tensor_copy(out[:, c0:c0 + w], bank[:, 0:w]))
        pb.wait("tensor", "q_vector", vv)
    return out, vv


def emit_l0attn(pb, p, idn, x_in, Wq, Wk, Wv, Wo, W1, W2, bq, bk, bv, bo, b1, b2, lnp, y, T, halo, tag=""):
    nc = pb.nc
    _oldpfx = pb.prefix
    pb.prefix = _oldpfx + tag
    TK = T + halo
    with pb.scope():
        xi_d = nc.dram_tensor(pb.prefix + "xi_d", [TK, D], F32, kind="Internal").ap()
        xmid_d = nc.dram_tensor(pb.prefix + "xmid_d", [T, D], F32, kind="Internal").ap()
        ctx_d = nc.dram_tensor(pb.prefix + "ctx_d", [D, T], F32, kind="Internal").ap()
        hg_d = nc.dram_tensor(pb.prefix + "hg_d", [DFF, T], F32, kind="Internal").ap()
        ones1 = sbuf(pb, "ones1", (1, 128))
        ov = pb.op("vector", lambda E: E.memset(ones1[:], 1.0))
        pb.wait("tensor", "q_vector", ov)
        lnrows = []
        for i in range(4):
            r = sbuf(pb, "lnrow%d" % i, (1, D))
            lv = pb.dma("sync", r[:], lnp[i:i + 1, :], "lnld")
            lnrows.append(r)
        pb.wait("tensor", "lnld", pb.cur("lnld"))
        pb.wait("tensor", "q_vector", pb.idn_ready)
        g1bc, _ = bcast_row(pb, "g1bc", lnrows[0][:, :], D, pb.idn_ready, p[0], ones1)
        b1bc, _ = bcast_row(pb, "b1bc", lnrows[1][:, :], D, pb.idn_ready, p[0], ones1)
        g2bc, _ = bcast_row(pb, "g2bc", lnrows[2][:, :], D, pb.idn_ready, p[0], ones1)
        b2bc, bcv = bcast_row(pb, "b2bc", lnrows[3][:, :], D, pb.idn_ready, p[0], ones1)
        b2row = sbuf(pb, "b2row", (1, D))
        lvb = pb.dma("sync", b2row[:], b2.rearrange("(a d) -> a d", a=1), "b2r_ld")
        pb.wait("tensor", "b2r_ld", lvb)
        b2bc2, bdv2 = bcast_row(pb, "b2bc2", b2row[:, :], D, 0, p[1], ones1)
        bqt, _ = load_bias_cols(pb, "bqt", bq, 8, idn, p[2])
        bkt, _ = load_bias_cols(pb, "bkt", bk, 8, idn, p[2])
        bvt, _ = load_bias_cols(pb, "bvt", bv, 8, idn, p[2])
        bot, _ = load_bias_cols(pb, "bot", bo, 8, idn, p[2])
        bf1t, bv2 = load_bias_cols(pb, "bf1t", b1, 32, idn, p[2])
        pb.wait("vector", "q_vector", bv2)
        mski = sbuf(pb, "mski", (128, 256), I32)
        msk = sbuf(pb, "msk", (128, 256))
        mskt = sbuf(pb, "mskt", (128, 256))
        pb.op("gpsimd", lambda E: E.iota(mski[:, 0:128], [[-1, 128]], base=128,
                                         channel_multiplier=1))
        gv = pb.op("gpsimd", lambda E: E.iota(mski[:, 128:256], [[-1, 128]], base=0,
                                              channel_multiplier=1))
        pb.wait("vector", "q_gpsimd", gv)
        pb.op("vector", lambda E: E.tensor_scalar(msk[:], mski[:], 0, None, ALU.is_lt))
        pb.op("vector", lambda E: E.tensor_scalar(mskt[:], mski[:], 128, None, ALU.is_gt))
        pb.op("vector", lambda E: E.tensor_tensor(msk[:], msk[:], mskt[:], ALU.add))
        mv = pb.op("vector", lambda E: E.tensor_scalar(msk[:], msk[:], NEG, None, ALU.mult))
        pb.wait("scalar", "q_vector", mv)
        xc = sbuf(pb, "xc", (128, D))
        xc2 = sbuf(pb, "xc2", (128, D))
        hc = sbuf(pb, "hc", (128, D))
        stats = sbuf(pb, "stats", (128, 8))
        sm = sbuf(pb, "sm", (128, 4))

        with pb.scope():
            h1T = sbuf(pb, "h1T", (128, 8, TK))
            wslab = sbuf(pb, "wslab", (128, 8, 256))
            kg = sbuf(pb, "kg", (128, 2, TK))
            vg = sbuf(pb, "vg", (128, 2, TK))
            qg = vg
            vtm = sbuf(pb, "vtm", (128, TK // 128, 256))
            ctxg = sbuf(pb, "ctxg", (128, 2, T))
            ptile = sbuf(pb, "ptile", (128, 256))
            ptt = sbuf(pb, "ptt", (128, 256))

            for c in range(TK // 128):
                r0 = c * 128
                pb.wait("sync", "q_vector", pb.tick("vector"))
                pb.wait("sync", "q_scalar", pb.tick("scalar"))
                lv = pb.dma("sync", xc[:], x_in[r0:r0 + 128, :], "xld_s0")
                pb.wait("vector", "xld_s0", lv)
                pb.wait("vector", "xist", pb.cur("xist"))
                pb.wait("scalar", "xld_s0", lv)
                xiv = emit_ln_chunk(pb, xc[:], xc2[:], stats, hc)
                pb.wait("gpsimd", "q_vector", xiv)
                pb.dma("gpsimd", xi_d[r0:r0 + 128, :], xc2[:], "xist")
                pb.wait("scalar", "q_vector", xiv)
                nv = emit_ln_chunk(pb, xc2[:], hc[:], stats, xc)
                pb.op("vector", lambda E: E.tensor_tensor(hc[:], hc[:], g1bc[:], ALU.mult))
                hv = pb.op("vector", lambda E: E.tensor_tensor(hc[:], hc[:], b1bc[:], ALU.add))
                pb.wait("tensor", "q_vector", hv)
                for dch in range(8):
                    bank = p[dch % 2]
                    tv = pb.op("tensor", lambda E, bank=bank, c=c, dch=dch:
                               E.transpose(bank[:, 0:128], hc[:, dch * 128:(dch + 1) * 128], idn[:]))
                    pb.wait("vector", "q_tensor", tv)
                    ev = pb.op("vector", lambda E, bank=bank, c=c, dch=dch:
                               E.tensor_copy(h1T[:, dch, c * 128:(c + 1) * 128], bank[:, 0:128]))
                    pb.wait("tensor", "q_vector", ev)
            pb.phase()

            for g in range(4):
                pb.phase()
                for n_t, Wmat, bias_t in (("k", Wk, bkt), ("v", Wv, bvt), ("q", Wq, bqt)):
                    pb.wait("sync", "q_tensor", pb.tick("tensor"))
                    lvw = pb.dma("sync", wslab[:],
                                 Wmat[:, g * 256:(g + 1) * 256].rearrange("(c p) o -> p c o", p=128),
                                 "wqkv_ld")
                    pb.wait("tensor", "wqkv_ld", lvw)
                    src_off = halo if n_t == "q" else 0
                    n_tok = T if n_t == "q" else TK
                    dst = {"k": kg, "v": vg, "q": qg}[n_t]
                    scl = 0.125 if n_t == "q" else None
                    for oc2 in range(2):
                        for tt in range((n_tok + 511) // 512):
                            t0 = tt * 512
                            bt = min(512, n_tok - t0)
                            bank = p[2 + (tt + oc2) % 2]
                            mmv = 0
                            for k in range(8):
                                mmv = pb.op("tensor", lambda E, k=k, o=oc2, t0=t0, bt=bt, bank=bank, so=src_off:
                                            E.matmul(bank[:, 0:bt], wslab[:, k, o * 128:(o + 1) * 128],
                                                     h1T[:, k, so + t0:so + t0 + bt],
                                                     start=(k == 0), stop=(k == 7)))
                            pb.wait("vector", "q_tensor", mmv)
                            vvv = pb.op("vector", lambda E, o=oc2, t0=t0, bt=bt, bank=bank, g=g,
                                        dd=dst, bias_t=bias_t, scl=scl:
                                        E.tensor_scalar(dd[:, o, t0:t0 + bt], bank[:, 0:bt],
                                                        bias_t[:, g * 2 + o:g * 2 + o + 1], scl,
                                                        ALU.add, ALU.mult if scl else ALU.bypass))
                            pb.wait("tensor", "q_vector", vvv)
                    if n_t == "v":
                        for c in range(TK // 128):
                            for oc2 in range(2):
                                bank = p[6 + oc2]
                                tv = pb.op("tensor", lambda E, c=c, o=oc2, bank=bank:
                                           E.transpose(bank[:, 0:128], vg[:, o, c * 128:(c + 1) * 128], idn[:]))
                                pb.wait("vector", "q_tensor", tv)
                                ev = pb.op("vector", lambda E, c=c, o=oc2, bank=bank:
                                           E.tensor_copy(vtm[:, c, o * 128:(o + 1) * 128], bank[:, 0:128]))
                                pb.wait("tensor", "q_vector", ev)
                for hh in range(4):
                    oc2 = hh // 2
                    row0 = (hh % 2) * 64
                    for it in range(T // 128):
                        i0 = it * 128
                        kb1 = i0 + halo
                        kb0 = kb1 - 128
                        two = kb0 >= 0
                        bank_s = p[2]
                        pb.wait("tensor", "q_scalar", pb.tick("scalar"))
                        if two:
                            pb.op("tensor", lambda E, q0=i0, kb=kb0, o=oc2, r=row0, bank=bank_s:
                                  E.matmul(bank[:, 0:128], qg[r:r + 64, o, q0:q0 + 128],
                                           kg[r:r + 64, o, kb:kb + 128], start=True, stop=True))
                        mmv = pb.op("tensor", lambda E, q0=i0, kb=kb1, o=oc2, r=row0, bank=bank_s:
                                    E.matmul(bank[:, 128:256], qg[r:r + 64, o, q0:q0 + 128],
                                             kg[r:r + 64, o, kb:kb + 128], start=True, stop=True))
                        pb.wait("vector", "q_tensor", mmv)
                        c0 = 0 if two else 128
                        cw = 256 - c0
                        pb.op("vector", lambda E, bank=bank_s, c0=c0, cw=cw:
                              E.tensor_tensor(bank[:, c0:c0 + cw], bank[:, c0:c0 + cw],
                                              msk[:, c0:c0 + cw], ALU.add))
                        pb.op("vector", lambda E, bank=bank_s, c0=c0, cw=cw:
                              E.reduce_max(sm[:, 0:1], bank[:, c0:c0 + cw], axis=AX.X))
                        mv2 = pb.op("vector", lambda E:
                                    E.tensor_scalar(sm[:, 1:2], sm[:, 0:1], -1.0, None, ALU.mult))
                        pb.wait("scalar", "q_vector", mv2)
                        sv = pb.op("scalar", lambda E, bank=bank_s, c0=c0, cw=cw:
                                   E.activation(ptile[:, c0:c0 + cw], bank[:, c0:c0 + cw], AF.Exp,
                                                bias=sm[:, 1:2], accum_out=sm[:, 2:3]))
                        pb.wait("vector", "q_scalar", sv)
                        pb.op("vector", lambda E: E.reciprocal(sm[:, 3:4], sm[:, 2:3]))
                        nv = pb.op("vector", lambda E, c0=c0, cw=cw:
                                   E.tensor_scalar(ptile[:, c0:c0 + cw], ptile[:, c0:c0 + cw],
                                                   sm[:, 3:4], None, ALU.mult))
                        pb.wait("tensor", "q_vector", nv)
                        bank_o = p[5][row0:row0 + 64, 0:128]
                        first = True
                        for half in range(0 if two else 1, 2):
                            bank_t = p[3 + (half % 2)]
                            tv = pb.op("tensor", lambda E, h2=half, bank_t=bank_t:
                                       E.transpose(bank_t[:, 0:128], ptile[:, h2 * 128:(h2 + 1) * 128],
                                                   idn[:]))
                            pb.wait("vector", "q_tensor", tv)
                            ev = pb.op("vector", lambda E, h2=half, bank_t=bank_t:
                                       E.tensor_copy(ptt[:, h2 * 128:(h2 + 1) * 128], bank_t[:, 0:128]))
                            pb.wait("tensor", "q_vector", ev)
                            kb = kb0 if half == 0 else kb1
                            pb.op("tensor", lambda E, kb=kb, h2=half, hh=hh, bank=bank_o, first=first:
                                  E.matmul(bank, vtm[:, kb // 128, hh * 64:(hh + 1) * 64],
                                           ptt[:, h2 * 128:(h2 + 1) * 128],
                                           start=first, stop=(h2 == 1)))
                            first = False
                        mmv = pb.tick("tensor")
                        pb.wait("vector", "q_tensor", mmv)
                        cv = pb.op("vector", lambda E, hh=hh, i0=i0, bank=bank_o, oc2=oc2:
                                   E.tensor_copy(ctxg[(hh % 2) * 64:(hh % 2) * 64 + 64, oc2, i0:i0 + 128],
                                                 bank))
                        pb.wait("tensor", "q_vector", cv)
                pb.wait("gpsimd", "q_vector", pb.tick("vector"))
                for oc2 in range(2):
                    pb.dma("gpsimd", ctx_d[g * 256 + oc2 * 128:g * 256 + (oc2 + 1) * 128, :],
                           ctxg[:, oc2, :], "ctxst")
            pb.phase()

        with pb.scope():
            woslab = sbuf(pb, "woslab", (128, 8, 128))
            ctxc = sbuf(pb, "ctxc", (128, 8, 512))
            y1 = sbuf(pb, "y1", (128, 8, 512))
            for tt in range(T // 512):
                t0 = tt * 512
                for cch in range(8):
                    pb.wait("sync", "q_tensor", pb.tick("tensor"))
                    pb.dma("sync", ctxc[:, cch, :], ctx_d[cch * 128:(cch + 1) * 128,
                                                          t0:t0 + 512], "ctx_ld")
                for och in range(8):
                    pb.wait("sync", "q_tensor", pb.tick("tensor"))
                    lvw = pb.dma("sync", woslab[:],
                                 Wo[:, och * 128:(och + 1) * 128].rearrange("(c p) o -> p c o", p=128),
                                 "wo_ld")
                    pb.wait("tensor", "wo_ld", pb.cur("wo_ld"))
                    pb.wait("tensor", "ctx_ld", pb.cur("ctx_ld"))
                    bank = p[och % 2]
                    pb.wait("tensor", "q_vector", pb.tick("vector"))
                    mmv = 0
                    for k in range(8):
                        mmv = pb.op("tensor", lambda E, k=k, bank=bank:
                                    E.matmul(bank[:, 0:512], woslab[:, k, :], ctxc[:, k, :],
                                             start=(k == 0), stop=(k == 7)))
                    pb.wait("vector", "q_tensor", mmv)
                    vv = pb.op("vector", lambda E, o=och, bank=bank:
                               E.tensor_scalar(y1[:, o, :], bank[:, 0:512], bot[:, o:o + 1], None,
                                               ALU.add))
                pb.wait("tensor", "q_vector", pb.tick("vector"))
                for c in range(4):
                    r0 = t0 + c * 128
                    pb.wait("gpsimd", "q_vector", pb.tick("vector"))
                    lv = pb.dma("gpsimd", xc[:], xi_d[halo + r0:halo + r0 + 128, :], "xild_s0")
                    for och in range(8):
                        bank = p[2 + och % 2]
                        tv = pb.op("tensor", lambda E, bank=bank, c=c, o=och:
                                   E.transpose(bank[:, 0:128], y1[:, o, c * 128:(c + 1) * 128], idn[:]))
                        pb.wait("vector", "q_tensor", tv)
                        if och == 0:
                            pb.wait("vector", "xild_s0", pb.cur("xild_s0"))
                            pb.wait("vector", "xmst", pb.cur("xmst"))
                        vv = pb.op("vector", lambda E, bank=bank, c=c, o=och:
                                   E.tensor_tensor(xc2[:, o * 128:(o + 1) * 128], bank[:, 0:128],
                                                   xc[:, o * 128:(o + 1) * 128], ALU.add))
                        pb.wait("tensor", "q_vector", vv)
                    pb.wait("gpsimd", "q_vector", pb.tick("vector"))
                    pb.dma("gpsimd", xmid_d[r0:r0 + 128, :], xc2[:], "xmst")
            pb.phase()

        with pb.scope():
            h2T = sbuf(pb, "h2T", (128, 8, T))
            w1slab = sbuf(pb, "w1slab", (128, 2, 8, 128))
            gelc = sbuf(pb, "gelc", (128, 2, 512))
            for c in range(T // 128):
                r0 = c * 128
                pb.wait("sync", "q_vector", pb.tick("vector"))
                pb.wait("sync", "q_scalar", pb.tick("scalar"))
                lv = pb.dma("sync", xc[:], xmid_d[r0:r0 + 128, :], "xm_s0")
                pb.wait("vector", "xm_s0", lv)
                pb.wait("scalar", "xm_s0", lv)
                nv = emit_ln_chunk(pb, xc[:], hc[:], stats, xc2)
                pb.op("vector", lambda E: E.tensor_tensor(hc[:], hc[:], g2bc[:], ALU.mult))
                hv = pb.op("vector", lambda E: E.tensor_tensor(hc[:], hc[:], b2bc[:], ALU.add))
                pb.wait("tensor", "q_vector", hv)
                for dch in range(8):
                    bank = p[dch % 2]
                    tv = pb.op("tensor", lambda E, bank=bank, c=c, dch=dch:
                               E.transpose(bank[:, 0:128], hc[:, dch * 128:(dch + 1) * 128], idn[:]))
                    pb.wait("vector", "q_tensor", tv)
                    ev = pb.op("vector", lambda E, bank=bank, c=c, dch=dch:
                               E.tensor_copy(h2T[:, dch, c * 128:(c + 1) * 128], bank[:, 0:128]))
                    pb.wait("tensor", "q_vector", ev)
            pb.phase()
            for f in range(32):
                sl = f % 2
                pb.wait("sync", "q_tensor", pb.tick("tensor"))
                lvw = pb.dma("sync", w1slab[:, sl, :, :],
                             W1[:, f * 128:(f + 1) * 128].rearrange("(c p) o -> p c o", p=128),
                             "w1_s%d" % sl)
                pb.wait("tensor", "w1_s%d" % sl, pb.cur("w1_s%d" % sl))
                for tt in range(T // 512):
                    t0 = tt * 512
                    bank = p[2 + tt % 2]
                    pb.wait("tensor", "q_scalar", pb.tick("scalar"))
                    mmv = 0
                    for k in range(8):
                        mmv = pb.op("tensor", lambda E, k=k, f=f, t0=t0, bank=bank, sl=sl:
                                    E.matmul(bank[:, 0:512], w1slab[:, sl, k, :],
                                             h2T[:, k, t0:t0 + 512], start=(k == 0), stop=(k == 7)))
                    pb.wait("scalar", "q_tensor", mmv)
                    pb.wait("scalar", "hgst", pb.cur("hgst"))
                    gv2 = pb.op("scalar", lambda E, f=f, tt=tt, bank=bank:
                                E.activation(gelc[:, tt % 2, :], bank[:, 0:512], AF.Gelu_apprx_tanh,
                                             bias=bf1t[:, f:f + 1]))
                    pb.wait("gpsimd", "q_scalar", gv2)
                    pb.dma("gpsimd", hg_d[f * 128:(f + 1) * 128, t0:t0 + 512],
                           gelc[:, tt % 2, :], "hgst")
            pb.phase()

        with pb.scope():
            hgt = sbuf(pb, "hgt", (128, 32, 512))
            w2slab = sbuf(pb, "w2slab", (128, 2, 32, 128))
            youtf = sbuf(pb, "youtf", (128, 8, 512))
            for tt in range(T // 512):
                t0 = tt * 512
                pb.wait("sync", "q_tensor", pb.tick("tensor"))
                for fch in range(32):
                    pb.dma("sync", hgt[:, fch, :], hg_d[fch * 128:(fch + 1) * 128, t0:t0 + 512],
                           "hg_ld")
                for och in range(8):
                    sl = och % 2
                    pb.wait("sync", "q_tensor", pb.tick("tensor"))
                    pb.dma("sync", w2slab[:, sl, :, :],
                           W2[:, och * 128:(och + 1) * 128].rearrange("(c p) o -> p c o", p=128),
                           "w2_s%d" % sl)
                    pb.wait("tensor", "w2_s%d" % sl, pb.cur("w2_s%d" % sl))
                    pb.wait("tensor", "hg_ld", pb.cur("hg_ld"))
                    bank = p[4 + och % 2]
                    pb.wait("tensor", "q_vector", pb.tick("vector"))
                    mmv = 0
                    for k in range(32):
                        mmv = pb.op("tensor", lambda E, k=k, bank=bank, sl=sl:
                                    E.matmul(bank[:, 0:512], w2slab[:, sl, k, :], hgt[:, k, :],
                                             start=(k == 0), stop=(k == 31)))
                    pb.wait("vector", "q_tensor", mmv)
                    vv = pb.op("vector", lambda E, o=och, bank=bank:
                               E.tensor_copy(youtf[:, o, :], bank[:, 0:512]))
                pb.wait("tensor", "q_vector", pb.tick("vector"))
                for c in range(4):
                    r0 = t0 + c * 128
                    pb.wait("gpsimd", "q_vector", pb.tick("vector"))
                    lvm = pb.dma("gpsimd", xc[:], xmid_d[r0:r0 + 128, :], "fm_s0")
                    lvx = pb.dma("gpsimd", xc2[:], x_in[halo + r0:halo + r0 + 128, :], "fx_s0")
                    for och in range(8):
                        bank = p[och % 2]
                        tv = pb.op("tensor", lambda E, bank=bank, c=c, o=och:
                                   E.transpose(bank[:, 0:128], youtf[:, o, c * 128:(c + 1) * 128],
                                               idn[:]))
                        pb.wait("vector", "q_tensor", tv)
                        if och == 0:
                            pb.wait("vector", "fm_s0", pb.cur("fm_s0"))
                            pb.wait("vector", "fx_s0", pb.cur("fx_s0"))
                            pb.wait("vector", "yst", pb.cur("yst"))
                        pb.op("vector", lambda E, bank=bank, o=och:
                              E.tensor_tensor(hc[:, o * 128:(o + 1) * 128], bank[:, 0:128],
                                              xc[:, o * 128:(o + 1) * 128], ALU.add))
                        vv = pb.op("vector", lambda E, bank=bank, o=och:
                                   E.tensor_tensor(hc[:, o * 128:(o + 1) * 128],
                                                   hc[:, o * 128:(o + 1) * 128],
                                                   xc2[:, o * 128:(o + 1) * 128], ALU.add))
                        pb.wait("tensor", "q_vector", vv)
                    fv = pb.op("vector", lambda E: E.tensor_tensor(hc[:], hc[:], b2bc2[:], ALU.add))
                    pb.wait("gpsimd", "q_vector", fv)
                    pb.dma("gpsimd", y[r0:r0 + 128, :], hc[:], "yst")
        pb.wait("gpsimd", "yst", pb.cur("yst"))
    pb.prefix = _oldpfx


def build_l0attn(T, halo):
    TK = T + halo
    nc = bass.Bass("TRN2", target_bir_lowering=False, debug=False)
    x_in = nc.dram_tensor("x", [TK, D], F32, kind="ExternalInput").ap()
    Wq = nc.dram_tensor("Wq", [D, D], F32, kind="ExternalInput").ap()
    Wk = nc.dram_tensor("Wk", [D, D], F32, kind="ExternalInput").ap()
    Wv = nc.dram_tensor("Wv", [D, D], F32, kind="ExternalInput").ap()
    Wo = nc.dram_tensor("Wo", [D, D], F32, kind="ExternalInput").ap()
    W1 = nc.dram_tensor("W1", [D, DFF], F32, kind="ExternalInput").ap()
    W2 = nc.dram_tensor("W2", [DFF, D], F32, kind="ExternalInput").ap()
    bq = nc.dram_tensor("bq", [D], F32, kind="ExternalInput").ap()
    bk = nc.dram_tensor("bk", [D], F32, kind="ExternalInput").ap()
    bv = nc.dram_tensor("bv", [D], F32, kind="ExternalInput").ap()
    bo = nc.dram_tensor("bo", [D], F32, kind="ExternalInput").ap()
    b1 = nc.dram_tensor("b1", [DFF], F32, kind="ExternalInput").ap()
    b2 = nc.dram_tensor("b2", [D], F32, kind="ExternalInput").ap()
    lnp = nc.dram_tensor("lnp", [4, D], F32, kind="ExternalInput").ap()
    y = nc.dram_tensor("y", [T, D], F32, kind="ExternalOutput").ap()
    est = contextlib.ExitStack()
    with est:
        pb = PB(nc, est)
        idn = make_identity(pb)
        p = psum8(pb)
        emit_l0attn(pb, p, idn, x_in, Wq, Wk, Wv, Wo, W1, W2, bq, bk, bv, bo, b1, b2, lnp, y, T, halo)
        pb.build()
    return nc


def emit_moe(pb, p, idn, x_in, gw, Wg, Wu, Wd, y, tag=""):
    nc = pb.nc
    _oldpfx = pb.prefix
    pb.prefix = _oldpfx + tag
    S = SEQ
    with pb.scope():
        xsel_d = nc.dram_tensor(pb.prefix + "xsel_d", [4, D, KSEL], F32, kind="Internal").ap()
        gat_d = nc.dram_tensor(pb.prefix + "gat_d", [4, KSEL], F32, kind="Internal").ap()
        ctbh_d = nc.dram_tensor(pb.prefix + "ctbh_d", [4, KSEL, D], BF16, kind="Internal").ap()
        ctbl_d = nc.dram_tensor(pb.prefix + "ctbl_d", [4, KSEL, D], BF16, kind="Internal").ap()
        xc = sbuf(pb, "xc", (128, D))
        hc = sbuf(pb, "hc", (128, D))
        stats = sbuf(pb, "stats", (128, 8))
        sm = sbuf(pb, "sm", (128, 4))
        logits_tm = sbuf(pb, "logits_tm", (128, 32, NE))
        aff_tm = sbuf(pb, "aff_tm", (128, 32, NE))
        affh_tm = sbuf(pb, "affh_tm", (128, 32, NE), BF16)
        affl_tm = sbuf(pb, "affl_tm", (128, 32, NE), BF16)
        slots_tm = sbuf(pb, "slots_tm", (128, 32, 4))
        slots_em = sbuf(pb, "slots_em", (8, S))
        srow = sbuf(pb, "srow", (128, 2, 512))
        scol = sbuf(pb, "scol", (128, 8))
        ones1 = sbuf(pb, "ones1", (1, 128))

        with pb.scope():
            t1 = sbuf(pb, "t1i", (128, 2, 512), I32)
            t2 = sbuf(pb, "t2i", (128, 8), I32)
            pb.op("gpsimd", lambda E: E.iota(t1[:], [[512, 2], [1, 512]], base=0,
                                             channel_multiplier=0))
            gv = pb.op("gpsimd", lambda E: E.iota(t2[:], [[128, 8]], base=0,
                                                  channel_multiplier=1))
            pb.wait("vector", "q_gpsimd", gv)
            pb.op("vector", lambda E: E.tensor_copy(srow[:], t1[:]))
            pb.op("vector", lambda E: E.tensor_copy(scol[:], t2[:]))
        ov = pb.op("vector", lambda E: E.memset(ones1[:], 1.0))
        pb.wait("tensor", "q_vector", ov)

        with pb.scope():
            h_hi = sbuf(pb, "h_hi", (128, 32, D), BF16)
            h_lo = sbuf(pb, "h_lo", (128, 32, D), BF16)
            gwt = sbuf(pb, "gwt", (128, 8, NE))
            pb.wait("sync", "q_vector", pb.tick("vector"))
            pb.dma("sync", gwt[:], gw.rearrange("(c p) e -> p c e", p=128), "gwld")
            pb.wait("tensor", "gwld", pb.cur("gwld"))
            for c in range(S // 128):
                r0 = c * 128
                pb.wait("sync", "q_vector", pb.tick("vector"))
                pb.wait("sync", "q_scalar", pb.tick("scalar"))
                lv = pb.dma("sync", xc[:], x_in[r0:r0 + 128, :], "xld_s0")
                pb.wait("vector", "xld_s0", lv)
                pb.wait("scalar", "xld_s0", lv)
                hv = emit_ln_chunk(pb, xc[:], hc[:], stats, hc[:])
                pb.op("vector", lambda E, c=c: E.tensor_copy(h_hi[:, c, :], hc[:]))
                pb.op("vector", lambda E, c=c: E.tensor_tensor(xc[:], hc[:], h_hi[:, c, :],
                                                               ALU.subtract))
                hv2 = pb.op("vector", lambda E, c=c: E.tensor_copy(h_lo[:, c, :], xc[:]))
                pb.wait("tensor", "q_vector", hv2)
                bank_l = p[2]
                for dch in range(8):
                    bank = p[dch % 2]
                    tv = pb.op("tensor", lambda E, bank=bank, dch=dch:
                               E.transpose(bank[:, 0:128], hc[:, dch * 128:(dch + 1) * 128],
                                           idn[:]))
                    pb.wait("vector", "q_tensor", tv)
                    ev = pb.op("vector", lambda E, bank=bank, dch=dch:
                               E.tensor_copy(hc[:, dch * 128:(dch + 1) * 128], bank[:, 0:128]))
                    pb.wait("tensor", "q_vector", ev)
                    pb.op("tensor", lambda E, bank=bank_l, dch=dch:
                          E.matmul(bank[:, 0:NE], hc[:, dch * 128:(dch + 1) * 128],
                                   gwt[:, dch, :], start=(dch == 0), stop=(dch == 7)))
                mmv = pb.tick("tensor")
                pb.wait("vector", "q_tensor", mmv)
                lvv = pb.op("vector", lambda E, c=c, bank=bank_l:
                            E.tensor_copy(logits_tm[:, c, :], bank[:, 0:NE]))
                pb.wait("tensor", "q_vector", lvv)

            pb.phase()
            with pb.scope():
                a_em = sbuf(pb, "a_em", (8, S))
                m_em = sbuf(pb, "m_em", (8, S), BF16)
                ones8 = sbuf(pb, "ones8", (8, S), BF16)
                th = sbuf(pb, "th", (8, 8))
                pb.op("vector", lambda E: E.memset(ones8[:], 1.0))
                for c in range(32):
                    pb.op("vector", lambda E, c=c: E.reduce_max(sm[:, 0:1], logits_tm[:, c, :],
                                                                axis=AX.X))
                    mv = pb.op("vector", lambda E: E.tensor_scalar(sm[:, 1:2], sm[:, 0:1], -1.0,
                                                                   None, ALU.mult))
                    pb.wait("scalar", "q_vector", mv)
                    sv = pb.op("scalar", lambda E, c=c:
                               E.activation(aff_tm[:, c, :], logits_tm[:, c, :], AF.Exp,
                                            bias=sm[:, 1:2], accum_out=sm[:, 2:3]))
                    pb.wait("vector", "q_scalar", sv)
                    pb.op("vector", lambda E: E.reciprocal(sm[:, 3:4], sm[:, 2:3]))
                    pb.op("vector", lambda E, c=c:
                          E.tensor_scalar(aff_tm[:, c, :], aff_tm[:, c, :], sm[:, 3:4],
                                          None, ALU.mult))
                    pb.op("vector", lambda E, c=c: E.tensor_copy(affh_tm[:, c, :],
                                                                 aff_tm[:, c, :]))
                    av2 = pb.op("vector", lambda E, c=c:
                                E.scalar_tensor_tensor(affl_tm[:, c, :], affh_tm[:, c, :], -1.0,
                                                       aff_tm[:, c, :], ALU.mult, ALU.add))
                    pb.wait("tensor", "q_vector", av2)
                    bank = p[3]
                    tv = pb.op("tensor", lambda E, c=c, bank=bank:
                               E.transpose(bank[0:NE, 0:128], aff_tm[:, c, :], idn[:]))
                    pb.wait("vector", "q_tensor", tv)
                    ev = pb.op("vector", lambda E, c=c, bank=bank:
                               E.tensor_copy(a_em[:, c * 128:(c + 1) * 128], bank[0:NE, 0:128]))
                    pb.wait("tensor", "q_vector", ev)
                pb.op("vector", lambda E: E.memset(th[:, 0:1], 0.0))
                pb.op("vector", lambda E: E.memset(th[:, 1:2], 1.0))
                for it in range(40):
                    pb.op("vector", lambda E: E.tensor_tensor(th[:, 2:3], th[:, 0:1], th[:, 1:2],
                                                              ALU.add))
                    pb.op("vector", lambda E: E.tensor_scalar(th[:, 2:3], th[:, 2:3], 0.5, None,
                                                              ALU.mult))
                    pb.op("vector", lambda E: E.tensor_scalar(m_em[:], a_em[:], th[:, 2:3], None,
                                                              ALU.is_ge))
                    pb.op("vector", lambda E: E.reduce_sum(th[:, 3:4], m_em[:], axis=AX.X))
                    pb.op("vector", lambda E: E.tensor_scalar(th[:, 4:5], th[:, 3:4], float(KSEL),
                                                              None, ALU.is_ge))
                    pb.op("vector", lambda E: E.tensor_scalar(th[:, 5:6], th[:, 4:5], -1.0, 1.0,
                                                              ALU.mult, ALU.add))
                    pb.op("vector", lambda E: E.tensor_tensor(th[:, 6:7], th[:, 0:1], th[:, 5:6],
                                                              ALU.mult))
                    pb.op("vector", lambda E: E.tensor_tensor(th[:, 7:8], th[:, 2:3], th[:, 4:5],
                                                              ALU.mult))
                    pb.op("vector", lambda E: E.tensor_tensor(th[:, 0:1], th[:, 6:7], th[:, 7:8],
                                                              ALU.add))
                    pb.op("vector", lambda E: E.tensor_tensor(th[:, 6:7], th[:, 1:2], th[:, 4:5],
                                                              ALU.mult))
                    pb.op("vector", lambda E: E.tensor_tensor(th[:, 7:8], th[:, 2:3], th[:, 5:6],
                                                              ALU.mult))
                    pb.op("vector", lambda E: E.tensor_tensor(th[:, 1:2], th[:, 6:7], th[:, 7:8],
                                                              ALU.add))
                pb.op("vector", lambda E: E.tensor_scalar(m_em[:], a_em[:], th[:, 0:1], None,
                                                          ALU.is_ge))
                pb.op("vector", lambda E: E.tensor_tensor_scan(slots_em[:], ones8[:], m_em[:],
                                                               0.0, ALU.mult, ALU.add))
                pb.op("vector", lambda E: E.tensor_scalar(a_em[:], slots_em[:], float(KSEL),
                                                          None, ALU.is_le))
                pb.op("vector", lambda E: E.tensor_tensor(m_em[:], m_em[:], a_em[:], ALU.mult))
                pb.op("vector", lambda E: E.tensor_scalar(slots_em[:], slots_em[:], 1.0, None,
                                                          ALU.subtract))
                pb.op("vector", lambda E: E.tensor_tensor(slots_em[:], slots_em[:], m_em[:],
                                                          ALU.mult))
                pb.op("vector", lambda E: E.tensor_scalar(a_em[:], m_em[:], -1.0, 1.0, ALU.mult,
                                                          ALU.add))
                pb.op("vector", lambda E: E.tensor_scalar(a_em[:], a_em[:], -7.0, None, ALU.mult))
                sev = pb.op("vector", lambda E: E.tensor_tensor(slots_em[:], slots_em[:], a_em[:],
                                                                ALU.add))
                pb.wait("tensor", "q_vector", sev)
                for c in range(32):
                    bank = p[3]
                    tv = pb.op("tensor", lambda E, c=c, bank=bank:
                               E.transpose(bank[:, 0:4], slots_em[0:4, c * 128:(c + 1) * 128],
                                           idn[0:4, 0:4]))
                    pb.wait("vector", "q_tensor", tv)
                    ev = pb.op("vector", lambda E, c=c, bank=bank:
                               E.tensor_copy(slots_tm[:, c, :], bank[:, 0:4]))
                    pb.wait("tensor", "q_vector", ev)
            pb.phase()

            with pb.scope():
                selT = sbuf(pb, "selT", (128, 32, 512), BF16)
                xso = sbuf(pb, "xso", (128, 2, 512))
                gatro = sbuf(pb, "gatro", (1, 512))
                for el in range(4):
                    for st in range(2):
                        bv = 0
                        for c in range(32):
                            bv = pb.op("vector", lambda E, el=el, st=st, c=c:
                                       E.tensor_scalar(selT[:, c, :], srow[:, st, :],
                                                       slots_tm[:, c, el:el + 1], None,
                                                       ALU.is_equal))
                        pb.wait("tensor", "q_vector", bv)
                        for c in range(32):
                            for dch in range(8):
                                for hl in range(2):
                                    hsrc = h_hi if hl == 0 else h_lo
                                    pb.op("tensor", lambda E, c=c, dch=dch, hsrc=hsrc, hl=hl:
                                          E.matmul(p[dch][:, 0:512],
                                                   hsrc[:, c, dch * 128:(dch + 1) * 128],
                                                   selT[:, c, :],
                                                   start=(c == 0 and hl == 0),
                                                   stop=(c == 31 and hl == 1)))
                        mmv = pb.tick("tensor")
                        pb.wait("vector", "q_tensor", mmv)
                        for dch in range(8):
                            sl = dch % 2
                            pb.wait("vector", "xsst_s%d" % sl, pb.cur("xsst_s%d" % sl))
                            vv = pb.op("vector", lambda E, dch=dch, sl=sl:
                                       E.tensor_copy(xso[:, sl, :], p[dch][:, 0:512]))
                            pb.wait("gpsimd", "q_vector", vv)
                            pb.dma("gpsimd", xsel_d[el, dch * 128:(dch + 1) * 128,
                                                    st * 512:(st + 1) * 512],
                                   xso[:, sl, :], "xsst_s%d" % sl)
                        pb.wait("tensor", "q_vector", pb.tick("vector"))
                        for c in range(32):
                            for hl in range(2):
                                asrc = affh_tm if hl == 0 else affl_tm
                                pb.op("tensor", lambda E, c=c, el=el, asrc=asrc, hl=hl:
                                      E.matmul(p[0][0:1, 0:512], asrc[:, c, el:el + 1],
                                               selT[:, c, :],
                                               start=(c == 0 and hl == 0),
                                               stop=(c == 31 and hl == 1)))
                        mmv = pb.tick("tensor")
                        pb.wait("vector", "q_tensor", mmv)
                        pb.wait("vector", "gtst", pb.cur("gtst"))
                        gvv = pb.op("vector", lambda E: E.tensor_copy(gatro[:], p[0][0:1, 0:512]))
                        pb.wait("gpsimd", "q_vector", gvv)
                        pb.dma("gpsimd",
                               gat_d[el].rearrange("(a s) -> a s", a=1)[:, st * 512:(st + 1) * 512],
                               gatro[:], "gtst")
        pb.phase()

        with pb.scope():
            xselb = sbuf(pb, "xselb", (128, 8, KSEL))
            wslab = sbuf(pb, "wslab", (128, 2, 8, 128))
            wdsl = sbuf(pb, "wdsl", (128, 2, 512))
            gvbuf = sbuf(pb, "gvbuf", (128, KSEL))
            hblk = sbuf(pb, "hblk", (128, 16, KSEL))
            gatc = sbuf(pb, "gatc", (128, 8))
            gro = sbuf(pb, "gro", (1, KSEL))
            ctf = sbuf(pb, "ctf", (128, 512))
            cth = sbuf(pb, "cth", (128, 2, 512), BF16)
            ctl = sbuf(pb, "ctl", (128, 2, 512), BF16)
            for el in range(4):
                pb.phase()
                pb.dma("sync", xselb[:], xsel_d[el].rearrange("(c p) s -> p c s", p=128), "xsl")
                lvg = pb.dma("sync", gro[:], gat_d[el].rearrange("(a s) -> a s", a=1), "gtl")
                pb.wait("tensor", "xsl", pb.cur("xsl"))
                pb.wait("tensor", "gtl", pb.cur("gtl"))
                for sc in range(8):
                    bank = p[6]
                    tv = pb.op("tensor", lambda E, sc=sc, bank=bank:
                               E.transpose(bank[:, 0:1], gro[:, sc * 128:(sc + 1) * 128],
                                           idn[0:1, 0:1]))
                    pb.wait("vector", "q_tensor", tv)
                    ev = pb.op("vector", lambda E, sc=sc, bank=bank:
                               E.tensor_copy(gatc[:, sc:sc + 1], bank[:, 0:1]))
                    pb.wait("tensor", "q_vector", ev)
                for ic in range(16):
                    sl = ic % 2
                    pb.wait("sync", "q_tensor", pb.tick("tensor"))
                    pb.dma("sync", wslab[:, sl, :, :],
                           Wg[el, :, ic * 128:(ic + 1) * 128].rearrange("(c p) i -> p c i", p=128),
                           "wg_s%d" % sl)
                    pb.wait("tensor", "wg_s%d" % sl, pb.cur("wg_s%d" % sl))
                    pb.wait("tensor", "q_scalar", pb.tick("scalar"))
                    pb.wait("scalar", "q_vector", pb.tick("vector"))
                    for st in range(2):
                        bank = p[2 + st]
                        mmv = 0
                        for k in range(8):
                            mmv = pb.op("tensor", lambda E, k=k, st=st, bank=bank, sl=sl:
                                        E.matmul(bank[:, 0:512], wslab[:, sl, k, :],
                                                 xselb[:, k, st * 512:(st + 1) * 512],
                                                 start=(k == 0), stop=(k == 7)))
                        pb.wait("scalar", "q_tensor", mmv)
                        pb.op("scalar", lambda E, st=st, bank=bank:
                              E.activation(gvbuf[:, st * 512:(st + 1) * 512], bank[:, 0:512],
                                           AF.Silu))
                    sv = pb.tick("scalar")
                    pb.wait("sync", "q_tensor", pb.tick("tensor"))
                    pb.dma("sync", wslab[:, sl, :, :],
                           Wu[el, :, ic * 128:(ic + 1) * 128].rearrange("(c p) i -> p c i", p=128),
                           "wu_s%d" % sl)
                    pb.wait("tensor", "wu_s%d" % sl, pb.cur("wu_s%d" % sl))
                    pb.wait("tensor", "q_scalar", sv)
                    for st in range(2):
                        bank = p[4 + st]
                        mmv = 0
                        for k in range(8):
                            mmv = pb.op("tensor", lambda E, k=k, st=st, bank=bank, sl=sl:
                                        E.matmul(bank[:, 0:512], wslab[:, sl, k, :],
                                                 xselb[:, k, st * 512:(st + 1) * 512],
                                                 start=(k == 0), stop=(k == 7)))
                        pb.wait("vector", "q_tensor", mmv)
                        pb.wait("vector", "q_scalar", sv)
                        pb.op("vector", lambda E, ic=ic, st=st, bank=bank:
                              E.tensor_tensor(hblk[:, ic, st * 512:(st + 1) * 512],
                                              gvbuf[:, st * 512:(st + 1) * 512],
                                              bank[:, 0:512], ALU.mult))
                    pb.wait("tensor", "q_vector", pb.tick("vector"))
                for dh in range(2):
                    pb.phase()
                    for ic in range(16):
                        sl = ic % 2
                        pb.wait("sync", "q_tensor", pb.tick("tensor"))
                        pb.dma("sync", wdsl[:, sl, :],
                               Wd[el, ic * 128:(ic + 1) * 128, dh * 512:(dh + 1) * 512],
                               "wd_s%d" % sl)
                        pb.wait("tensor", "wd_s%d" % sl, pb.cur("wd_s%d" % sl))
                        for sg in range(8):
                            pb.op("tensor", lambda E, ic=ic, sg=sg, sl=sl:
                                  E.matmul(p[sg][:, 0:512], hblk[:, ic, sg * 128:(sg + 1) * 128],
                                           wdsl[:, sl, :], start=(ic == 0), stop=(ic == 15)))
                    mmv = pb.tick("tensor")
                    pb.wait("vector", "q_tensor", mmv)
                    for sg in range(8):
                        sl = sg % 2
                        pb.wait("vector", "cst_s%d" % sl, pb.cur("cst_s%d" % sl))
                        pb.op("vector", lambda E, sg=sg:
                              E.tensor_scalar(ctf[:], p[sg][:, 0:512], gatc[:, sg:sg + 1], 0.5,
                                              ALU.mult, ALU.mult))
                        pb.op("vector", lambda E, sl=sl: E.tensor_copy(cth[:, sl, :], ctf[:]))
                        vv = pb.op("vector", lambda E, sl=sl:
                                   E.scalar_tensor_tensor(ctl[:, sl, :], cth[:, sl, :],
                                                          -1.0, ctf[:], ALU.mult, ALU.add))
                        pb.wait("gpsimd", "q_vector", vv)
                        pb.dma("gpsimd", ctbh_d[el, sg * 128:(sg + 1) * 128,
                                                dh * 512:(dh + 1) * 512], cth[:, sl, :],
                               "cst_s%d" % sl)
                        pb.dma("gpsimd", ctbl_d[el, sg * 128:(sg + 1) * 128,
                                                dh * 512:(dh + 1) * 512], ctl[:, sl, :],
                               "cst_s%d" % sl)
        pb.phase()

        with pb.scope():
            ctbh = sbuf(pb, "ctbh", (128, 2, 8, D), BF16)
            ctbl = sbuf(pb, "ctbl", (128, 2, 8, D), BF16)
            sbc = sbuf(pb, "sbc", (128, 2, S))
            selS = sbuf(pb, "selS", (128, 2, 8, 512), BF16)
            yo = sbuf(pb, "yo", (128, D))
            yi = sbuf(pb, "yi", (128, D))
            eh = sbuf(pb, "eh", (8, 512))
            ti = sbuf(pb, "ti", (8, 512), I32)
            tp = sbuf(pb, "tp", (8, 1), I32)
            tpf = sbuf(pb, "tpf", (8, 1))
            pb.op("gpsimd", lambda E: E.iota(ti[:], [[1, 4], [0, 128]], base=0,
                                             channel_multiplier=0))
            gv = pb.op("gpsimd", lambda E: E.iota(tp[:], [[0, 1]], base=0,
                                                  channel_multiplier=1))
            pb.wait("vector", "q_gpsimd", gv)
            pb.op("vector", lambda E: E.tensor_copy(tpf[:], tp[:]))
            ev0 = pb.op("vector", lambda E: E.tensor_scalar(eh[:], ti[:], tpf[:, 0:1], None,
                                                            ALU.is_equal))
            pb.wait("tensor", "q_vector", ev0)
            for grp in range(2):
                pb.phase()
                for gi in range(2):
                    el = grp * 2 + gi
                    pb.dma("sync", ctbh[:, gi, :, :],
                           ctbh_d[el].rearrange("(g p) d -> p g d", p=128), "ctl_ld")
                    pb.dma("sync", ctbl[:, gi, :, :],
                           ctbl_d[el].rearrange("(g p) d -> p g d", p=128), "ctl_ld")
                    for c0 in range(0, S, 512):
                        bank = p[6]
                        tv = pb.op("tensor", lambda E, el=el, c0=c0, bank=bank:
                                   E.matmul(bank[:, 0:512], eh[0:8, el * 128:(el + 1) * 128],
                                            slots_em[0:8, c0:c0 + 512], start=True, stop=True))
                        pb.wait("vector", "q_tensor", tv)
                        cv = pb.op("vector", lambda E, gi=gi, c0=c0, bank=bank:
                                   E.tensor_copy(sbc[:, gi, c0:c0 + 512], bank[:, 0:512]))
                        pb.wait("tensor", "q_vector", cv)
                pb.wait("tensor", "ctl_ld", pb.cur("ctl_ld"))
                for tg in range(8):
                    bv = 0
                    for gi in range(2):
                        for sc in range(8):
                            bv = pb.op("vector", lambda E, gi=gi, sc=sc, tg=tg:
                                       E.tensor_scalar(selS[:, gi, sc, :],
                                                       sbc[:, gi, tg * 512:(tg + 1) * 512],
                                                       scol[:, sc:sc + 1], None, ALU.is_equal))
                    pb.wait("tensor", "q_vector", bv)
                    for tc in range(4):
                        for dt in range(2):
                            bank = p[tc * 2 + dt]
                            nmm = 0
                            for gi in range(2):
                                for sc in range(8):
                                    for hl in range(2):
                                        csrc = ctbh if hl == 0 else ctbl
                                        pb.op("tensor", lambda E, gi=gi, sc=sc, hl=hl, tc=tc,
                                              dt=dt, bank=bank, csrc=csrc, nmm=nmm:
                                              E.matmul(bank[:, 0:512],
                                                       selS[:, gi, sc, tc * 128:(tc + 1) * 128],
                                                       csrc[:, gi, sc, dt * 512:(dt + 1) * 512],
                                                       start=(nmm == 0), stop=(nmm == 31)))
                                        nmm += 1
                    mmv = pb.tick("tensor")
                    pb.wait("vector", "q_tensor", mmv)
                    for tc in range(4):
                        r0 = tg * 512 + tc * 128
                        if grp == 1:
                            pb.wait("sync", "q_vector", pb.tick("vector"))
                            lv = pb.dma("sync", yi[:], y[r0:r0 + 128, :], "yrd_s0")
                            pb.wait("vector", "yrd_s0", lv)
                        pb.wait("vector", "yst", pb.cur("yst"))
                        for dt in range(2):
                            bank = p[tc * 2 + dt]
                            if grp == 0:
                                pb.op("vector", lambda E, dt=dt, bank=bank:
                                      E.tensor_copy(yo[:, dt * 512:(dt + 1) * 512],
                                                    bank[:, 0:512]))
                            else:
                                pb.op("vector", lambda E, dt=dt, bank=bank:
                                      E.tensor_tensor(yo[:, dt * 512:(dt + 1) * 512],
                                                      bank[:, 0:512],
                                                      yi[:, dt * 512:(dt + 1) * 512], ALU.add))
                        vv = pb.tick("vector")
                        pb.wait("gpsimd", "q_vector", vv)
                        pb.dma("gpsimd", y[r0:r0 + 128, :], yo[:], "yst")
        pb.wait("gpsimd", "yst", pb.cur("yst"))
    pb.prefix = _oldpfx


def build_moe():
    S = SEQ
    nc = bass.Bass("TRN2", target_bir_lowering=False, debug=False)
    x_in = nc.dram_tensor("x", [S, D], F32, kind="ExternalInput").ap()
    gw = nc.dram_tensor("gw", [D, NE], F32, kind="ExternalInput").ap()
    Wg = nc.dram_tensor("Wg", [4, D, NI], F32, kind="ExternalInput").ap()
    Wu = nc.dram_tensor("Wu", [4, D, NI], F32, kind="ExternalInput").ap()
    Wd = nc.dram_tensor("Wd", [4, NI, D], F32, kind="ExternalInput").ap()
    y = nc.dram_tensor("y", [S, D], F32, kind="ExternalOutput").ap()
    est = contextlib.ExitStack()
    with est:
        pb = PB(nc, est)
        idn = make_identity(pb)
        p = psum8(pb)
        emit_moe(pb, p, idn, x_in, gw, Wg, Wu, Wd, y)
        pb.build()
    return nc


def emit_l1attn(pb, p, idn, x3, pa, pbx, wqkvT, wprojT, bproj, rpeq, rpek, rpev, y, N, NQ,
                q0, tag="", HH=NH, resid=True):
    nc = pb.nc
    _oldpfx = pb.prefix
    pb.prefix = _oldpfx + tag
    H, hd = HH, HD
    HQ = H * 64
    QT = HQ // 512
    KCH = HQ // 128
    NC = N // 128
    NJT = N // 512
    NCQ = NQ // 128
    SW = 64 + N + 64
    with pb.scope():
        x4_d = nc.dram_tensor(pb.prefix + "x4_d", [N, D], F32, kind="Internal").ap()
        hT_d = nc.dram_tensor(pb.prefix + "hT_d", [D, N], F32, kind="Internal").ap()
        qkTh_d = nc.dram_tensor(pb.prefix + "qkTh_d", [2 * HQ, N], BF16, kind="Internal").ap()
        qkTl_d = nc.dram_tensor(pb.prefix + "qkTl_d", [2 * HQ, N], BF16, kind="Internal").ap()
        vth_d = nc.dram_tensor(pb.prefix + "vth_d", [N, HQ], BF16, kind="Internal").ap()
        vtl_d = nc.dram_tensor(pb.prefix + "vtl_d", [N, HQ], BF16, kind="Internal").ap()
        kldbm_d = nc.dram_tensor(pb.prefix + "kldbm_d", [NB, N + 128], F32, kind="Internal").ap()
        band_d = nc.dram_tensor(pb.prefix + "band_d", [128, 256], F32, kind="Internal").ap()
        ctx_d = nc.dram_tensor(pb.prefix + "ctx_d", [HQ, NQ], F32, kind="Internal").ap()
        ones1 = sbuf(pb, "ones1", (1, 128))
        onescol = sbuf(pb, "onescol", (128, 1))
        pb.op("vector", lambda E: E.memset(ones1[:], 1.0))
        ov = pb.op("vector", lambda E: E.memset(onescol[:], 1.0))
        Tm = {}
        Tl = {}
        dclf = sbuf(pb, "dclf", (128, 512))
        with pb.scope():
            dcli = sbuf(pb, "dcli", (128, 512), I32)
            gv = pb.op("gpsimd", lambda E: E.iota(dcli[:], [[1, 512]], base=0,
                                                  channel_multiplier=-1))
            pb.wait("vector", "q_gpsimd", gv)
            pb.op("vector", lambda E: E.tensor_copy(dclf[:], dcli[:]))
        for s in range(-1, 4):
            Tm[s] = sbuf(pb, "tm_s%d" % (s + 1), (128, 512))
            pb.op("vector", lambda E, s=s: E.tensor_scalar(Tm[s][:], dclf[:],
                                                           float(32 + 128 * s), None, ALU.is_ge))
        for s in range(0, 5):
            Tl[s] = sbuf(pb, "tl_s%d" % s, (128, 512))
            mv = pb.op("vector", lambda E, s=s: E.tensor_scalar(Tl[s][:], dclf[:],
                                                                float(-32 + 128 * s), None,
                                                                ALU.is_le))
        pb.wait("tensor", "q_vector", mv)
        pb.wait("sync", "q_vector", mv)

        with pb.scope():
            xc = sbuf(pb, "p0xc", (128, D))
            x2c = sbuf(pb, "p0x2", (128, D))
            hc = sbuf(pb, "p0hc", (128, D))
            stats = sbuf(pb, "p0st", (128, 8))
            for c in range(NC):
                r0 = c * 128
                pb.wait("sync", "q_vector", pb.tick("vector"))
                pb.wait("sync", "q_scalar", pb.tick("scalar"))
                pb.wait("sync", "x4st", pb.cur("x4st"))
                pb.wait("sync", "hTst", pb.cur("hTst"))
                lv1 = pb.dma("sync", xc[:], x3[r0:r0 + 128, :], "xl1")
                pb.wait("vector", "xl1", lv1)
                pb.wait("scalar", "xl1", lv1)
                if pa is not None:
                    lv2 = pb.dma("sync", x2c[:], pa[r0:r0 + 128, :], "xl2")
                    pb.wait("vector", "xl2", lv2)
                    a1 = pb.op("vector", lambda E: E.tensor_tensor(xc[:], xc[:], x2c[:],
                                                                   ALU.add))
                    pb.wait("sync", "q_vector", a1)
                    lv3 = pb.dma("sync", x2c[:], pbx[r0:r0 + 128, :], "xl2")
                    pb.wait("vector", "xl2", lv3)
                    av = pb.op("vector", lambda E: E.tensor_tensor(xc[:], xc[:], x2c[:],
                                                                   ALU.add))
                    pb.wait("gpsimd", "q_vector", av)
                    pb.wait("scalar", "q_vector", av)
                    pb.dma("gpsimd", x4_d[r0:r0 + 128, :], xc[:], "x4st")
                emit_ln_chunk(pb, xc[:], hc[:], stats, hc[:])
                pb.wait("tensor", "q_vector", pb.tick("vector"))
                pb.wait("tensor", "hTst", pb.cur("hTst"))
                for dch in range(8):
                    bank = p[dch % 2]
                    tv = pb.op("tensor", lambda E, bank=bank, dch=dch:
                               E.transpose(bank[:, 0:128], hc[:, dch * 128:(dch + 1) * 128],
                                           idn[:]))
                    pb.wait("vector", "q_tensor", tv)
                    ev = pb.op("vector", lambda E, bank=bank, dch=dch:
                               E.tensor_copy(x2c[:, dch * 128:(dch + 1) * 128], bank[:, 0:128]))
                    pb.wait("tensor", "q_vector", ev)
                    pb.wait("gpsimd", "q_vector", ev)
                    pb.dma("gpsimd", hT_d[dch * 128:(dch + 1) * 128, r0:r0 + 128],
                           x2c[:, dch * 128:(dch + 1) * 128], "hTst")
        pb.phase()

        with pb.scope():
            wsl = sbuf(pb, "p1w", (128, 8, 512))
            htb = sbuf(pb, "p1h", (128, 2, 8, 128))
            ev1 = sbuf(pb, "p1e", (128, 2, 512))
            tsb = sbuf(pb, "p1t", (128, 2, 128))
            tsh = sbuf(pb, "p1th", (128, 2, 128), BF16)
            tsl = sbuf(pb, "p1tl", (128, 2, 128), BF16)
            evh = sbuf(pb, "p1eh", (128, 2, 512), BF16)
            evl = sbuf(pb, "p1el", (128, 2, 512), BF16)
            for ft in range(3 * QT):
                pb.wait("sync", "q_tensor", pb.tick("tensor"))
                pb.dma("sync", wsl[:],
                       wqkvT[:, ft * 512:(ft + 1) * 512].rearrange("(c p) f -> p c f", p=128),
                       "p1wl")
                pb.wait("tensor", "p1wl", pb.cur("p1wl"))
                for tc in range(NC):
                    sl = tc % 2
                    pb.wait("sync", "q_tensor", pb.tick("tensor"))
                    pb.dma("sync", htb[:, sl, :, :],
                           hT_d.rearrange("(c p) n -> p c n", p=128)[:, :,
                                                                    tc * 128:(tc + 1) * 128],
                           "p1hl_s%d" % sl)
                    pb.wait("tensor", "p1hl_s%d" % sl, pb.cur("p1hl_s%d" % sl))
                    pb.wait("tensor", "q_vector", pb.tick("vector"))
                    bank = p[sl]
                    for k in range(8):
                        mmv = pb.op("tensor", lambda E, k=k, sl=sl, bank=bank:
                                    E.matmul(bank[:, 0:512], htb[:, sl, k, :], wsl[:, k, :],
                                             start=(k == 0), stop=(k == 7)))
                    pb.wait("vector", "q_tensor", mmv)
                    pb.wait("vector", "vst", pb.cur("vst"))
                    if ft < QT:
                        vv = pb.op("vector", lambda E, sl=sl, bank=bank:
                                   E.tensor_scalar(ev1[:, sl, :], bank[:, 0:512], 0.125, None,
                                                   ALU.mult))
                    else:
                        vv = pb.op("vector", lambda E, sl=sl, bank=bank:
                                   E.tensor_copy(ev1[:, sl, :], bank[:, 0:512]))
                    if ft >= 2 * QT:
                        pb.op("vector", lambda E, sl=sl:
                              E.tensor_copy(evh[:, sl, :], ev1[:, sl, :]))
                        vv = pb.op("vector", lambda E, sl=sl:
                                   E.scalar_tensor_tensor(evl[:, sl, :], evh[:, sl, :], -1.0,
                                                          ev1[:, sl, :], ALU.mult, ALU.add))
                        pb.wait("gpsimd", "q_vector", vv)
                        cs = slice((ft - 2 * QT) * 512, (ft - 2 * QT + 1) * 512)
                        pb.dma("gpsimd", vth_d[tc * 128:(tc + 1) * 128, cs], evh[:, sl, :],
                               "vst")
                        pb.dma("gpsimd", vtl_d[tc * 128:(tc + 1) * 128, cs], evl[:, sl, :],
                               "vst")
                    else:
                        pb.wait("tensor", "q_vector", vv)
                        for sub in range(4):
                            tb = p[2 + sub % 2]
                            pb.wait("tensor", "q_vector", pb.tick("vector"))
                            tv = pb.op("tensor", lambda E, sl=sl, sub=sub, tb=tb:
                                       E.transpose(tb[:, 0:128],
                                                   ev1[:, sl, sub * 128:(sub + 1) * 128],
                                                   idn[:]))
                            pb.wait("vector", "q_tensor", tv)
                            pb.wait("vector", "qkst_s%d" % (sub % 2),
                                    pb.cur("qkst_s%d" % (sub % 2)))
                            ev = pb.op("vector", lambda E, sub=sub, tb=tb:
                                       E.tensor_copy(tsb[:, sub % 2, :], tb[:, 0:128]))
                            pb.op("vector", lambda E, sub=sub:
                                  E.tensor_copy(tsh[:, sub % 2, :], tsb[:, sub % 2, :]))
                            ev = pb.op("vector", lambda E, sub=sub:
                                       E.scalar_tensor_tensor(tsl[:, sub % 2, :],
                                                              tsh[:, sub % 2, :], -1.0,
                                                              tsb[:, sub % 2, :], ALU.mult,
                                                              ALU.add))
                            pb.wait("gpsimd", "q_vector", ev)
                            rs2 = slice(ft * 512 + sub * 128, ft * 512 + (sub + 1) * 128)
                            cs2 = slice(tc * 128, (tc + 1) * 128)
                            pb.dma("gpsimd", qkTh_d[rs2, cs2], tsh[:, sub % 2, :],
                                   "qkst_s%d" % (sub % 2))
                            pb.dma("gpsimd", qkTl_d[rs2, cs2], tsl[:, sub % 2, :],
                                   "qkst_s%d" % (sub % 2))
        pb.phase()

        with pb.scope():
            kp = sbuf(pb, "kp", (128, N), BF16)
            qp_a = sbuf(pb, "qp_a", (128, NQ), BF16)
            qp_b = sbuf(pb, "qp_b", (128, NQ), BF16)
            vh = sbuf(pb, "vh", (128, NC, 64), BF16)
            vl = sbuf(pb, "vl", (128, NC, 64), BF16)
            pth = sbuf(pb, "pth", (128, 2, 512), BF16)
            ptl = sbuf(pb, "ptl", (128, 2, 512), BF16)
            rpk = sbuf(pb, "rpk", (64, NB))
            rpq = sbuf(pb, "rpq", (64, NB))
            rpkp = sbuf(pb, "rpkp", (128, NB), BF16)
            rpqp = sbuf(pb, "rpqp", (128, NB), BF16)
            rpqp_b = sbuf(pb, "rpqp_b", (128, NB), BF16)
            lof63 = sbuf(pb, "lof63", (64, NB))
            qhb = sbuf(pb, "qhb", (128, 2), BF16)
            vaug = sbuf(pb, "vaug", (65, 64))
            qex_lo = sbuf(pb, "qex_lo", (4, NQ), BF16)
            qex_hi = sbuf(pb, "qex_hi", (4, NQ), BF16)
            kex_lo = sbuf(pb, "kex_lo", (4, N), BF16)
            kex_hi = sbuf(pb, "kex_hi", (4, N), BF16)
            dq = sbuf(pb, "dq", (128, NCQ))
            dkr = sbuf(pb, "dkr", (1, N))
            qld = sbuf(pb, "qld", (128, NCQ, NB))
            srow = sbuf(pb, "srow_sc", (128, 2, SW))
            qlt = sbuf(pb, "qlt", (128, NB))
            qcols = sbuf(pb, "qcols", (128, 9))
            bt = sbuf(pb, "bt", (128, NB))
            kbT = sbuf(pb, "kbT", (NB, 128))
            kb = sbuf(pb, "kb", (128, NB))
            tmp5 = sbuf(pb, "tmp5", (128, 512))
            mxs = sbuf(pb, "mxs", (128, 8))
            rss = sbuf(pb, "rss", (128, NJT))
            scc = sbuf(pb, "scc", (128, 8))
            pext = sbuf(pb, "pext", (128, 65))
            ptx = sbuf(pb, "ptx", (65, 128))
            ctxs = sbuf(pb, "ctxs", (128, 64))
            ctxT = sbuf(pb, "ctxT", (64, 128))

            def cls_tile(i0, j0):
                if j0 + 511 <= i0 - 32:
                    return ("lo", None)
                if j0 >= i0 + 159:
                    return ("hi", None)
                return ("st", (i0 - j0) // 128)

            for h in range(H):
                pb.phase()
                qs = slice(h * 64, (h + 1) * 64)
                ks = slice(HQ + h * 64, HQ + (h + 1) * 64)
                pb.dma("sync", kp[0:64, :], qkTh_d[ks, :], "hkl")
                pb.dma("sync", kp[64:128, :], qkTl_d[ks, :], "hkl")
                pb.dma("sync", qp_a[0:64, :], qkTh_d[qs, q0:q0 + NQ], "hkl")
                pb.dma("sync", qp_a[64:128, :], qkTl_d[qs, q0:q0 + NQ], "hkl")
                pb.dma("sync", qp_b[0:64, :], qkTl_d[qs, q0:q0 + NQ], "hkl")
                pb.dma("sync", qp_b[64:128, :], qkTh_d[qs, q0:q0 + NQ], "hkl")
                pb.dma("sync", vh[:],
                       vth_d.rearrange("(c p) f -> p c f", p=128)[:, :, h * 64:(h + 1) * 64],
                       "hkl")
                pb.dma("sync", vl[:],
                       vtl_d.rearrange("(c p) f -> p c f", p=128)[:, :, h * 64:(h + 1) * 64],
                       "hkl")
                pb.dma("sync", rpk[:], rpek[h], "hkl")
                pb.dma("sync", rpq[:], rpeq[h], "hkl")
                pb.dma("sync", vaug[0:63, :], rpev[h], "hkl")
                pb.dma("sync", vaug[63:64, :], rpev[h, 0:1, :], "hkl")
                lv = pb.dma("sync", vaug[64:65, :], rpev[h, 62:63, :], "hkl")
                pb.wait("tensor", "hkl", lv)
                pb.wait("vector", "hkl", lv)
                pb.op("vector", lambda E: E.tensor_copy(rpkp[0:64, :], rpk[:]))
                v1 = pb.op("vector", lambda E: E.scalar_tensor_tensor(lof63[:], rpkp[0:64, :],
                                                                      -1.0, rpk[:], ALU.mult,
                                                                      ALU.add))
                pb.wait("gpsimd", "q_vector", v1)
                pb.dma("gpsimd", rpkp[64:128, :], lof63[:], "rpl")
                pb.wait("vector", "rpl", pb.cur("rpl"))
                pb.op("vector", lambda E: E.tensor_copy(rpqp[0:64, :], rpq[:]))
                v1 = pb.op("vector", lambda E: E.scalar_tensor_tensor(lof63[:], rpqp[0:64, :],
                                                                      -1.0, rpq[:], ALU.mult,
                                                                      ALU.add))
                v2 = pb.op("vector", lambda E: E.tensor_copy(rpqp_b[0:64, :], lof63[:]))
                pb.wait("gpsimd", "q_vector", v2)
                pb.dma("gpsimd", rpqp[64:128, :], lof63[:], "rpl")
                pb.dma("gpsimd", rpqp_b[64:128, :], rpq[:], "rpl")
                lv = pb.cur("rpl")
                pb.wait("tensor", "rpl", lv)
                pb.wait("tensor", "q_vector", v2)
                for ci in range(NCQ):
                    pb.op("tensor", lambda E, ci=ci:
                          E.matmul(p[5][:, 0:NB], qp_a[:, ci * 128:(ci + 1) * 128], rpkp[:],
                                   start=True, stop=False))
                    mmv = pb.op("tensor", lambda E, ci=ci:
                                E.matmul(p[5][:, 0:NB], qp_b[:, ci * 128:(ci + 1) * 128],
                                         rpkp[:], start=False, stop=True))
                    pb.wait("vector", "q_tensor", mmv)
                    vv = pb.op("vector", lambda E: E.tensor_copy(qlt[:], p[5][:, 0:NB]))
                    pb.op("vector", lambda E, ci=ci:
                          E.tensor_scalar(qld[:, ci, :], qlt[:], qlt[:, 0:1], None, ALU.subtract))
                    pb.op("vector", lambda E, ci=ci:
                          E.tensor_tensor(dq[:, ci:ci + 1], qlt[:, 62:63], qlt[:, 0:1],
                                          ALU.subtract))
                    pb.op("vector", lambda E: E.tensor_copy(qhb[:, 0:1], qlt[:, 0:1]))
                    pb.op("vector", lambda E: E.tensor_copy(qhb[:, 1:2], qlt[:, 62:63]))
                    pb.op("vector", lambda E: E.tensor_copy(qcols[:, 0:1], onescol[:]))
                    pb.op("vector", lambda E: E.tensor_copy(qcols[:, 1:2], qhb[:, 0:1]))
                    pb.op("vector", lambda E: E.scalar_tensor_tensor(qcols[:, 2:3], qhb[:, 0:1],
                                                                     -1.0, qlt[:, 0:1],
                                                                     ALU.mult, ALU.add))
                    pb.op("vector", lambda E: E.tensor_copy(qcols[:, 3:4], onescol[:]))
                    pb.op("vector", lambda E: E.tensor_copy(qcols[:, 4:5], onescol[:]))
                    pb.op("vector", lambda E: E.tensor_copy(qcols[:, 5:6], qhb[:, 1:2]))
                    pb.op("vector", lambda E: E.scalar_tensor_tensor(qcols[:, 6:7], qhb[:, 1:2],
                                                                     -1.0, qlt[:, 62:63],
                                                                     ALU.mult, ALU.add))
                    vv = pb.op("vector", lambda E: E.tensor_copy(qcols[:, 7:8], onescol[:]))
                    pb.wait("tensor", "q_vector", vv)
                    tv = pb.op("tensor", lambda E: E.transpose(p[4][0:4, 0:128], qcols[:, 0:4],
                                                               idn[:]))
                    tv = pb.op("tensor", lambda E: E.transpose(p[4][0:4, 128:256], qcols[:, 4:8],
                                                               idn[:]))
                    pb.wait("vector", "q_tensor", tv)
                    pb.op("vector", lambda E, ci=ci:
                          E.tensor_copy(qex_lo[0:4, ci * 128:(ci + 1) * 128], p[4][0:4, 0:128]))
                    vv = pb.op("vector", lambda E, ci=ci:
                               E.tensor_copy(qex_hi[0:4, ci * 128:(ci + 1) * 128],
                                             p[4][0:4, 128:256]))
                    pb.wait("tensor", "q_vector", vv)
                for cj in range(NC):
                    pb.op("tensor", lambda E, cj=cj:
                          E.matmul(p[5][:, 0:NB], kp[:, cj * 128:(cj + 1) * 128], rpqp[:],
                                   start=True, stop=False))
                    mmv = pb.op("tensor", lambda E, cj=cj:
                                E.matmul(p[5][:, 0:NB], kp[:, cj * 128:(cj + 1) * 128],
                                         rpqp_b[:], start=False, stop=True))
                    pb.wait("vector", "q_tensor", mmv)
                    pb.wait("vector", "kldst", pb.cur("kldst"))
                    vv = pb.op("vector", lambda E: E.tensor_scalar(qlt[:], p[5][:, 0:NB], 0.125,
                                                                   None, ALU.mult))
                    pb.op("vector", lambda E: E.tensor_scalar(bt[:], qlt[:], qlt[:, 0:1], None,
                                                              ALU.subtract))
                    pb.op("vector", lambda E: E.tensor_copy(qhb[:, 0:1], qlt[:, 0:1]))
                    pb.op("vector", lambda E: E.tensor_copy(qhb[:, 1:2], qlt[:, 62:63]))
                    pb.op("vector", lambda E: E.tensor_copy(qcols[:, 0:1], qhb[:, 0:1]))
                    pb.op("vector", lambda E: E.tensor_copy(qcols[:, 1:2], onescol[:]))
                    pb.op("vector", lambda E: E.tensor_copy(qcols[:, 2:3], onescol[:]))
                    pb.op("vector", lambda E: E.scalar_tensor_tensor(qcols[:, 3:4], qhb[:, 0:1],
                                                                     -1.0, qlt[:, 0:1],
                                                                     ALU.mult, ALU.add))
                    pb.op("vector", lambda E: E.tensor_copy(qcols[:, 4:5], qhb[:, 1:2]))
                    pb.op("vector", lambda E: E.tensor_copy(qcols[:, 5:6], onescol[:]))
                    pb.op("vector", lambda E: E.tensor_copy(qcols[:, 6:7], onescol[:]))
                    pb.op("vector", lambda E: E.scalar_tensor_tensor(qcols[:, 7:8], qhb[:, 1:2],
                                                                     -1.0, qlt[:, 62:63],
                                                                     ALU.mult, ALU.add))
                    vv = pb.op("vector", lambda E: E.tensor_tensor(qcols[:, 8:9], qlt[:, 62:63],
                                                                   qlt[:, 0:1], ALU.subtract))
                    pb.wait("tensor", "q_vector", vv)
                    tv = pb.op("tensor", lambda E: E.transpose(p[4][0:63, 0:128], bt[:], idn[:]))
                    tv = pb.op("tensor", lambda E: E.transpose(p[4][0:4, 128:256], qcols[:, 0:4],
                                                               idn[:]))
                    tv = pb.op("tensor", lambda E: E.transpose(p[4][0:4, 256:384], qcols[:, 4:8],
                                                               idn[:]))
                    tv = pb.op("tensor", lambda E: E.transpose(p[4][0:1, 384:512], qcols[:, 8:9],
                                                               idn[:]))
                    pb.wait("vector", "q_tensor", tv)
                    vv = pb.op("vector", lambda E: E.tensor_copy(kbT[:, :], p[4][0:63, 0:128]))
                    pb.op("vector", lambda E, cj=cj:
                          E.tensor_copy(kex_lo[0:4, cj * 128:(cj + 1) * 128], p[4][0:4, 128:256]))
                    pb.op("vector", lambda E, cj=cj:
                          E.tensor_copy(kex_hi[0:4, cj * 128:(cj + 1) * 128], p[4][0:4, 256:384]))
                    vv = pb.op("vector", lambda E, cj=cj:
                               E.tensor_copy(dkr[0:1, cj * 128:(cj + 1) * 128],
                                             p[4][0:1, 384:512]))
                    pb.wait("gpsimd", "q_vector", vv)
                    pb.wait("tensor", "q_vector", vv)
                    pb.dma("gpsimd", kldbm_d[:, 64 + cj * 128:64 + (cj + 1) * 128], kbT[:, :],
                           "kldst")
                sb_last = {0: 0, 1: 0}
                pvb_last = {2: 0, 3: 0}
                pv_done = [0, 0]
                exp_last = [0, 0]
                ptile_rd = {0: 0, 1: 0}
                kb_read = [0]
                sh_r = bass.AP(tensor=band_d.tensor, offset=33, ap=[[257, 128], [1, 63]])
                sh_w = bass.AP(tensor=band_d.tensor, offset=33, ap=[[257, 128], [1, 63]])

                def emit_scores(ci):
                    buf = ci % 2
                    i0 = q0 + ci * 128
                    kinds = [cls_tile(i0, jt * 512) for jt in range(NJT)]
                    pb.wait("vector", "q_tensor", pv_done[buf])
                    pb.wait("vector", "q_scalar", exp_last[buf])
                    pb.wait("vector", "bwd", pb.cur("bwd"))
                    pb.wait("vector", "bre", pb.cur("bre"))
                    pb.wait("vector", "pxd", pb.cur("pxd"))
                    for jt in range(NJT):
                        kind, s = kinds[jt]
                        bank = p[jt % 2]
                        qx = qex_hi if kind == "hi" else qex_lo
                        kx = kex_hi if kind == "hi" else kex_lo
                        pb.wait("tensor", "q_vector", sb_last[jt % 2])
                        pb.op("tensor", lambda E, jt=jt, ci=ci, bank=bank:
                              E.matmul(bank[:, 0:512], qp_a[:, ci * 128:(ci + 1) * 128],
                                       kp[:, jt * 512:(jt + 1) * 512], start=True, stop=False))
                        pb.op("tensor", lambda E, jt=jt, ci=ci, bank=bank:
                              E.matmul(bank[:, 0:512], qp_b[:, ci * 128:(ci + 1) * 128],
                                       kp[:, jt * 512:(jt + 1) * 512], start=False, stop=False))
                        mmv = pb.op("tensor", lambda E, jt=jt, ci=ci, bank=bank, qx=qx, kx=kx:
                                    E.matmul(bank[:, 0:512], qx[0:4, ci * 128:(ci + 1) * 128],
                                             kx[0:4, jt * 512:(jt + 1) * 512], start=False,
                                             stop=True))
                        pb.wait("vector", "q_tensor", mmv)
                        sb_last[jt % 2] = pb.op(
                            "vector", lambda E, jt=jt, bank=bank, buf=buf:
                            E.tensor_copy(srow[:, buf, 64 + jt * 512:64 + (jt + 1) * 512],
                                          bank[:, 0:512]))

                def emit_rest(ci):
                    buf = ci % 2
                    i0 = q0 + ci * 128
                    kinds = [cls_tile(i0, jt * 512) for jt in range(NJT)]
                    for jt in range(NJT):
                        kind, s = kinds[jt]
                        if kind != "st" or 32 + 128 * s > 511:
                            continue
                        sl = slice(64 + jt * 512, 64 + (jt + 1) * 512)
                        pb.op("vector", lambda E, sl=sl, s=s, ci=ci, buf=buf:
                              E.scalar_tensor_tensor(srow[:, buf, sl], Tm[s][:],
                                                     dq[:, ci:ci + 1], srow[:, buf, sl],
                                                     ALU.mult, ALU.add))
                        pb.wait("tensor", "q_vector", pb.tick("vector"))
                        mmv = pb.op("tensor", lambda E, jt=jt:
                                    E.matmul(p[5][:, 0:512], ones1[:, 0:128],
                                             dkr[0:1, jt * 512:(jt + 1) * 512], start=True,
                                             stop=True))
                        pb.wait("vector", "q_tensor", mmv)
                        pb.op("vector", lambda E, s=s: E.tensor_tensor(tmp5[:], p[5][:, 0:512],
                                                                       Tm[s][:], ALU.mult))
                        pb.op("vector", lambda E, sl=sl, buf=buf:
                              E.tensor_tensor(srow[:, buf, sl], srow[:, buf, sl], tmp5[:],
                                              ALU.add))
                    vv = pb.tick("vector")
                    pb.wait("sync", "q_vector", vv)
                    pb.wait("sync", "bre", pb.cur("bre"))
                    pb.wait("sync", "pxd", pb.cur("pxd"))
                    bw = pb.dma("sync", band_d[:, :], srow[:, buf, i0:i0 + 256], "bwd")
                    pb.wait("sync", "bwd", bw)
                    br = pb.dma("sync", bt[:], sh_r, "brd")
                    sh_k = bass.AP(tensor=kldbm_d.tensor, offset=33 + i0,
                                   ap=[[N + 129, 63], [1, 128]])
                    pb.wait("sync", "q_tensor", kb_read[0])
                    pb.wait("sync", "kldst", pb.cur("kldst"))
                    kl = pb.dma("sync", kbT[:, :], sh_k, "kbl")
                    pb.wait("tensor", "kbl", kl)
                    pb.wait("tensor", "q_vector", pb.tick("vector"))
                    tv = pb.op("tensor", lambda E: E.transpose(p[4][:, 0:63], kbT[:, :],
                                                               idn[0:63, 0:63]))
                    kb_read[0] = tv
                    pb.wait("vector", "q_tensor", tv)
                    pb.op("vector", lambda E: E.tensor_copy(kb[:], p[4][:, 0:63]))
                    pb.wait("vector", "brd", br)
                    pb.op("vector", lambda E, ci=ci: E.tensor_tensor(bt[:], bt[:], qld[:, ci, :],
                                                                     ALU.add))
                    vv = pb.op("vector", lambda E: E.tensor_tensor(bt[:], bt[:], kb[:], ALU.add))
                    pb.wait("sync", "q_vector", vv)
                    bwr = pb.dma("sync", sh_w, bt[:], "bwr")
                    pb.wait("sync", "bwr", bwr)
                    bre = pb.dma("sync", srow[:, buf, i0:i0 + 256], band_d[:, :], "bre")
                    pb.wait("vector", "bre", bre)
                    pb.wait("scalar", "bre", bre)
                    for jt in range(NJT):
                        pb.op("vector", lambda E, jt=jt, buf=buf:
                              E.reduce_max(mxs[:, jt:jt + 1],
                                           srow[:, buf, 64 + jt * 512:64 + (jt + 1) * 512],
                                           axis=AX.X))
                    pb.op("vector", lambda E: E.reduce_max(scc[:, 0:1], mxs[:, 0:NJT], axis=AX.X))
                    mv = pb.op("vector", lambda E: E.tensor_scalar(scc[:, 1:2], scc[:, 0:1],
                                                                   -1.0, None, ALU.mult))
                    pb.wait("scalar", "q_vector", mv)
                    for jt in range(NJT):
                        sv = pb.op("scalar", lambda E, jt=jt, buf=buf:
                                   E.activation(srow[:, buf, 64 + jt * 512:64 + (jt + 1) * 512],
                                                srow[:, buf, 64 + jt * 512:64 + (jt + 1) * 512],
                                                AF.Exp, bias=scc[:, 1:2],
                                                accum_out=rss[:, jt:jt + 1]))
                    exp_last[buf] = sv
                    pb.wait("vector", "q_scalar", sv)
                    pb.op("vector", lambda E: E.reduce_sum(scc[:, 2:3], rss[:, 0:NJT], axis=AX.X))
                    pb.op("vector", lambda E: E.reciprocal(scc[:, 3:4], scc[:, 2:3]))
                    if i0 == 0:
                        pb.op("vector", lambda E, buf=buf: E.memset(srow[:, buf, 0:64], 0.0))
                    if i0 == N - 128:
                        pb.op("vector", lambda E, buf=buf:
                              E.memset(srow[:, buf, 64 + N:64 + N + 64], 0.0))
                    pb.op("vector", lambda E: E.memset(scc[:, 4:5], 0.0))
                    pb.op("vector", lambda E: E.memset(scc[:, 5:6], 0.0))
                    for jt in range(NJT):
                        kind, s = kinds[jt]
                        if kind == "lo":
                            pb.op("vector", lambda E, jt=jt:
                                  E.tensor_tensor(scc[:, 4:5], scc[:, 4:5], rss[:, jt:jt + 1],
                                                  ALU.add))
                        elif kind == "hi":
                            pb.op("vector", lambda E, jt=jt:
                                  E.tensor_tensor(scc[:, 5:6], scc[:, 5:6], rss[:, jt:jt + 1],
                                                  ALU.add))
                        else:
                            sl = slice(64 + jt * 512, 64 + (jt + 1) * 512)
                            if -32 + 128 * s >= -127:
                                pb.op("vector", lambda E, sl=sl, s=s, buf=buf:
                                      E.tensor_tensor(tmp5[:], srow[:, buf, sl], Tl[s][:],
                                                      ALU.mult))
                                pb.op("vector", lambda E: E.reduce_sum(scc[:, 6:7], tmp5[:],
                                                                       axis=AX.X))
                                pb.op("vector", lambda E:
                                      E.tensor_tensor(scc[:, 4:5], scc[:, 4:5], scc[:, 6:7],
                                                      ALU.add))
                            if 32 + 128 * s <= 511:
                                pb.op("vector", lambda E, sl=sl, s=s, buf=buf:
                                      E.tensor_tensor(tmp5[:], srow[:, buf, sl], Tm[s][:],
                                                      ALU.mult))
                                pb.op("vector", lambda E: E.reduce_sum(scc[:, 6:7], tmp5[:],
                                                                       axis=AX.X))
                                pb.op("vector", lambda E:
                                      E.tensor_tensor(scc[:, 5:6], scc[:, 5:6], scc[:, 6:7],
                                                      ALU.add))
                    vv = pb.tick("vector")
                    pb.wait("sync", "q_vector", vv)
                    pb.wait("sync", "q_scalar", pb.tick("scalar"))
                    px = pb.dma("sync", band_d[:, :], srow[:, buf, i0:i0 + 256], "pxd")
                    pb.wait("sync", "pxd", px)
                    pxr = pb.dma("sync", pext[:, 0:63], sh_r, "pxr")
                    pb.wait("vector", "pxr", pxr)
                    pb.op("vector", lambda E: E.tensor_copy(pext[:, 63:64], scc[:, 4:5]))
                    vv = pb.op("vector", lambda E: E.tensor_copy(pext[:, 64:65], scc[:, 5:6]))
                    pb.wait("tensor", "q_vector", vv)
                    tv = pb.op("tensor", lambda E: E.transpose(p[4][0:65, 0:128], pext[:],
                                                               idn[:]))
                    pb.wait("vector", "q_tensor", tv)
                    vv = pb.op("vector", lambda E: E.tensor_copy(ptx[:, :], p[4][0:65, 0:128]))
                    pb.wait("tensor", "q_vector", vv)
                    NG = NC // 4
                    gevs = {}

                    def pv_mms(g):
                        pb.wait("tensor", "q_vector", gevs[g])
                        for jj in range(4):
                            mj = g * 4 + jj
                            pb.op("tensor", lambda E, g=g, jj=jj, mj=mj:
                                  E.matmul(p[7][:, 0:64],
                                           pth[:, g % 2, jj * 128:(jj + 1) * 128],
                                           vh[:, mj, :], start=(mj == 0), stop=False))
                            pb.op("tensor", lambda E, g=g, jj=jj, mj=mj:
                                  E.matmul(p[7][:, 0:64],
                                           pth[:, g % 2, jj * 128:(jj + 1) * 128],
                                           vl[:, mj, :], start=False, stop=False))
                            tk = pb.op("tensor", lambda E, g=g, jj=jj, mj=mj:
                                       E.matmul(p[7][:, 0:64],
                                                ptl[:, g % 2, jj * 128:(jj + 1) * 128],
                                                vh[:, mj, :], start=False, stop=False))
                        ptile_rd[g % 2] = tk

                    for g in range(NG):
                        tb = p[2 + g % 2]
                        for jj in range(4):
                            jc = g * 4 + jj
                            if jj == 0:
                                pb.wait("tensor", "q_vector", pvb_last[2 + g % 2])
                            tv = pb.op("tensor", lambda E, jc=jc, jj=jj, tb=tb, buf=buf:
                                       E.transpose(tb[:, jj * 128:(jj + 1) * 128],
                                                   srow[:, buf,
                                                        64 + jc * 128:64 + (jc + 1) * 128],
                                                   idn[:]))
                        pb.wait("vector", "q_tensor", tv)
                        pb.wait("vector", "q_tensor", ptile_rd[g % 2])
                        pb.op("vector", lambda E, g=g, tb=tb:
                              E.tensor_copy(pth[:, g % 2, :], tb[:, 0:512]))
                        pvb_last[2 + g % 2] = pb.op(
                            "vector", lambda E, g=g, tb=tb:
                            E.scalar_tensor_tensor(ptl[:, g % 2, :], pth[:, g % 2, :], -1.0,
                                                   tb[:, 0:512], ALU.mult, ALU.add))
                        gevs[g] = pvb_last[2 + g % 2]
                        if g >= 1:
                            pv_mms(g - 1)
                    pv_mms(NG - 1)
                    mmv = pb.op("tensor", lambda E: E.matmul(p[7][:, 0:64], ptx[:, :], vaug[:],
                                                             start=False, stop=True))
                    pv_done[buf] = mmv
                    pb.wait("vector", "q_tensor", mmv)
                    vv = pb.op("vector", lambda E: E.tensor_scalar(ctxs[:], p[7][:, 0:64],
                                                                   scc[:, 3:4], None, ALU.mult))
                    pb.wait("tensor", "q_vector", vv)
                    tv = pb.op("tensor", lambda E: E.transpose(p[4][0:64, 0:128], ctxs[:],
                                                               idn[:]))
                    pb.wait("vector", "q_tensor", tv)
                    pb.wait("vector", "ctxst", pb.cur("ctxst"))
                    vv = pb.op("vector", lambda E: E.tensor_copy(ctxT[:, :], p[4][0:64, 0:128]))
                    pb.wait("gpsimd", "q_vector", vv)
                    pb.dma("gpsimd", ctx_d[h * 64:(h + 1) * 64, ci * 128:(ci + 1) * 128],
                           ctxT[:, :], "ctxst")

                for ci in range(NCQ):
                    if ci == 0:
                        emit_scores(0)
                    if ci + 1 < NCQ:
                        emit_scores(ci + 1)
                    emit_rest(ci)
        pb.phase()

        with pb.scope():
            wpsl = sbuf(pb, "p3w", (128, KCH, D))
            cxsl = sbuf(pb, "p3c", (128, 2, KCH, 128))
            x4c = sbuf(pb, "p3x", (128, 2, D))
            yo = sbuf(pb, "p3y", (128, 2, 512))
            bprow = sbuf(pb, "p3br", (1, D))
            pb.dma("sync", wpsl[:], wprojT.rearrange("(c p) o -> p c o", p=128), "p3wl")
            lv = pb.dma("sync", bprow[:], bproj.rearrange("(a o) -> a o", a=1), "p3wl")
            pb.wait("tensor", "p3wl", lv)
            pb.wait("vector", "p3wl", lv)
            bpb, _ = bcast_row(pb, "p3bb", bprow[:], D, 0, p[6], ones1)
            for tc in range(NCQ):
                sl = tc % 2
                pb.wait("sync", "q_tensor", pb.tick("tensor"))
                pb.wait("sync", "q_vector", pb.tick("vector"))
                pb.dma("sync", cxsl[:, sl, :, :],
                       ctx_d.rearrange("(c p) n -> p c n", p=128)[:, :, tc * 128:(tc + 1) * 128],
                       "p3cl_s%d" % sl)
                if resid:
                    x4read = x4_d if pa is not None else x3
                    lv = pb.dma("sync", x4c[:, sl, :],
                                x4read[q0 + tc * 128:q0 + (tc + 1) * 128, :], "p3xl_s%d" % sl)
                    pb.wait("vector", "p3xl_s%d" % sl, lv)
                pb.wait("tensor", "p3cl_s%d" % sl, pb.cur("p3cl_s%d" % sl))
                for ot in range(2):
                    bank = p[4 + ot]
                    for k in range(KCH):
                        mmv = pb.op("tensor", lambda E, k=k, sl=sl, ot=ot, bank=bank:
                                    E.matmul(bank[:, 0:512], cxsl[:, sl, k, :],
                                             wpsl[:, k, ot * 512:(ot + 1) * 512],
                                             start=(k == 0), stop=(k == 7)))
                    pb.wait("vector", "q_tensor", mmv)
                    pb.wait("vector", "yst_s%d" % ot, pb.cur("yst_s%d" % ot))
                    if resid:
                        pb.op("vector", lambda E, sl=sl, ot=ot, bank=bank:
                              E.tensor_tensor(yo[:, ot % 2, :], bank[:, 0:512],
                                              x4c[:, sl, ot * 512:(ot + 1) * 512], ALU.add))
                        vv = pb.op("vector", lambda E, sl=sl, ot=ot:
                                   E.tensor_tensor(yo[:, ot % 2, :], yo[:, ot % 2, :],
                                                   bpb[:, ot * 512:(ot + 1) * 512], ALU.add))
                    else:
                        vv = pb.op("vector", lambda E, sl=sl, ot=ot, bank=bank:
                                   E.tensor_tensor(yo[:, ot % 2, :], bank[:, 0:512],
                                                   bpb[:, ot * 512:(ot + 1) * 512], ALU.add))
                    pb.wait("gpsimd", "q_vector", vv)
                    pb.dma("gpsimd", y[tc * 128:(tc + 1) * 128, ot * 512:(ot + 1) * 512],
                           yo[:, ot % 2, :], "yst_s%d" % ot)
        pb.wait("gpsimd", "yst_s0", pb.cur("yst_s0"))
        pb.wait("gpsimd", "yst_s1", pb.cur("yst_s1"))
    pb.prefix = _oldpfx


def build_l1attn(N, NQ, q0):
    H, hd = NH, HD
    NC = N // 128
    NJT = N // 512
    NCQ = NQ // 128
    SW = 64 + N + 64
    nc = bass.Bass("TRN2", target_bir_lowering=False, debug=False)
    x3 = nc.dram_tensor("x3", [N, D], F32, kind="ExternalInput").ap()
    pa = nc.dram_tensor("pa", [N, D], F32, kind="ExternalInput").ap()
    pbx = nc.dram_tensor("pb", [N, D], F32, kind="ExternalInput").ap()
    wqkvT = nc.dram_tensor("wqkvT", [D, 3 * D], F32, kind="ExternalInput").ap()
    wprojT = nc.dram_tensor("wprojT", [D, D], F32, kind="ExternalInput").ap()
    bproj = nc.dram_tensor("bproj", [D], F32, kind="ExternalInput").ap()
    rpeq = nc.dram_tensor("rpeq", [H, hd, NB], F32, kind="ExternalInput").ap()
    rpek = nc.dram_tensor("rpek", [H, hd, NB], F32, kind="ExternalInput").ap()
    rpev = nc.dram_tensor("rpev", [H, NB, hd], F32, kind="ExternalInput").ap()
    y = nc.dram_tensor("y", [NQ, D], F32, kind="ExternalOutput").ap()
    est = contextlib.ExitStack()
    with est:
        pb = PB(nc, est)
        idn = make_identity(pb)
        p = psum8(pb)
        emit_l1attn(pb, p, idn, x3, pa, pbx, wqkvT, wprojT, bproj, rpeq, rpek, rpev, y, N, NQ, q0)
        pb.build()
    return nc


def build_add3(T):
    nc = bass.Bass("TRN2", target_bir_lowering=False, debug=False)
    a = nc.dram_tensor("a", [T, D], F32, kind="ExternalInput").ap()
    b = nc.dram_tensor("b", [T, D], F32, kind="ExternalInput").ap()
    c = nc.dram_tensor("c", [T, D], F32, kind="ExternalInput").ap()
    y = nc.dram_tensor("y", [T, D], F32, kind="ExternalOutput").ap()
    est = contextlib.ExitStack()
    with est:
        pb = PB(nc, est)
        xc = sbuf(pb, "xc", (128, 2, D))
        x2 = sbuf(pb, "x2", (128, 2, D))
        for ch in range(T // 128):
            r0 = ch * 128
            sl = ch % 2
            pb.wait("sync", "q_vector", pb.tick("vector"))
            pb.wait("sync", "yst_s%d" % sl, pb.cur("yst_s%d" % sl))
            l1 = pb.dma("sync", xc[:, sl, :], a[r0:r0 + 128, :], "al_s%d" % sl)
            l2 = pb.dma("sync", x2[:, sl, :], b[r0:r0 + 128, :], "bl_s%d" % sl)
            pb.wait("vector", "al_s%d" % sl, l1)
            pb.wait("vector", "bl_s%d" % sl, l2)
            a1 = pb.op("vector", lambda E, sl=sl: E.tensor_tensor(xc[:, sl, :], xc[:, sl, :],
                                                                  x2[:, sl, :], ALU.add))
            pb.wait("sync", "q_vector", a1)
            l3 = pb.dma("sync", x2[:, sl, :], c[r0:r0 + 128, :], "bl_s%d" % sl)
            pb.wait("vector", "bl_s%d" % sl, l3)
            vv = pb.op("vector", lambda E, sl=sl: E.tensor_tensor(xc[:, sl, :], xc[:, sl, :],
                                                                  x2[:, sl, :], ALU.add))
            pb.wait("gpsimd", "q_vector", vv)
            pb.dma("gpsimd", y[r0:r0 + 128, :], xc[:, sl, :], "yst_s%d" % sl)
        pb.wait("gpsimd", "yst_s0", pb.cur("yst_s0"))
        pb.wait("gpsimd", "yst_s1", pb.cur("yst_s1"))
        pb.build()
    return nc


_CACHE = {}


def _get(key, builder, *args):
    if key not in _CACHE:
        _CACHE[key] = builder(*args)
    return _CACHE[key]


def run_l0ffn(x_slices, w):
    T = x_slices[0].shape[0]
    nc = _get(("l0ffn", T), build_l0ffn, T)
    in_maps = [{"x": np.ascontiguousarray(xs),
                "U1": w["L0_ffn_U1"], "V1": w["L0_ffn_V1"], "b1": w["L0_ffn_b1"],
                "U2": w["L0_ffn_U2"], "V2": w["L0_ffn_V2"], "b2": w["L0_ffn_b2"]}
               for xs in x_slices]
    res = run_bass_kernel_spmd(nc, in_maps, core_ids=list(range(len(in_maps))))
    return [r["y"] for r in res.results]


def _l0attn_map(xslice, w):
    return {"x": np.ascontiguousarray(xslice),
            "Wq": w["L0_attn_Wq"], "Wk": w["L0_attn_Wk"], "Wv": w["L0_attn_Wv"],
            "Wo": w["L0_attn_Wo"], "W1": w["L0_attn_W1"], "W2": w["L0_attn_W2"],
            "bq": w["L0_attn_bq"], "bk": w["L0_attn_bk"], "bv": w["L0_attn_bv"],
            "bo": w["L0_attn_bo"], "b1": w["L0_attn_b1"], "b2": w["L0_attn_b2"],
            "lnp": np.stack([w["L0_attn_ln1_w"], w["L0_attn_ln1_b"],
                             w["L0_attn_ln2_w"], w["L0_attn_ln2_b"]])}


def run_l0attn(x1, w):
    nc0 = _get(("l0attn", 0), build_l0attn, HALF, 0)
    nc1 = _get(("l0attn", 128), build_l0attn, HALF, 128)
    m0 = [_l0attn_map(x1[b, 0:HALF], w) for b in range(B)]
    m1 = [_l0attn_map(x1[b, HALF - 128:SEQ], w) for b in range(B)]
    r0 = run_bass_kernel_spmd(nc0, m0, core_ids=[0, 1, 2, 3])
    r1 = run_bass_kernel_spmd(nc1, m1, core_ids=[4, 5, 6, 7])
    x2 = np.empty_like(x1)
    for b in range(B):
        x2[b, 0:HALF] = r0.results[b]["y"]
        x2[b, HALF:SEQ] = r1.results[b]["y"]
    return x2


_MOE_PERM = {0: [0, 1, 2, 3, 4, 5, 6, 7], 1: [4, 5, 6, 7, 0, 1, 2, 3]}


def run_moe(x, w):
    nc = _get(("moe",), build_moe)
    maps = []
    for b in range(B):
        for eg in range(2):
            sl = slice(0, 4) if eg == 0 else slice(4, 8)
            maps.append({"x": np.ascontiguousarray(x[b]),
                         "gw": np.ascontiguousarray(w["L1_ffn_gate_w"][:, _MOE_PERM[eg]]),
                         "Wg": np.ascontiguousarray(w["L1_ffn_Wg"][sl]),
                         "Wu": np.ascontiguousarray(w["L1_ffn_Wu"][sl]),
                         "Wd": np.ascontiguousarray(w["L1_ffn_Wd"][sl])})
    res = run_bass_kernel_spmd(nc, maps, core_ids=list(range(8)))
    return [[res.results[b * 2 + eg]["y"] for eg in range(2)] for b in range(B)]


def run_l1attn(x3, parts, w):
    wqkvT = np.ascontiguousarray(w["L1_attn_w_qkv"].T)
    wprojT = np.ascontiguousarray(w["L1_attn_w_proj"].T)
    rpeqR = np.ascontiguousarray(w["L1_attn_rpe_q_table"][:, :, ::-1])
    x5 = np.empty_like(x3)
    for hi, q0 in enumerate((0, HALF)):
        nc = _get(("l1attn", q0), build_l1attn, SEQ, HALF, q0)
        maps = [{"x3": np.ascontiguousarray(x3[b]),
                 "pa": np.ascontiguousarray(parts[b][0]),
                 "pb": np.ascontiguousarray(parts[b][1]),
                 "wqkvT": wqkvT, "wprojT": wprojT, "bproj": w["L1_attn_b_proj"],
                 "rpeq": rpeqR, "rpek": w["L1_attn_rpe_k_table"],
                 "rpev": w["L1_attn_rpe_v_table"]} for b in range(B)]
        res = run_bass_kernel_spmd(nc, maps, core_ids=[0 + 4 * hi, 1 + 4 * hi,
                                                       2 + 4 * hi, 3 + 4 * hi])
        for b in range(B):
            x5[b, q0:q0 + HALF] = res.results[b]["y"]
    return x5


def run_add3(x5, parts):
    nc = _get(("add3",), build_add3, HALF)
    maps = []
    for b in range(B):
        for h in range(2):
            sl = slice(h * HALF, (h + 1) * HALF)
            maps.append({"a": np.ascontiguousarray(x5[b, sl]),
                         "b": np.ascontiguousarray(parts[b][0][sl]),
                         "c": np.ascontiguousarray(parts[b][1][sl])})
    res = run_bass_kernel_spmd(nc, maps, core_ids=list(range(8)))
    out = np.empty_like(x5)
    for b in range(B):
        for h in range(2):
            out[b, h * HALF:(h + 1) * HALF] = res.results[b * 2 + h]["y"]
    return out


def emit_combine(pb, a, b, c, dst, T, tag=""):
    _oldpfx = pb.prefix
    pb.prefix = _oldpfx + tag
    with pb.scope():
        xc = sbuf(pb, "cxc", (128, 2, D))
        x2 = sbuf(pb, "cx2", (128, 2, D))
        for ch in range(T // 128):
            r0 = ch * 128
            sl = ch % 2
            pb.wait("sync", "q_vector", pb.tick("vector"))
            pb.wait("sync", "cyst_s%d" % sl, pb.cur("cyst_s%d" % sl))
            l1 = pb.dma("sync", xc[:, sl, :], a[r0:r0 + 128, :], "cal_s%d" % sl)
            l2 = pb.dma("sync", x2[:, sl, :], b[r0:r0 + 128, :], "cbl_s%d" % sl)
            pb.wait("vector", "cal_s%d" % sl, l1)
            pb.wait("vector", "cbl_s%d" % sl, l2)
            a1 = pb.op("vector", lambda E, sl=sl: E.tensor_tensor(xc[:, sl, :], xc[:, sl, :],
                                                                  x2[:, sl, :], ALU.add))
            pb.wait("sync", "q_vector", a1)
            l3 = pb.dma("sync", x2[:, sl, :], c[r0:r0 + 128, :], "cbl_s%d" % sl)
            pb.wait("vector", "cbl_s%d" % sl, l3)
            vv = pb.op("vector", lambda E, sl=sl: E.tensor_tensor(xc[:, sl, :], xc[:, sl, :],
                                                                  x2[:, sl, :], ALU.add))
            pb.wait("gpsimd", "q_vector", vv)
            pb.dma("gpsimd", dst[r0:r0 + 128, :], xc[:, sl, :], "cyst_s%d" % sl)
        pb.wait("gpsimd", "cyst_s0", pb.cur("cyst_s0"))
        pb.wait("gpsimd", "cyst_s1", pb.cur("cyst_s1"))
    pb.prefix = _oldpfx


def emit_cc_allgather(pb, ci, co):
    pb.phase()
    pb.op("gpsimd", lambda E: E.collective_compute(
        "AllGather", ALU.bypass, ins=[ci[:, :]], outs=[co[:, :, :]],
        replica_groups=[[0, 1], [2, 3], [4, 5], [6, 7]]))
    pb.phase()


def build_mega():
    nc = bass.Bass("TRN2", target_bir_lowering=False, debug=False)
    x0 = nc.dram_tensor("x0", [SEQ, D], F32, kind="ExternalInput").ap()
    fU1 = nc.dram_tensor("fU1", [D, RANK], F32, kind="ExternalInput").ap()
    fV1 = nc.dram_tensor("fV1", [RANK, DFF], F32, kind="ExternalInput").ap()
    fb1 = nc.dram_tensor("fb1", [DFF], F32, kind="ExternalInput").ap()
    fU2 = nc.dram_tensor("fU2", [DFF, RANK], F32, kind="ExternalInput").ap()
    fV2 = nc.dram_tensor("fV2", [RANK, D], F32, kind="ExternalInput").ap()
    fb2 = nc.dram_tensor("fb2", [D], F32, kind="ExternalInput").ap()
    aWq = nc.dram_tensor("aWq", [D, D], F32, kind="ExternalInput").ap()
    aWk = nc.dram_tensor("aWk", [D, D], F32, kind="ExternalInput").ap()
    aWv = nc.dram_tensor("aWv", [D, D], F32, kind="ExternalInput").ap()
    aWo = nc.dram_tensor("aWo", [D, D], F32, kind="ExternalInput").ap()
    aW1 = nc.dram_tensor("aW1", [D, DFF], F32, kind="ExternalInput").ap()
    aW2 = nc.dram_tensor("aW2", [DFF, D], F32, kind="ExternalInput").ap()
    abq = nc.dram_tensor("abq", [D], F32, kind="ExternalInput").ap()
    abk = nc.dram_tensor("abk", [D], F32, kind="ExternalInput").ap()
    abv = nc.dram_tensor("abv", [D], F32, kind="ExternalInput").ap()
    abo = nc.dram_tensor("abo", [D], F32, kind="ExternalInput").ap()
    ab1 = nc.dram_tensor("ab1", [DFF], F32, kind="ExternalInput").ap()
    ab2 = nc.dram_tensor("ab2", [D], F32, kind="ExternalInput").ap()
    alnp = nc.dram_tensor("alnp", [4, D], F32, kind="ExternalInput").ap()
    mgw = nc.dram_tensor("mgw", [D, NE], F32, kind="ExternalInput").ap()
    mWg = nc.dram_tensor("mWg", [4, D, NI], F32, kind="ExternalInput").ap()
    mWu = nc.dram_tensor("mWu", [4, D, NI], F32, kind="ExternalInput").ap()
    mWd = nc.dram_tensor("mWd", [4, NI, D], F32, kind="ExternalInput").ap()
    HQ8 = 512
    wqkvT = nc.dram_tensor("wqkvT", [D, 3 * HQ8], F32, kind="ExternalInput").ap()
    wprojT = nc.dram_tensor("wprojT", [HQ8, D], F32, kind="ExternalInput").ap()
    bproj = nc.dram_tensor("bproj", [D], F32, kind="ExternalInput").ap()
    rpeq = nc.dram_tensor("rpeq", [8, HD, NB], F32, kind="ExternalInput").ap()
    rpek = nc.dram_tensor("rpek", [8, HD, NB], F32, kind="ExternalInput").ap()
    rpev = nc.dram_tensor("rpev", [8, NB, HD], F32, kind="ExternalInput").ap()
    y = nc.dram_tensor("y", [SEQ, D], F32, kind="ExternalOutput").ap()

    x1_d = nc.dram_tensor("x1_d", [SEQ, D], F32, kind="Internal").ap()
    x2_d = nc.dram_tensor("x2_d", [SEQ, D], F32, kind="Internal").ap()
    x3_d = nc.dram_tensor("x3_d", [SEQ, D], F32, kind="Internal").ap()
    x4_d = nc.dram_tensor("x4c_d", [SEQ, D], F32, kind="Internal").ap()
    x5_d = nc.dram_tensor("x5_d", [SEQ, D], F32, kind="Internal").ap()
    cc1i = nc.dram_tensor("cc1i", [SEQ, D], F32, kind="Internal").ap()
    cc1o = nc.dram_tensor("cc1o", [2, SEQ, D], F32, kind="Internal").ap()
    cc2i = nc.dram_tensor("cc2i", [SEQ, D], F32, kind="Internal").ap()
    cc2o = nc.dram_tensor("cc2o", [2, SEQ, D], F32, kind="Internal").ap()
    cc3i = nc.dram_tensor("cc3i", [SEQ, D], F32, kind="Internal").ap()
    cc3o = nc.dram_tensor("cc3o", [2, SEQ, D], F32, kind="Internal").ap()

    est = contextlib.ExitStack()
    with est:
        pb = PB(nc, est)
        idn = make_identity(pb)
        p = psum8(pb)
        emit_l0ffn(pb, p, idn, x0, fU1, fV1, fb1, fU2, fV2, fb2, x1_d, SEQ, tag="s1")
        pb.phase()
        emit_l0attn(pb, p, idn, x1_d[0:HALF, :], aWq, aWk, aWv, aWo, aW1, aW2,
                    abq, abk, abv, abo, ab1, ab2, alnp, x2_d[0:HALF, :],
                    HALF, 0, tag="s2a")
        pb.phase()
        emit_l0attn(pb, p, idn, x1_d[HALF - 128:SEQ, :], aWq, aWk, aWv, aWo, aW1, aW2,
                    abq, abk, abv, abo, ab1, ab2, alnp, x2_d[HALF:SEQ, :],
                    HALF, 128, tag="s2b")
        pb.phase()
        emit_l0ffn(pb, p, idn, x2_d, fU1, fV1, fb1, fU2, fV2, fb2, x3_d, SEQ, tag="s3")
        pb.phase()
        emit_moe(pb, p, idn, x3_d, mgw, mWg, mWu, mWd, cc1i, tag="s4")
        emit_cc_allgather(pb, cc1i, cc1o)
        emit_combine(pb, x3_d, cc1o[0], cc1o[1], x4_d, SEQ, tag="c1")
        pb.phase()
        emit_l1attn(pb, p, idn, x4_d, None, None, wqkvT, wprojT, bproj,
                    rpeq, rpek, rpev, cc2i, SEQ, SEQ, 0, tag="s5", HH=8, resid=False)
        emit_cc_allgather(pb, cc2i, cc2o)
        emit_combine(pb, x4_d, cc2o[0], cc2o[1], x5_d, SEQ, tag="c2")
        pb.phase()
        emit_moe(pb, p, idn, x5_d, mgw, mWg, mWu, mWd, cc3i, tag="s6")
        emit_cc_allgather(pb, cc3i, cc3o)
        emit_combine(pb, x5_d, cc3o[0], cc3o[1], y, SEQ, tag="c3")
        pb.phase()
        pb.build()
    return nc


def kernel(**inputs):
    w = inputs
    nc = _get(("mega",), build_mega)
    x0 = np.asarray(w["L0_attn_x"], dtype=np.float32)
    wqkvT_f = np.asarray(w["L1_attn_w_qkv"], dtype=np.float32).T
    wprojT_f = np.asarray(w["L1_attn_w_proj"], dtype=np.float32).T
    rpeqR = np.asarray(w["L1_attn_rpe_q_table"], dtype=np.float32)[:, :, ::-1]
    alnp = np.stack([w["L0_attn_ln1_w"], w["L0_attn_ln1_b"],
                     w["L0_attn_ln2_w"], w["L0_attn_ln2_b"]]).astype(np.float32)
    maps = []
    for b in range(B):
        for hg in range(2):
            esl = slice(0, 4) if hg == 0 else slice(4, 8)
            hsl = slice(hg * 8, hg * 8 + 8)
            csl = slice(hg * 512, (hg + 1) * 512)
            wqkvT_hg = np.concatenate([wqkvT_f[:, csl],
                                       wqkvT_f[:, 1024 + hg * 512:1024 + (hg + 1) * 512],
                                       wqkvT_f[:, 2048 + hg * 512:2048 + (hg + 1) * 512]], 1)
            bproj_hg = (np.asarray(w["L1_attn_b_proj"], np.float32) if hg == 0
                        else np.zeros(D, np.float32))
            maps.append({
                "x0": np.ascontiguousarray(x0[b]),
                "fU1": w["L0_ffn_U1"], "fV1": w["L0_ffn_V1"], "fb1": w["L0_ffn_b1"],
                "fU2": w["L0_ffn_U2"], "fV2": w["L0_ffn_V2"], "fb2": w["L0_ffn_b2"],
                "aWq": w["L0_attn_Wq"], "aWk": w["L0_attn_Wk"], "aWv": w["L0_attn_Wv"],
                "aWo": w["L0_attn_Wo"], "aW1": w["L0_attn_W1"], "aW2": w["L0_attn_W2"],
                "abq": w["L0_attn_bq"], "abk": w["L0_attn_bk"], "abv": w["L0_attn_bv"],
                "abo": w["L0_attn_bo"], "ab1": w["L0_attn_b1"], "ab2": w["L0_attn_b2"],
                "alnp": alnp,
                "mgw": np.ascontiguousarray(
                    np.asarray(w["L1_ffn_gate_w"], np.float32)[:, _MOE_PERM[hg]]),
                "mWg": np.ascontiguousarray(w["L1_ffn_Wg"][esl]),
                "mWu": np.ascontiguousarray(w["L1_ffn_Wu"][esl]),
                "mWd": np.ascontiguousarray(w["L1_ffn_Wd"][esl]),
                "wqkvT": np.ascontiguousarray(wqkvT_hg),
                "wprojT": np.ascontiguousarray(wprojT_f[csl, :]),
                "bproj": bproj_hg,
                "rpeq": np.ascontiguousarray(rpeqR[hsl]),
                "rpek": np.ascontiguousarray(w["L1_attn_rpe_k_table"][hsl]),
                "rpev": np.ascontiguousarray(w["L1_attn_rpe_v_table"][hsl]),
            })
    res = run_bass_kernel_spmd(nc, maps, core_ids=list(range(8)))
    out = np.empty((B, SEQ, D), np.float32)
    for b in range(B):
        out[b] = res.results[2 * b]["y"]
    return out
